# version 03
# baseline (speedup 1.0000x reference)
import numpy as np
from contextlib import ExitStack

import concourse.bass as bass
import concourse.mybir as mybir
from concourse.alu_op_type import AluOpType as OP
from concourse.bass_utils import run_bass_kernel_spmd

F32 = mybir.dt.float32
BF16 = mybir.dt.bfloat16
U32 = mybir.dt.uint32
I32 = mybir.dt.int32
AF = mybir.ActivationFunctionType
AX = mybir.AxisListType

NCORES = 8
TOK = 4096
T = 32
SS = 4
NSUP = T // SS
D = 2048
NS = 1024
EPS = 1e-5
BIG = 1000.0


def build_program(debug=False):
    nc = bass.Bass("TRN2", target_bir_lowering=False)

    x0d = nc.dram_tensor("x0", [TOK, D], F32, kind="ExternalInput")
    btd = nc.dram_tensor("bt", [D, NS], F32, kind="ExternalInput")
    ctd = nc.dram_tensor("ct", [NS, D], F32, kind="ExternalInput")
    dcold = nc.dram_tensor("dcol", [128, 16], F32, kind="ExternalInput")
    gamd = nc.dram_tensor("gam", [64, 32], F32, kind="ExternalInput")
    betd = nc.dram_tensor("bet", [64, 32], F32, kind="ExternalInput")
    basd = nc.dram_tensor("bas", [64, 512], F32, kind="ExternalInput")
    slod = nc.dram_tensor("slo", [64, 512], F32, kind="ExternalInput")
    oscd = nc.dram_tensor("osc", [64, 1], F32, kind="ExternalInput")
    outd = nc.dram_tensor("out", [TOK, D], F32, kind="ExternalOutput")
    if debug:
        dbg1d = nc.dram_tensor("dbg1", [128, D], F32, kind="ExternalOutput")
        dbg2d = nc.dram_tensor("dbg2", [128, 2048], F32, kind="ExternalOutput")

    x1sp = nc.dram_tensor("x1sp", [TOK * 64, 32], F32)
    th_d = nc.dram_tensor("th_d", [1, D], F32)
    w2_d = nc.dram_tensor("w2_d", [64, 32], BF16)
    bia_d = nc.dram_tensor("bia_d", [1, 64], BF16)
    cc_in = nc.dram_tensor("cc_in", [1, 128], F32)
    cc_out = nc.dram_tensor("cc_out", [1, 128], F32)

    with ExitStack() as st:
        def sb(name, shape, dt):
            return st.enter_context(nc.sbuf_tensor(name, shape, dt))

        BT = sb("BT", [128, 16 * NS], BF16)
        CT = sb("CT", [128, 8 * D], BF16)
        TOKI = sb("TOKI", [128, T], F32)
        EPSC = sb("EPSC", [128, 1], F32)
        VMM = sb("VMM", [128, T * 64], F32)
        SOSC = sb("SOSC", [64, 1], F32)
        TBH = sb("TBH", [64, 1024], BF16)
        TBL = sb("TBL", [64, 1024], BF16)
        GMM = sb("GMM", [64, 2], F32)
        PMM = sb("PMM", [64, 2], F32)

        X0B = [sb(f"X0B{i}", [128, D], F32) for i in range(4)]
        ATTNB = [sb(f"ATTNB{i}", [128, D], F32) for i in range(2)]

        MU0 = sb("MU0", [128, T], F32)
        SM0 = sb("SM0", [128, T], F32)
        SQ0 = sb("SQ0", [128, T], F32)
        V0C = sb("V0C", [128, T], F32)
        R0A = sb("R0A", [128, T], F32)
        B0A = sb("B0A", [128, T], F32)
        SMA = sb("SMA", [128, T], F32)
        SQA = sb("SQA", [128, T], F32)
        MUA = sb("MUA", [128, T], F32)
        VAC = sb("VAC", [128, T], F32)
        RAA = sb("RAA", [128, T], F32)
        BAA = sb("BAA", [128, T], F32)
        SM1 = sb("SM1", [128, T], F32)
        SQ1 = sb("SQ1", [128, T], F32)
        MU1 = sb("MU1", [128, T], F32)
        V1A = sb("V1A", [128, T], F32)
        R1A = sb("R1A", [128, T], F32)
        V2C = sb("V2C", [128, T], F32)
        R2A = sb("R2A", [128, T], F32)
        S12 = sb("S12", [128, T], F32)
        TIDA = sb("TIDA", [128, T], F32)
        IDXF = sb("IDXF", [128, T], F32)
        IDXU = sb("IDXU", [128, T], U32)
        R12A = sb("R12A", [128, T], F32)
        TMIN = sb("TMIN", [128, T], F32)
        TMAX = sb("TMAX", [128, T], F32)
        MUG = sb("MUG", [128, T], F32)
        RGA = sb("RGA", [128, T], F32)
        S2A = sb("S2A", [128, T], F32)
        B2A = sb("B2A", [128, T], F32)
        INV = sb("INV", [128, T], F32)
        W32a = sb("W32a", [128, T], F32)
        W32b = sb("W32b", [128, T], F32)
        W32c = sb("W32c", [128, T], F32)


        OHT = sb("OHT", [64, T * 128], BF16)
        GBA = sb("GBA", [128, T * 64], F32)
        XSL = sb("XSL", [128, T * 32], F32)
        XNS = sb("XNS", [128, T * 32], F32)
        RED = sb("RED", [128, 64], F32)

        SEM = {n: st.enter_context(nc.semaphore(n)) for n in
               ["LD", "PD", "CC", "V", "A", "P", "G",
                "SP0", "SP1", "SP2", "SP3",
                "LX0", "LX1", "LX2", "LX3", "RL0", "RL1", "OS0", "OS1"]}
        cnt = {n: 0 for n in SEM}
        tick = {}

        def bump(inst, e, key=None):
            cnt[e] += 1
            inst.then_inc(SEM[e], 1)
            if key is not None:
                tick[key] = cnt[e]
            return cnt[e]

        def dbump(inst, e, key=None):
            cnt[e] += 16
            inst.then_inc(SEM[e], 16)
            if key is not None:
                tick[key] = cnt[e]
            return cnt[e]

        blk = st.enter_context(nc.Block())

        pst_free = [("V", 0)]

        if True:
            PS1 = [st.enter_context(nc.psum_tensor("PS1a", [128, 512], F32)),
                   st.enter_context(nc.psum_tensor("PS1b", [128, 512], F32))]
            PS2 = st.enter_context(nc.psum_tensor("PS2", [128, 2048], F32))
            PST = st.enter_context(nc.psum_tensor("PST", [128, 512], BF16))
            PSS = st.enter_context(nc.psum_tensor("PSS", [128, 512], F32))
            PSSb = PSS.ap().bitcast(BF16)
            sb1 = sb
            DD = sb1("DD", [128, 16 * 128], BF16)
            DCOL = sb1("DCOL", [128, 16], F32)
            WSS = sb1("WSS", [128, 16 * 64], BF16)
            THB = sb1("THB", [128, D], BF16)
            SID = sb1("SID", [128, 128], BF16)
            SIDF = sb1("SIDF", [128, 128], F32)
            ONE1 = sb1("ONE1", [1, 128], BF16)
            BIAR = sb1("BIAR", [1, 64], BF16)
            TIOT = sb1("TIOT", [128, 64], F32)
            CLB = sb1("CLB", [128, 64], F32)
            IOT128 = sb1("IOT128", [128, 128], F32)
            PIDX = sb1("PIDX", [128, 1], F32)
            SGAM = sb1("SGAM", [64, 32], F32)
            SBET = sb1("SBET", [64, 32], F32)
            GBH = sb1("GBH", [64, 64], BF16)
            GBL = sb1("GBL", [64, 64], BF16)
            W2BF = sb1("W2BF", [64, 32], BF16)
            BI64 = sb1("BI64", [64, 1], BF16)
            XAB = [sb("XAB0", [128, D], BF16)]
            SIGB = [sb("SIGB0", [128, D], BF16)]
            XAT = sb1("XAT", [128, SS * D], BF16)
            STT = sb1("STT", [128, 8 * 512], BF16)
            SGT = [sb("SGT0", [128, D], BF16)]
            CS8 = sb1("CS8", [128, 8], F32)
            MX8 = sb1("MX8", [128, 8], F32)
            IX8 = sb1("IX8", [128, 8], U32)
            TSC = sb1("TSC", [128, 64], F32)
            TSM = sb1("TSM", [128, 64], F32)
            MX8b = sb1("MX8b", [128, 8], F32)
            IX8b = sb1("IX8b", [128, 8], U32)
            BCF = sb1("BCF", [128, 1], F32)
            OHB = sb1("OHB", [128, T * 64], BF16)

            @blk.sync
            def _(sy):
                dbump(sy.dma_start(SGAM[:], gamd.ap()), "LD")
                dbump(sy.dma_start(SBET[:], betd.ap()), "LD")
                dbump(sy.dma_start(VMM[0:64, 0:512], basd.ap()), "LD")
                dbump(sy.dma_start(VMM[0:64, 512:1024], slod.ap()), "LD")
                dbump(sy.dma_start(SOSC[:], oscd.ap()), "LD")
                dbump(sy.dma_start(DCOL[:], dcold.ap()), "LD", "ldconst")

            @blk.gpsimd
            def _(g):
                g.memset(EPSC[:], EPS)
                g.memset(ONE1[:], 1.0)
                g.memset(WSS[:], 0)
                _gi = g.iota(TIOT[:], [[1, 64]], channel_multiplier=0,
                       allow_small_or_imprecise_dtypes=True)
                _gi = g.iota(CLB[:], [[1, 8], [0, 8]], channel_multiplier=0,
                       allow_small_or_imprecise_dtypes=True)
                _gi = g.iota(TOKI[:], [[128, T]], channel_multiplier=1,
                       allow_small_or_imprecise_dtypes=True)
                _gi = g.iota(PIDX[:], [[0, 1]], channel_multiplier=1,
                       allow_small_or_imprecise_dtypes=True)
                _gi = g.iota(IOT128[:], [[1, 128]], channel_multiplier=0,
                       allow_small_or_imprecise_dtypes=True)
                bump(_gi, "G", "iotas")
                dbump(g.dma_start(
                    BT[:], bass.AP(btd, 0, [[NS, 128], [128 * NS, 16], [1, NS]])),
                    "PD")
                dbump(g.dma_start(
                    CT[:], bass.AP(ctd, 0, [[D, 128], [128 * D, 8], [1, D]])),
                    "PD", "wload")
                g.wait_ge(SEM["PD"], cnt["PD"])

            @blk.vector
            def _(v):
                v.wait_ge(SEM["G"], tick["iotas"])
                bump(v.tensor_scalar(SID[:], IOT128[:], PIDX[:], None,
                                     OP.is_equal), "V", "sid")
                v.wait_ge(SEM["V"], tick["sid"])
                bump(v.tensor_copy(SIDF[:], SID[:]), "V", "sidf")
                v.wait_ge(SEM["LD"], tick["ldconst"])
                for j in range(16):
                    _vi = v.tensor_scalar(DD[:, 128 * j:128 * (j + 1)], SID[:],
                                          DCOL[:, j:j + 1], None, OP.mult)
                bump(_vi, "V", "ddiag")
                bump(v.reciprocal(VMM[0:64, 1300:1332], SGAM[:]), "V", "recg")
                v.wait_ge(SEM["V"], tick["recg"])
                bump(v.tensor_tensor(VMM[0:64, 1332:1364], VMM[0:64, 1300:1332],
                                     SBET[:], OP.mult), "V", "th0")
                v.wait_ge(SEM["V"], tick["th0"])
                bump(v.tensor_scalar(VMM[0:64, 1332:1364], VMM[0:64, 1332:1364],
                                     -1.0, None, OP.mult), "V", "theta")
                v.wait_ge(SEM["LD"], tick["ldconst"])
                bump(v.tensor_reduce(VMM[0:64, 1364:1396],
                                     VMM[0:64, 512:1024].rearrange(
                                         "p (a b) -> p a b", a=32),
                                     AX.X, OP.add), "V", "ssum")

            @blk.gpsimd
            def _(g):
                g.wait_ge(SEM["V"], tick["theta"])
                dbump(g.dma_start(th_d.ap(), VMM[0:64, 1332:1364]), "PD", "thdma")
                g.wait_ge(SEM["PD"], cnt["PD"])
                dbump(g.dma_start(THB[:], bass.AP(th_d, 0, [[0, 128], [1, D]])),
                      "PD", "thbcast")

            @blk.scalar
            def _(a):
                a.wait_ge(SEM["V"], tick["ssum"])
                bump(a.sign(VMM[0:64, 1396:1428], VMM[0:64, 1364:1396]),
                     "A", "ssgn")
                bump(a.sign(VMM[0:64, 1428:1460], SGAM[:]), "A", "sgam")

            @blk.vector
            def _(v):
                v.wait_ge(SEM["A"], tick["sgam"])
                bump(v.tensor_tensor(VMM[0:64, 1460:1492], VMM[0:64, 1396:1428],
                                     VMM[0:64, 1428:1460], OP.mult), "V", "w0")
                v.wait_ge(SEM["V"], tick["w0"])
                bump(v.tensor_reduce(VMM[0:64, 1492:1493],
                                     VMM[0:64, 1460:1492].unsqueeze(1), AX.X,
                                     OP.add), "V", "bsum")
                v.wait_ge(SEM["V"], tick["bsum"])
                bump(v.tensor_scalar(BI64[:], VMM[0:64, 1492:1493], -1.0, None,
                                     OP.mult), "V", "bi64")
                bump(v.tensor_scalar(W2BF[:], VMM[0:64, 1460:1492], 2.0, None,
                                     OP.mult), "V", "w2bf")
                v.wait_ge(SEM["LD"], tick["ldconst"])
                v.tensor_copy(VMM[0:64, 1024:1056], SGAM[:])
                bump(v.tensor_copy(VMM[0:64, 1056:1088], SBET[:]), "V", "gb0")
                v.wait_ge(SEM["V"], tick["gb0"])
                bump(v.tensor_copy(GBH[:], VMM[0:64, 1024:1088]), "V", "gbh")
                v.wait_ge(SEM["V"], tick["gbh"])
                bump(v.tensor_copy(VMM[0:64, 1088:1152], GBH[:]), "V", "gbb")
                v.wait_ge(SEM["V"], tick["gbb"])
                bump(v.tensor_tensor(VMM[0:64, 1152:1216], VMM[0:64, 1024:1088],
                                     VMM[0:64, 1088:1152], OP.subtract), "V", "gbl0")
                v.wait_ge(SEM["V"], tick["gbl0"])
                bump(v.tensor_copy(GBL[:], VMM[0:64, 1152:1216]), "V", "gbl")
                v.wait_ge(SEM["PD"], tick["thdma"])
                bump(v.tensor_copy(TBH[:], VMM[0:64, 0:1024]), "V", "tbh")
                v.wait_ge(SEM["V"], tick["tbh"])
                bump(v.tensor_copy(VMM[0:64, 1024:2048], TBH[:]), "V", "tbb")
                v.wait_ge(SEM["V"], tick["tbb"])
                bump(v.tensor_tensor(VMM[0:64, 1024:2048], VMM[0:64, 0:1024],
                                     VMM[0:64, 1024:2048], OP.subtract), "V", "tbl0")
                v.wait_ge(SEM["V"], tick["tbl0"])
                bump(v.tensor_copy(TBL[:], VMM[0:64, 1024:2048]), "V", "tbl")

            @blk.gpsimd
            def _(g):
                g.wait_ge(SEM["V"], tick["w2bf"])
                g.wait_ge(SEM["PD"], cnt["PD"])
                dbump(g.dma_start(w2_d.ap(), W2BF[:]), "PD")
                g.wait_ge(SEM["PD"], cnt["PD"])
                for b in range(4):
                    dbump(g.dma_start(
                        bass.AP(WSS, 32 * b * (16 * 64) + b, [[16 * 64, 32],
                                                              [68, 16]]),
                        bass.AP(w2_d, 32 * b, [[1, 32], [128, 16]]),
                        allow_slow_non_contiguous=True), "PD")
                g.wait_ge(SEM["V"], tick["bi64"])
                dbump(g.dma_start(bia_d.ap(), BI64[:]), "PD")
                g.wait_ge(SEM["PD"], cnt["PD"])
                dbump(g.dma_start(BIAR[:], bia_d.ap()), "PD")
                bump(g.wait_ge(SEM["PD"], cnt["PD"]), "G", "winit")



            def p1_load(k):
                @blk.sync
                def _(sy, k=k):
                    if k >= 4:
                        sy.wait_ge(SEM["V"], tick[("sig", k - 4)])
                        sy.wait_ge(SEM[f"SP{k % 4}"], tick[("sp", k - 4)])
                        sy.wait_ge(SEM["A"], tick[("sq1", k - 4)])
                    dbump(sy.dma_start(
                        X0B[k % 4][:],
                        bass.AP(x0d, k * 128 * D, [[D, 128], [1, D]])),
                        f"LX{k % 4}", ("x0", k))

            def p1_pre(k):
                @blk.vector
                def _(v, k=k):
                    v.wait_ge(SEM[f"LX{k % 4}"], tick[("x0", k)])
                    bump(v.tensor_reduce(SM0[:, k:k + 1], X0B[k % 4][:],
                                         AX.X, OP.add), "V", ("sum0", k))
                    v.wait_ge(SEM["V"], tick[("sum0", k)])
                    bump(v.tensor_scalar(MU0[:, k:k + 1], SM0[:, k:k + 1],
                                         1.0 / D, None, OP.mult),
                         "V", ("mu0", k))

                @blk.scalar
                def _(a, k=k):
                    a.wait_ge(SEM[f"LX{k % 4}"], tick[("x0", k)])
                    if k == 0:
                        a.wait_ge(SEM["V"], tick["tbl"])
                        a.wait_ge(SEM["PD"], tick["thdma"])
                    bump(a.activation(VMM[:], X0B[k % 4][:], AF.Square,
                                      accum_out=SQ0[:, k:k + 1]),
                         "A", ("sq0", k))

                @blk.vector
                def _(v, k=k):
                    v.wait_ge(SEM["V"], tick[("mu0", k)])
                    v.wait_ge(SEM["A"], tick[("sq0", k)])
                    bump(v.tensor_tensor(W32a[:, k:k + 1], MU0[:, k:k + 1],
                                         MU0[:, k:k + 1], OP.mult),
                         "V", ("m20", k))
                    v.wait_ge(SEM["V"], tick[("m20", k)])
                    bump(v.scalar_tensor_tensor(V0C[:, k:k + 1],
                                                SQ0[:, k:k + 1], 1.0 / D,
                                                W32a[:, k:k + 1], OP.mult,
                                                OP.subtract), "V", ("v0", k))

                @blk.scalar
                def _(a, k=k):
                    a.wait_ge(SEM["V"], tick[("v0", k)])
                    bump(a.activation(V0C[:, k:k + 1], V0C[:, k:k + 1], AF.Sqrt,
                                      bias=EPSC[:]), "A", ("ln0", k))

                @blk.vector
                def _(v, k=k):
                    v.wait_ge(SEM["A"], tick[("ln0", k)])
                    bump(v.reciprocal(R0A[:, k:k + 1], V0C[:, k:k + 1]),
                         "V", ("r0", k))
                    v.wait_ge(SEM["V"], tick[("r0", k)])
                    bump(v.scalar_tensor_tensor(B0A[:, k:k + 1],
                                                MU0[:, k:k + 1], -1.0,
                                                R0A[:, k:k + 1], OP.mult,
                                                OP.mult), "V", ("b0", k))

                @blk.scalar
                def _(a, k=k):
                    a.wait_ge(SEM["V"], tick[("b0", k)])
                    if k >= 1:
                        a.wait_ge(SEM["P"], tick[("xat", k - 1)])
                    bump(a.activation(XAB[0][:], X0B[k % 4][:], AF.Identity,
                                      bias=B0A[:, k:k + 1],
                                      scale=R0A[:, k:k + 1]), "A", ("xa", k))

            def p1_txa(k):
                c = k % SS
                for grp in range(4):
                    @blk.tensor
                    def _(t, grp=grp, c=c, k=k):
                        if grp == 0:
                            t.wait_ge(SEM["A"], tick[("xa", k)])
                            t.wait_ge(SEM["V"], tick["sid"])
                        e, tk_ = pst_free[0]
                        t.wait_ge(SEM[e], tk_)
                        for j4 in range(4):
                            j = grp * 4 + j4
                            _mi = t.transpose(PST[:, 128 * j4:128 * (j4 + 1)],
                                        XAB[0][:, 128 * j:128 * (j + 1)],
                                        SID[:])
                        bump(_mi, "P", ("trx", k, grp))

                    @blk.scalar
                    def _(a, grp=grp, c=c, k=k):
                        a.wait_ge(SEM["P"], tick[("trx", k, grp)])
                        bump(a.copy(
                            XAT[:, D * c + 512 * grp:D * c + 512 * (grp + 1)],
                            PST[:]), "A", ("evT", k, grp))
                        pst_free[0] = ("A", tick[("evT", k, grp)])
                tick[("xat", k)] = tick[("trx", k, 3)]

            def p1_mm1(s):
                xat3 = XAT[:].rearrange("p (c f) -> p c f", c=SS)
                for i in range(8):
                    @blk.tensor
                    def _(t, i=i, s=s):
                        if i == 0:
                            t.wait_ge(SEM["A"], tick[("evT", s * SS + SS - 1, 3)])
                            if s == 0:
                                t.wait_ge(SEM["PD"], tick["wload"])
                        prev = (s, i - 2) if i >= 2 else \
                            ((s - 1, i + 6) if s > 0 else None)
                        if prev is not None:
                            t.wait_ge(SEM["A"], tick[("tanh",) + prev])
                        for j in range(16):
                            _mi = t.matmul(PS1[i % 2][:],
                                     BT[:, NS * j + 128 * i:
                                        NS * j + 128 * (i + 1)],
                                     xat3[:, :, 128 * j:128 * (j + 1)],
                                     start=(j == 0), stop=(j == 15))
                        bump(_mi, "P", ("mm1", s, i))

                    @blk.scalar
                    def _(a, i=i, s=s):
                        a.wait_ge(SEM["P"], tick[("mm1", s, i)])
                        bump(a.activation(STT[:, 512 * i:512 * (i + 1)],
                                          PS1[i % 2][:], AF.Tanh),
                             "A", ("tanh", s, i))

            def p1_mm2(k):
                s, c = k // SS, k % SS

                @blk.tensor
                def _(t, k=k, s=s, c=c):
                    if k >= 1:
                        t.wait_ge(SEM["V"], tick[("aev", k - 1)])
                    t.wait_ge(SEM["A"], tick[("tanh", s, 7)])
                    if k == 0:
                        t.wait_ge(SEM["V"], tick["ddiag"])
                    for j in range(16):
                        _mi = t.matmul(PS2[:, 128 * j:128 * (j + 1)],
                                 XAT[:, D * c + 128 * j:D * c + 128 * (j + 1)],
                                 DD[:, 128 * j:128 * (j + 1)],
                                 start=(j % 4 == 0), stop=False,
                                 skip_group_check=True)
                    for i in range(8):
                        for h in range(4):
                            _mi = t.matmul(
                                PS2[:, 512 * h:512 * (h + 1)],
                                STT[:, 512 * i + 128 * c:
                                    512 * i + 128 * (c + 1)],
                                CT[:, D * i + 512 * h:D * i + 512 * (h + 1)],
                                start=False, stop=(i == 7),
                                skip_group_check=True)
                    bump(_mi, "P", ("mm2", k))

            def p1_post(k):
                @blk.vector
                def _(v, k=k):
                    v.wait_ge(SEM["P"], tick[("mm2", k)])
                    if k >= 2:
                        v.wait_ge(SEM["V"], tick[("sig", k - 2)])
                    bump(v.tensor_copy(ATTNB[k % 2][:], PS2[:]),
                         "V", ("aev", k))
                    v.wait_ge(SEM["V"], tick[("aev", k)])
                    bump(v.tensor_reduce(SMA[:, k:k + 1], ATTNB[k % 2][:],
                                         AX.X, OP.add), "V", ("suma", k))
                    v.wait_ge(SEM["V"], tick[("suma", k)])
                    bump(v.tensor_scalar(MUA[:, k:k + 1], SMA[:, k:k + 1],
                                         1.0 / D, None, OP.mult),
                         "V", ("mua", k))

                @blk.scalar
                def _(a, k=k):
                    a.wait_ge(SEM["V"], tick[("aev", k)])
                    bump(a.activation(VMM[:], ATTNB[k % 2][:], AF.Square,
                                      accum_out=SQA[:, k:k + 1]),
                         "A", ("sqa", k))

                @blk.vector
                def _(v, k=k):
                    v.wait_ge(SEM["V"], tick[("mua", k)])
                    v.wait_ge(SEM["A"], tick[("sqa", k)])
                    bump(v.tensor_tensor(W32a[:, k:k + 1], MUA[:, k:k + 1],
                                         MUA[:, k:k + 1], OP.mult),
                         "V", ("m2a", k))
                    v.wait_ge(SEM["V"], tick[("m2a", k)])
                    bump(v.scalar_tensor_tensor(VAC[:, k:k + 1],
                                                SQA[:, k:k + 1], 1.0 / D,
                                                W32a[:, k:k + 1], OP.mult,
                                                OP.subtract), "V", ("va", k))

                @blk.scalar
                def _(a, k=k):
                    a.wait_ge(SEM["V"], tick[("va", k)])
                    bump(a.activation(VAC[:, k:k + 1], VAC[:, k:k + 1], AF.Sqrt,
                                      bias=EPSC[:]), "A", ("lna", k))

                @blk.vector
                def _(v, k=k):
                    v.wait_ge(SEM["A"], tick[("lna", k)])
                    bump(v.reciprocal(RAA[:, k:k + 1], VAC[:, k:k + 1]),
                         "V", ("ra", k))
                    v.wait_ge(SEM["V"], tick[("ra", k)])
                    bump(v.scalar_tensor_tensor(BAA[:, k:k + 1],
                                                MUA[:, k:k + 1], -1.0,
                                                RAA[:, k:k + 1], OP.mult,
                                                OP.mult), "V", ("ba", k))

                @blk.scalar
                def _(a, k=k):
                    a.wait_ge(SEM["V"], tick[("ba", k)])
                    bump(a.activation(VMM[:], ATTNB[k % 2][:], AF.Identity,
                                      bias=BAA[:, k:k + 1],
                                      scale=RAA[:, k:k + 1]), "A", ("t", k))

                @blk.vector
                def _(v, k=k):
                    v.wait_ge(SEM["A"], tick[("t", k)])
                    bump(v.scalar_tensor_tensor(X0B[k % 4][:], VMM[:], 1.0,
                                                X0B[k % 4][:], OP.mult, OP.add,
                                                accum_out=SM1[:, k:k + 1]),
                         "V", ("x1", k))

                @blk.sync
                def _(sy, k=k):
                    sy.wait_ge(SEM["V"], tick[("x1", k)])
                    dbump(sy.dma_start(
                        bass.AP(x1sp, k * 128 * D, [[D, 128], [1, D]]),
                        X0B[k % 4][:]), f"SP{k % 4}", ("sp", k))

                @blk.scalar
                def _(a, k=k):
                    a.wait_ge(SEM["V"], tick[("x1", k)])
                    bump(a.activation(VMM[:], X0B[k % 4][:], AF.Square,
                                      accum_out=SQ1[:, k:k + 1]),
                         "A", ("sq1", k))

                @blk.vector
                def _(v, k=k):
                    v.wait_ge(SEM["V"], tick[("x1", k)])
                    bump(v.tensor_scalar(MU1[:, k:k + 1], SM1[:, k:k + 1],
                                         1.0 / D, None, OP.mult),
                         "V", ("mu1", k))
                    v.wait_ge(SEM["V"], tick[("mu1", k)])
                    v.wait_ge(SEM["A"], tick[("sq1", k)])
                    bump(v.tensor_tensor(W32a[:, k:k + 1], MU1[:, k:k + 1],
                                         MU1[:, k:k + 1], OP.mult),
                         "V", ("m21", k))
                    v.wait_ge(SEM["V"], tick[("m21", k)])
                    bump(v.scalar_tensor_tensor(V1A[:, k:k + 1],
                                                SQ1[:, k:k + 1], 1.0 / D,
                                                W32a[:, k:k + 1], OP.mult,
                                                OP.subtract), "V", ("v1", k))

                @blk.scalar
                def _(a, k=k):
                    a.wait_ge(SEM["V"], tick[("v1", k)])
                    bump(a.activation(W32b[:, k:k + 1], V1A[:, k:k + 1],
                                      AF.Sqrt, bias=EPSC[:]), "A", ("ln1", k))

                @blk.vector
                def _(v, k=k):
                    v.wait_ge(SEM["A"], tick[("ln1", k)])
                    bump(v.reciprocal(R1A[:, k:k + 1], W32b[:, k:k + 1]),
                         "V", ("r1", k))
                    v.wait_ge(SEM["V"], tick[("r1", k)])
                    bump(v.tensor_tensor(W32a[:, k:k + 1], R1A[:, k:k + 1],
                                         R1A[:, k:k + 1], OP.mult),
                         "V", ("r1s", k))
                    v.wait_ge(SEM["V"], tick[("r1s", k)])
                    bump(v.tensor_tensor(V2C[:, k:k + 1], W32a[:, k:k + 1],
                                         V1A[:, k:k + 1], OP.mult),
                         "V", ("v2", k))

                @blk.scalar
                def _(a, k=k):
                    a.wait_ge(SEM["V"], tick[("v2", k)])
                    bump(a.activation(W32b[:, k:k + 1], V2C[:, k:k + 1],
                                      AF.Sqrt, bias=EPSC[:]), "A", ("ln2", k))

                @blk.vector
                def _(v, k=k):
                    v.wait_ge(SEM["A"], tick[("ln2", k)])
                    bump(v.reciprocal(R2A[:, k:k + 1], W32b[:, k:k + 1]),
                         "V", ("r2", k))
                    v.wait_ge(SEM["V"], tick[("r2", k)])
                    bump(v.tensor_tensor(R12A[:, k:k + 1], R1A[:, k:k + 1],
                                         R2A[:, k:k + 1], OP.mult),
                         "V", ("rr", k))
                    v.wait_ge(SEM["V"], tick[("rr", k)])
                    bump(v.reciprocal(S12[:, k:k + 1], R12A[:, k:k + 1]),
                         "V", ("s12", k))
                    v.wait_ge(SEM["V"], tick[("s12", k)])
                    if k == 0:
                        v.wait_ge(SEM["PD"], tick["thbcast"])
                    bump(v.tensor_scalar(ATTNB[k % 2][:], THB[:],
                                         S12[:, k:k + 1], MU1[:, k:k + 1],
                                         OP.mult, OP.add), "V", ("thr", k))
                    v.wait_ge(SEM["V"], tick[("thr", k)])
                    if k >= 1:
                        v.wait_ge(SEM["P"], tick[("sgt", k - 1)])
                    bump(v.tensor_tensor(SIGB[0][:], X0B[k % 4][:],
                                         ATTNB[k % 2][:], OP.is_gt),
                         "V", ("sig", k))

            def p1_tsg(k):
                for grp in range(4):
                    @blk.tensor
                    def _(t, grp=grp, k=k):
                        if grp == 0:
                            t.wait_ge(SEM["V"], tick[("sig", k)])
                        e, tk_ = pst_free[0]
                        t.wait_ge(SEM[e], tk_)
                        for j4 in range(4):
                            j = grp * 4 + j4
                            _mi = t.transpose(PST[:, 128 * j4:128 * (j4 + 1)],
                                        SIGB[0][:, 128 * j:128 * (j + 1)],
                                        SID[:])
                        bump(_mi, "P", ("trs", k, grp))

                    @blk.vector
                    def _(v, grp=grp, k=k):
                        v.wait_ge(SEM["P"], tick[("trs", k, grp)])
                        bump(v.tensor_copy(
                            SGT[0][:, 512 * grp:512 * (grp + 1)], PST[:]),
                            "V", ("evS", k, grp))
                        pst_free[0] = ("V", tick[("evS", k, grp)])
                tick[("sgt", k)] = tick[("trs", k, 3)]

            def p1_scores(k):
                @blk.tensor
                def _(t, k=k):
                    t.wait_ge(SEM["V"], tick[("evS", k, 3)])
                    if k == 0:
                        t.wait_ge(SEM["G"], tick["winit"])
                    else:
                        t.wait_ge(SEM["V"], tick[("gbev", k - 1)])
                    for j in range(16):
                        _mi = t.matmul(PSS[:, 0:64],
                                 SGT[0][:, 128 * j:128 * (j + 1)],
                                 WSS[:, 64 * j:64 * (j + 1)],
                                 start=(j == 0), stop=False,
                                 skip_group_check=True)
                    _mi = t.matmul(PSS[:, 0:64], ONE1[:], BIAR[:], start=False,
                             stop=True, skip_group_check=True)
                    bump(_mi, "P", ("sc", k))

                @blk.vector
                def _(v, k=k):
                    v.wait_ge(SEM["P"], tick[("sc", k)])
                    bump(v.tensor_reduce(
                        CS8[:], PSS[:, 0:64].rearrange("p (a b) -> p a b", a=8),
                        AX.X, OP.add), "V", ("cs8", k))
                    bump(v.tensor_copy(TSC[:], PSS[:, 0:64]), "V", ("tsev", k))
                    v.wait_ge(SEM["V"], tick[("cs8", k)])
                    bump(v.max(MX8[:], CS8[:]), "V", ("cmx", k))
                    v.wait_ge(SEM["V"], tick[("cmx", k)])
                    bump(v.max_index(IX8[:], MX8[:], CS8[:]), "V", ("cix", k))
                    v.wait_ge(SEM["V"], tick[("cix", k)])
                    bump(v.tensor_copy(BCF[:], IX8[:, 0:1]), "V", ("bcf", k))
                    v.wait_ge(SEM["V"], tick[("bcf", k)])
                    bump(v.tensor_scalar(TSM[:], CLB[:], BCF[:], None,
                                         OP.is_equal), "V", ("m64", k))
                    v.wait_ge(SEM["V"], tick[("m64", k)])
                    bump(v.scalar_tensor_tensor(TSM[:], TSC[:], BIG, TSM[:],
                                                OP.add, OP.mult),
                         "V", ("msk", k))
                    v.wait_ge(SEM["V"], tick[("msk", k)])
                    bump(v.max(MX8b[:], TSM[:]), "V", ("tmx", k))
                    v.wait_ge(SEM["V"], tick[("tmx", k)])
                    bump(v.max_index(IX8b[:], MX8b[:], TSM[:]),
                         "V", ("tix", k))
                    v.wait_ge(SEM["V"], tick[("tix", k)])
                    bump(v.tensor_copy(TIDA[:, k:k + 1], IX8b[:, 0:1]),
                         "V", ("tid", k))
                    v.wait_ge(SEM["V"], tick[("tid", k)])
                    bump(v.tensor_scalar(OHB[:, 64 * k:64 * (k + 1)], TIOT[:],
                                         TIDA[:, k:k + 1], None, OP.is_equal),
                         "V", ("oh", k))
                    bump(v.scalar_tensor_tensor(IDXF[:, k:k + 1],
                                                TOKI[:, k:k + 1], 64.0,
                                                TIDA[:, k:k + 1], OP.mult,
                                                OP.add), "V", ("idxf", k))
                    v.wait_ge(SEM["V"], tick[("idxf", k)])
                    bump(v.tensor_copy(IDXU[:, k:k + 1], IDXF[:, k:k + 1]),
                         "V", ("idxu", k))

                @blk.tensor
                def _(t, k=k):
                    t.wait_ge(SEM["V"], tick[("oh", k)])
                    t.wait_ge(SEM["V"], tick[("tsev", k)])
                    _mi = t.transpose(PSSb[0:64, 512:640],
                                OHB[:, 64 * k:64 * (k + 1)], SID[:])
                    bump(_mi, "P", ("oht", k))

                @blk.vector
                def _(v, k=k):
                    v.wait_ge(SEM["P"], tick[("oht", k)])
                    bump(v.tensor_copy(OHT[:, 128 * k:128 * (k + 1)],
                                       PSSb[0:64, 512:640]), "V", ("ohtev", k))

                @blk.tensor
                def _(t, k=k):
                    t.wait_ge(SEM["V"], tick[("ohtev", k)])
                    if k == 0:
                        t.wait_ge(SEM["V"], tick["gbl"])
                    _mi = t.matmul(PSS[:, 64:128], OHT[:, 128 * k:128 * (k + 1)],
                             GBH[:], start=True, stop=False,
                             skip_group_check=True)
                    _mi = t.matmul(PSS[:, 64:128], OHT[:, 128 * k:128 * (k + 1)],
                             GBL[:], start=False, stop=True,
                             skip_group_check=True)
                    bump(_mi, "P", ("gbmm", k))

                @blk.vector
                def _(v, k=k):
                    v.wait_ge(SEM["P"], tick[("gbmm", k)])
                    bump(v.tensor_copy(GBA[:, 64 * k:64 * (k + 1)],
                                       PSS[:, 64:128]), "V", ("gbev", k))


            for s in range(NSUP):
                for c in range(SS):
                    k = s * SS + c
                    p1_load(k)
                    p1_pre(k)
                    p1_txa(k)
                p1_mm1(s)
                for c in range(SS):
                    k = s * SS + c
                    p1_mm2(k)
                    p1_post(k)
                    p1_tsg(k)
                    p1_scores(k)

            @blk.gpsimd
            def _(g):
                for q in range(4):
                    g.wait_ge(SEM[f"SP{q}"], tick[("sp", T - 4 + q)])
                g.wait_ge(SEM["V"], tick[("idxu", T - 1)])
                for k in range(T):
                    dbump(g.indirect_dma_start(
                        out=XSL[:, 32 * k:32 * (k + 1)], out_offset=None,
                        in_=x1sp.ap(),
                        in_offset=bass.IndirectOffsetOnAxis(
                            ap=IDXU[:, k:k + 1], axis=0)), "PD", ("xg", k))

            @blk.vector
            def _(v):
                v.wait_ge(SEM["PD"], tick[("xg", T - 1)])
                v.wait_ge(SEM["A"], tick[("sq1", T - 1)])
                bump(v.tensor_tensor(XNS[:], XSL[:],
                                MU1[:].to_broadcast([128, T, 32]), OP.subtract), "V", "h1")
                v.wait_ge(SEM["V"], tick["h1"])
                bump(v.tensor_tensor(XNS[:], XNS[:],
                                R12A[:].to_broadcast([128, T, 32]), OP.mult), "V", "h3")
                v.wait_ge(SEM["V"], tick["h3"])
                bump(v.tensor_tensor(XNS[:], XNS[:],
                                bass.AP(GBA, 0, [[T * 64, 128], [64, T],
                                                 [1, 32]]), OP.mult), "V", "h4")
                v.wait_ge(SEM["V"], tick["h4"])
                bump(v.tensor_tensor(XNS[:], XNS[:],
                                bass.AP(GBA, 32, [[T * 64, 128], [64, T],
                                                  [1, 32]]), OP.add), "V", "xns")
                v.wait_ge(SEM["V"], tick["xns"])
                v.tensor_reduce(TMIN[:],
                                XNS[:].rearrange("p (a b) -> p a b", a=T),
                                AX.X, OP.min)
                bump(v.tensor_reduce(TMAX[:],
                                     XNS[:].rearrange("p (a b) -> p a b", a=T),
                                     AX.X, OP.max), "V", "tmm")
                v.wait_ge(SEM["V"], tick["tmm"])
                bump(v.tensor_scalar(
                    bass.AP(VMM, 0, [[2048, 128], [32, 64], [1, 32]]),
                    bass.AP(OHB, 0, [[T * 64, 128], [1, 64], [64, T]]),
                    -1e38, 1e38, OP.mult, OP.add), "V", "ms1")
                v.wait_ge(SEM["V"], tick["ms1"])
                bump(v.tensor_tensor(
                    bass.AP(VMM, 0, [[2048, 128], [32, 64], [1, 32]]),
                    bass.AP(VMM, 0, [[2048, 128], [32, 64], [1, 32]]),
                    bass.AP(TMIN, 0, [[T, 128], [0, 64], [1, T]]),
                    OP.add), "V", "vm1")
                v.wait_ge(SEM["V"], tick["vm1"])
                bump(v.tensor_reduce(
                    RED[:], VMM[:].rearrange("p (a b) -> p a b", a=64),
                    AX.X, OP.min), "V", "redmin")

            @blk.tensor
            def _(t):
                t.wait_ge(SEM["V"], tick["redmin"])
                t.wait_ge(SEM["V"], tick["sidf"])
                _mi = t.transpose(PSS[0:64, 128:256], RED[:], SIDF[:])
                bump(_mi, "P", "trmin")

            @blk.vector
            def _(v):
                v.wait_ge(SEM["P"], tick["trmin"])
                bump(v.tensor_reduce(PMM[:, 0:1],
                                     PSS[0:64, 128:256], AX.X, OP.min),
                     "V", "pmin")
                v.wait_ge(SEM["V"], tick["pmin"])
                bump(v.tensor_scalar(PMM[:, 0:1], PMM[:, 0:1], -1.0, None,
                                     OP.mult), "V", "pminneg")
                bump(v.tensor_scalar(
                    bass.AP(VMM, 0, [[2048, 128], [32, 64], [1, 32]]),
                    bass.AP(OHB, 0, [[T * 64, 128], [1, 64], [64, T]]),
                    1e38, -1e38, OP.mult, OP.add), "V", "ms2")
                v.wait_ge(SEM["V"], tick["ms2"])
                bump(v.tensor_tensor(
                    bass.AP(VMM, 0, [[2048, 128], [32, 64], [1, 32]]),
                    bass.AP(VMM, 0, [[2048, 128], [32, 64], [1, 32]]),
                    bass.AP(TMAX, 0, [[T, 128], [0, 64], [1, T]]),
                    OP.add), "V", "vx1")
                v.wait_ge(SEM["V"], tick["vx1"])
                bump(v.tensor_reduce(
                    RED[:], VMM[:].rearrange("p (a b) -> p a b", a=64),
                    AX.X, OP.max), "V", "redmax")

            @blk.tensor
            def _(t):
                t.wait_ge(SEM["V"], tick["redmax"])
                _mi = t.transpose(PSS[0:64, 256:384], RED[:], SIDF[:])
                bump(_mi, "P", "trmax")

            @blk.vector
            def _(v):
                v.wait_ge(SEM["P"], tick["trmax"])
                bump(v.tensor_reduce(PMM[:, 1:2], PSS[0:64, 256:384], AX.X,
                                     OP.max), "V", "pmax")

        @blk.gpsimd
        def _(g):
            g.wait_ge(SEM["V"], tick["pmax"])
            g.wait_ge(SEM["V"], tick["pminneg"])
            dbump(g.dma_start(bass.AP(cc_in, 0, [[2, 64], [1, 2]]), PMM[:]),
                  "PD")
            g.wait_ge(SEM["PD"], cnt["PD"])
            g.collective_compute(
                "AllReduce", OP.max, replica_groups=[list(range(NCORES))],
                ins=[cc_in.ap().opt()], outs=[cc_out.ap().opt()],
            ).then_inc(SEM["CC"], 1)
            cnt["CC"] += 1
            g.wait_ge(SEM["CC"], cnt["CC"])
            dbump(g.dma_start(GMM[:], bass.AP(cc_out, 0, [[2, 64], [1, 2]])),
                  "PD", "ccb")
            bump(g.wait_ge(SEM["PD"], cnt["PD"]), "G", "ccdone")

        if True:
            PSS2 = PSS
            PSP = [PS2.ap()[:, 0:1024], PS2.ap()[:, 1024:2048]]
            XATF = XAT.ap().bitcast(F32)
            XATB = XAT.ap()
            STTF = STT.ap().bitcast(F32)
            UAL = XATF[:, 0:1024]
            GIF = XATF[:, 1024:2048]
            DPP = XATF[:, 2048:3072]
            GIOT = XATB[:, 6656:7168]
            GMH = XATB[0:64, 7168:7171]
            GML = XATB[0:64, 7200:7203]
            MNX = STTF[:, 0:96]
            OHG = VMM[:, 1024:1536]
            GIU = bass.AP(VMM, 1536, [[T * 64, 128], [1, 512]]).bitcast(I32) \
                if False else None
            GIU = bass.AP(VMM, 1024, [[T * 64, 128], [1, 1024]]).bitcast(I32)
            XLC = UAL
            BSA = GBA

            @blk.gpsimd
            def _(g):
                _gi = g.iota(GIOT, [[0, 32], [1, 16]], channel_multiplier=0,
                       allow_small_or_imprecise_dtypes=True)
                bump(_gi, "G", "giot")

            @blk.vector
            def _(v):
                v.wait_ge(SEM["G"], tick["ccdone"])
                v.tensor_scalar(VMM[0:64, 0:1], GMM[:, 0:1], -1.0, None, OP.mult)
                v.tensor_copy(VMM[0:64, 1:2], GMM[:, 1:2])
                bump(v.tensor_copy(VMM[0:64, 2:3], SOSC[:]), "V", "gm0")
                v.wait_ge(SEM["V"], tick["gm0"])
                bump(v.tensor_copy(GMH, VMM[0:64, 0:3]), "V", "gmh")
                v.wait_ge(SEM["V"], tick["gmh"])
                bump(v.tensor_copy(VMM[0:64, 3:6], GMH), "V", "gmb")
                v.wait_ge(SEM["V"], tick["gmb"])
                bump(v.tensor_tensor(VMM[0:64, 6:9], VMM[0:64, 0:3], VMM[0:64, 3:6],
                                     OP.subtract), "V", "gml0")
                v.wait_ge(SEM["V"], tick["gml0"])
                bump(v.tensor_copy(GML, VMM[0:64, 6:9]), "V", "gml")

            for k in range(T):
                @blk.tensor
                def _(t, k=k):
                    if k == 0:
                        t.wait_ge(SEM["V"], tick["gml"])
                    _mi = t.matmul(PSS2[:, 3 * k:3 * (k + 1)],
                             OHT[:, 128 * k:128 * (k + 1)], GMH,
                             start=(k == 0), stop=False, skip_group_check=True)
                    _mi = t.matmul(PSS2[:, 3 * k:3 * (k + 1)],
                             OHT[:, 128 * k:128 * (k + 1)], GML,
                             start=False, stop=True, skip_group_check=True)
                    if k == T - 1:
                        bump(_mi, "P", "mnmx")

            @blk.vector
            def _(v):
                v.wait_ge(SEM["P"], tick["mnmx"])
                bump(v.tensor_copy(MNX, PSS2[:, 0:96]), "V", "mnxev")
                v.wait_ge(SEM["V"], tick["mnxev"])
                m3 = MNX.rearrange("p (a b) -> p a b", a=T)
                mnv = m3[:, :, 0:1]
                mxv = m3[:, :, 1:2]
                bump(v.tensor_tensor(W32a[:], mxv, mnv, OP.subtract),
                     "V", "den0")
                v.wait_ge(SEM["V"], tick["den0"])
                bump(v.tensor_scalar(W32a[:], W32a[:], 1e-8, None, OP.add),
                     "V", "den1")
                v.wait_ge(SEM["V"], tick["den1"])
                bump(v.reciprocal(INV[:], W32a[:]), "V", "inv")
                v.wait_ge(SEM["V"], tick["inv"])
                bump(v.tensor_tensor(UAL, XNS[:],
                                m3[:, :, 0:1].to_broadcast([128, T, 1, 32]),
                                OP.subtract), "V", "u0")
                v.wait_ge(SEM["V"], tick["u0"])
                bump(v.tensor_tensor(UAL, UAL,
                                INV[:].to_broadcast([128, T, 32]), OP.mult), "V", "u1")
                v.wait_ge(SEM["V"], tick["u1"])
                bump(v.tensor_scalar(UAL, UAL, 0.0, 1.0 - 1e-6, OP.max,
                                OP.min), "V", "u2")
                v.wait_ge(SEM["V"], tick["u2"])
                bump(v.tensor_scalar(GIF, UAL, 16.0, None, OP.mult), "V", "g16")
                v.wait_ge(SEM["V"], tick["g16"])
                bump(v.tensor_copy(GIU, GIF), "V", "giu")
                v.wait_ge(SEM["V"], tick["giu"])
                bump(v.tensor_copy(GIF, GIU), "V", "gif")
                v.wait_ge(SEM["V"], tick["gif"])
                bump(v.scalar_tensor_tensor(XLC[:], UAL, 16.0, GIF, OP.mult,
                                       OP.subtract), "V", "xlc")

            for k in range(T):
                @blk.tensor
                def _(t, k=k):
                    if k == 0:
                        t.wait_ge(SEM["V"], tick["tbl"])
                        t.wait_ge(SEM["V"], tick["xlc"])
                    if k >= 2:
                        t.wait_ge(SEM["V"], tick[("bs", k - 2)])
                    for h in range(2):
                        _mi = t.matmul(PSP[k % 2][:, 512 * h:512 * (h + 1)],
                                 OHT[:, 128 * k:128 * (k + 1)],
                                 TBH[:, 512 * h:512 * (h + 1)],
                                 start=True, stop=False, skip_group_check=True)
                        _mi = t.matmul(PSP[k % 2][:, 512 * h:512 * (h + 1)],
                                 OHT[:, 128 * k:128 * (k + 1)],
                                 TBL[:, 512 * h:512 * (h + 1)],
                                 start=False, stop=True, skip_group_check=True)
                    bump(_mi, "P", ("pg", k))

                @blk.vector
                def _(v, k=k):
                    v.wait_ge(SEM["P"], tick[("pg", k)])
                    if k == 0:
                        v.wait_ge(SEM["G"], tick["giot"])
                    bump(v.tensor_tensor(
                        OHG, GIF[:, 32 * k:32 * (k + 1)]
                        .to_broadcast([128, 32, 16]), GIOT, OP.is_equal), "V", ("ohg", k))
                    v.wait_ge(SEM["V"], tick[("ohg", k)])
                    bump(v.tensor_tensor(VMM[:, 0:1024], bass.AP(
                        VMM, 1024, [[T * 64, 128], [0, 2], [1, 512]]),
                        PSP[k % 2], OP.mult), "V", ("prod", k))
                    v.wait_ge(SEM["V"], tick[("prod", k)])
                    bump(v.tensor_reduce(
                        BSA[:, 64 * k:64 * (k + 1)],
                        VMM[:, 0:1024].rearrange("p (a b) -> p a b", a=64),
                        AX.X, OP.add), "V", ("bs", k))

            @blk.vector
            def _(v):
                v.wait_ge(SEM["V"], tick[("bs", T - 1)])
                bview = bass.AP(BSA, 0, [[T * 64, 128], [64, T], [1, 32]])
                sview = bass.AP(BSA, 32, [[T * 64, 128], [64, T], [1, 32]])
                oview = MNX.rearrange("p (a b) -> p a b", a=T)[:, :, 2:3] \
                    .to_broadcast([128, T, 1, 32])
                bump(v.tensor_tensor(DPP, sview, XLC[:], OP.mult), "V", "d1")
                v.wait_ge(SEM["V"], tick["d1"])
                bump(v.tensor_tensor(DPP, DPP, bview, OP.add), "V", "d2")
                v.wait_ge(SEM["V"], tick["d2"])
                bump(v.tensor_tensor(DPP, DPP, oview, OP.mult), "V", "delta")
                v.wait_ge(SEM["V"], tick["delta"])
                bump(v.tensor_reduce(W32a[:],
                                DPP.rearrange("p (a b) -> p a b", a=T),
                                AX.X, OP.add), "V", "sd")
                v.wait_ge(SEM["V"], tick["sd"])
                bump(v.tensor_scalar(MUG[:], W32a[:], 1.0 / D, None, OP.mult), "V", "mug")
                bump(v.tensor_tensor(VMM[:, 0:T * 32], XSL[:],
                                     MU1[:].to_broadcast([128, T, 32]),
                                     OP.subtract), "V", "hn0")
                v.wait_ge(SEM["V"], tick["hn0"])
                bump(v.tensor_tensor(VMM[:, 0:T * 32], VMM[:, 0:T * 32],
                                     R1A[:].to_broadcast([128, T, 32]),
                                     OP.mult), "V", "hn1")
                v.wait_ge(SEM["V"], tick["hn1"])
                bump(v.tensor_tensor(VMM[:, T * 32:T * 64], VMM[:, 0:T * 32],
                                     DPP, OP.mult), "V", "cr0")
                v.wait_ge(SEM["V"], tick["cr0"])
                bump(v.tensor_reduce(W32b[:],
                                VMM[:, T * 32:T * 64].rearrange(
                                    "p (a b) -> p a b", a=T), AX.X, OP.add), "V", "cr")
                v.wait_ge(SEM["V"], tick["cr"])
                bump(v.tensor_tensor(VMM[:, T * 32:T * 64], DPP, DPP, OP.mult), "V", "dd0")
                v.wait_ge(SEM["V"], tick["dd0"])
                bump(v.tensor_reduce(W32c[:],
                                VMM[:, T * 32:T * 64].rearrange(
                                    "p (a b) -> p a b", a=T), AX.X, OP.add), "V", "ddq")
                v.wait_ge(SEM["V"], tick["ddq"])
                bump(v.tensor_tensor(W32a[:], R1A[:], R1A[:], OP.mult), "V", "q1")
                v.wait_ge(SEM["V"], tick["q1"])
                bump(v.tensor_tensor(W32a[:], W32a[:], V1A[:], OP.mult), "V", "q2")
                v.wait_ge(SEM["V"], tick["q2"])
                bump(v.scalar_tensor_tensor(W32a[:], W32a[:], float(D), W32b[:],
                                       OP.mult, OP.add), "V", "q3")
                v.wait_ge(SEM["V"], tick["q3"])
                bump(v.tensor_tensor(W32a[:], W32a[:], W32b[:], OP.add), "V", "q4")
                v.wait_ge(SEM["V"], tick["q4"])
                bump(v.tensor_tensor(W32a[:], W32a[:], W32c[:], OP.add), "V", "q5")
                v.wait_ge(SEM["V"], tick["q5"])
                bump(v.tensor_tensor(W32b[:], MUG[:], MUG[:], OP.mult), "V", "q6")
                v.wait_ge(SEM["V"], tick["q6"])
                bump(v.scalar_tensor_tensor(W32a[:], W32a[:], 1.0 / D, W32b[:],
                                       OP.mult, OP.subtract), "V", "vg")

            @blk.scalar
            def _(a):
                a.wait_ge(SEM["V"], tick["vg"])
                bump(a.activation(W32a[:], W32a[:], AF.Sqrt, bias=EPSC[:]),
                     "A", "lng")

            @blk.vector
            def _(v):
                v.wait_ge(SEM["A"], tick["lng"])
                bump(v.reciprocal(RGA[:], W32a[:]), "V", "rg")
                v.wait_ge(SEM["V"], tick["rg"])
                bump(v.tensor_tensor(W32a[:], R1A[:], RGA[:], OP.mult), "V", "s20")
                v.wait_ge(SEM["V"], tick["s20"])
                bump(v.tensor_scalar(S2A[:], W32a[:], 1.0, None, OP.add),
                     "V", "s2")
                bump(v.tensor_tensor(W32b[:], MU1[:], R1A[:], OP.mult), "V", "b20")
                v.wait_ge(SEM["V"], tick["b20"])
                bump(v.tensor_tensor(W32b[:], W32b[:], MUG[:], OP.add), "V", "b21")
                v.wait_ge(SEM["V"], tick["b21"])
                bump(v.tensor_tensor(W32b[:], W32b[:], RGA[:], OP.mult), "V", "b22")
                v.wait_ge(SEM["V"], tick["b22"])
                bump(v.tensor_scalar(B2A[:], W32b[:], -1.0, None, OP.mult),
                     "V", "b2")
                bump(v.tensor_tensor(DPP, DPP, VMM[:, 0:T * 32], OP.add), "V", "e1")
                v.wait_ge(SEM["V"], tick["e1"])
                bump(v.tensor_tensor(DPP, DPP,
                                MUG[:].to_broadcast([128, T, 32]), OP.subtract), "V", "e2")
                v.wait_ge(SEM["V"], tick["e2"])
                bump(v.tensor_tensor(DPP, DPP,
                                RGA[:].to_broadcast([128, T, 32]), OP.mult), "V", "e3")
                v.wait_ge(SEM["V"], tick["e3"])
                bump(v.tensor_tensor(DPP, DPP, XSL[:], OP.add), "V", "dpp")

            for k in range(T):
                @blk.sync
                def _(sy, k=k):
                    if k == 0:
                        for q in range(4):
                            sy.wait_ge(SEM[f"SP{q}"], tick[("sp", T - 4 + q)])
                        sy.wait_ge(SEM["V"], tick[("sig", T - 1)])
                        sy.wait_ge(SEM["A"], tick[("sq1", T - 1)])
                    if k >= 2:
                        sy.wait_ge(SEM[f"OS{k % 2}"], tick[("st", k - 2)])
                    dbump(sy.dma_start(
                        X0B[k % 2][:],
                        bass.AP(x1sp, k * 128 * D, [[D, 128], [1, D]])),
                        f"RL{k % 2}", ("rl", k))

                @blk.scalar
                def _(a, k=k):
                    a.wait_ge(SEM[f"RL{k % 2}"], tick[("rl", k)])
                    if k == 0:
                        a.wait_ge(SEM["V"], tick["b2"])
                    bump(a.activation(X0B[k % 2][:], X0B[k % 2][:],
                                      AF.Identity, bias=B2A[:, k:k + 1],
                                      scale=S2A[:, k:k + 1]), "A", ("t2", k))

                @blk.sync
                def _(sy, k=k):
                    sy.wait_ge(SEM["A"], tick[("t2", k)])
                    dbump(sy.dma_start(
                        bass.AP(outd, k * 128 * D, [[D, 128], [1, D]]),
                        X0B[k % 2][:]), f"OS{k % 2}", ("st", k))

                @blk.gpsimd
                def _(g, k=k):
                    g.wait_ge(SEM[f"OS{k % 2}"], tick[("st", k)])
                    if k == 0:
                        g.wait_ge(SEM["V"], tick["dpp"])
                    dbump(g.indirect_dma_start(
                        out=bass.AP(outd, 0, [[32, TOK * 64], [1, 32]]),
                        out_offset=bass.IndirectOffsetOnAxis(
                            ap=IDXU[:, k:k + 1], axis=0),
                        in_=DPP[:, 32 * k:32 * (k + 1)], in_offset=None),
                        "PD", ("scat", k))

            if debug:
                @blk.sync
                def _(sy):
                    sy.wait_ge(SEM["V"], tick["dpp"])
                    dbump(sy.dma_start(dbg1d.ap(), XNS[:, 0:D]), "OS0")
                    dbump(sy.dma_start(dbg2d.ap(), BSA[:, 0:2048]), "OS0")

            @blk.sync
            def _(sy):
                sy.wait_ge(SEM["OS0"], cnt["OS0"])
                sy.wait_ge(SEM["OS1"], cnt["OS1"])
                sy.wait_ge(SEM["PD"], cnt["PD"])

    return nc


_PROG_CACHE = {}


def _get_prog(debug=False):
    key = bool(debug)
    if key not in _PROG_CACHE:
        _PROG_CACHE[key] = build_program(debug=key)
    return _PROG_CACHE[key]


def make_in_maps(L0_attn_x, L0_attn_B, L0_attn_C, L0_attn_D, L0_ffn_ln_gamma,
                 L0_ffn_ln_beta, L0_ffn_spline_bases, L0_ffn_spline_slopes,
                 L0_ffn_output_scale):
    x = np.ascontiguousarray(np.asarray(L0_attn_x, dtype=np.float32)
                             .reshape(NCORES * TOK, D))
    bt = np.ascontiguousarray(np.asarray(L0_attn_B, dtype=np.float32).T)
    ct = np.ascontiguousarray(np.asarray(L0_attn_C, dtype=np.float32).T)
    dcol = np.ascontiguousarray(
        np.asarray(L0_attn_D, dtype=np.float32).reshape(16, 128).T)
    gam = np.ascontiguousarray(
        np.asarray(L0_ffn_ln_gamma, dtype=np.float32).reshape(64, 32))
    bet = np.ascontiguousarray(
        np.asarray(L0_ffn_ln_beta, dtype=np.float32).reshape(64, 32))
    bas = np.ascontiguousarray(
        np.asarray(L0_ffn_spline_bases, dtype=np.float32).reshape(64, 512))
    slo = np.ascontiguousarray(
        np.asarray(L0_ffn_spline_slopes, dtype=np.float32).reshape(64, 512))
    osc = np.ascontiguousarray(
        np.asarray(L0_ffn_output_scale, dtype=np.float32).reshape(64, 1))
    return [{"x0": x[core * TOK:(core + 1) * TOK],
             "bt": bt, "ct": ct, "dcol": dcol, "gam": gam, "bet": bet,
             "bas": bas, "slo": slo, "osc": osc} for core in range(NCORES)]


def kernel(**inputs):
    in_maps = make_in_maps(**inputs)
    nc = _get_prog(debug=False)
    res = run_bass_kernel_spmd(nc, in_maps, core_ids=list(range(NCORES)))
    out = np.concatenate([r["out"] for r in res.results], axis=0)
    return out.reshape(2, 16384, D).astype(np.float32)


# version 04
# speedup vs baseline: 1.0976x; 1.0976x over previous
import numpy as np
from contextlib import ExitStack

import concourse.bass as bass
import concourse.mybir as mybir
from concourse.alu_op_type import AluOpType as OP
from concourse.bass_utils import run_bass_kernel_spmd

F32 = mybir.dt.float32
BF16 = mybir.dt.bfloat16
U32 = mybir.dt.uint32
I32 = mybir.dt.int32
AF = mybir.ActivationFunctionType
AX = mybir.AxisListType

NCORES = 8
TOK = 4096
T = 32
SS = 4
NSUP = T // SS
D = 2048
NS = 1024
EPS = 1e-5
BIG = 1000.0


def build_program(debug=False):
    nc = bass.Bass("TRN2", target_bir_lowering=False)

    x0d = nc.dram_tensor("x0", [TOK, D], F32, kind="ExternalInput")
    btd = nc.dram_tensor("bt", [D, NS], F32, kind="ExternalInput")
    ctd = nc.dram_tensor("ct", [NS, D], F32, kind="ExternalInput")
    dcold = nc.dram_tensor("dcol", [128, 16], F32, kind="ExternalInput")
    gamd = nc.dram_tensor("gam", [64, 32], F32, kind="ExternalInput")
    betd = nc.dram_tensor("bet", [64, 32], F32, kind="ExternalInput")
    basd = nc.dram_tensor("bas", [64, 512], F32, kind="ExternalInput")
    slod = nc.dram_tensor("slo", [64, 512], F32, kind="ExternalInput")
    oscd = nc.dram_tensor("osc", [64, 1], F32, kind="ExternalInput")
    outd = nc.dram_tensor("out", [TOK, D], F32, kind="ExternalOutput")
    if debug:
        dbg1d = nc.dram_tensor("dbg1", [128, D], F32, kind="ExternalOutput")
        dbg2d = nc.dram_tensor("dbg2", [128, 2048], F32, kind="ExternalOutput")

    x1sp = nc.dram_tensor("x1sp", [TOK * 64, 32], F32)
    th_d = nc.dram_tensor("th_d", [1, D], F32)
    w2_d = nc.dram_tensor("w2_d", [64, 32], BF16)
    bia_d = nc.dram_tensor("bia_d", [1, 64], BF16)
    cc_in = nc.dram_tensor("cc_in", [1, 128], F32)
    cc_out = nc.dram_tensor("cc_out", [1, 128], F32)

    with ExitStack() as st:
        def sb(name, shape, dt):
            return st.enter_context(nc.sbuf_tensor(name, shape, dt))

        BT = sb("BT", [128, 16 * NS], BF16)
        CT = sb("CT", [128, 8 * D], BF16)
        TOKI = sb("TOKI", [128, T], F32)
        EPSC = sb("EPSC", [128, 1], F32)
        VMM = sb("VMM", [128, T * 64], F32)
        SOSC = sb("SOSC", [64, 1], F32)
        TBH = sb("TBH", [64, 1024], BF16)
        TBL = sb("TBL", [64, 1024], BF16)
        GMM = sb("GMM", [64, 2], F32)
        PMM = sb("PMM", [64, 2], F32)

        X0B = [sb(f"X0B{i}", [128, D], F32) for i in range(4)]
        ATTNB = [sb(f"ATTNB{i}", [128, D], F32) for i in range(2)]

        MU0 = sb("MU0", [128, T], F32)
        BNS = sb("BNS", [128, 24], F32)
        MUV_ = sb("MUV", [128, 2 * T], F32)
        SM0 = sb("SM0", [128, T], F32)
        SQ0 = sb("SQ0", [128, T], F32)
        V0C = sb("V0C", [128, T], F32)
        R0A = sb("R0A", [128, T], F32)
        B0A = sb("B0A", [128, T], F32)
        SMA = sb("SMA", [128, T], F32)
        SQA = sb("SQA", [128, T], F32)
        MUA = sb("MUA", [128, T], F32)
        VAC = sb("VAC", [128, T], F32)
        RAA = sb("RAA", [128, T], F32)
        BAA = sb("BAA", [128, T], F32)
        SM1 = sb("SM1", [128, T], F32)
        SQ1 = sb("SQ1", [128, T], F32)
        MU1 = sb("MU1", [128, T], F32)
        V1A = sb("V1A", [128, T], F32)
        R1A = sb("R1A", [128, T], F32)
        V2C = sb("V2C", [128, T], F32)
        S12 = sb("S12", [128, T], F32)
        TIDA = sb("TIDA", [128, T], F32)
        IDXF = sb("IDXF", [128, T], F32)
        IDXU = sb("IDXU", [128, T], U32)
        R12A = sb("R12A", [128, T], F32)
        TMIN = sb("TMIN", [128, T], F32)
        TMAX = sb("TMAX", [128, T], F32)
        MUG = sb("MUG", [128, T], F32)
        RGA = sb("RGA", [128, T], F32)
        S2A = sb("S2A", [128, T], F32)
        B2A = sb("B2A", [128, T], F32)
        INV = sb("INV", [128, T], F32)
        W32a = sb("W32a", [128, T], F32)
        W32b = sb("W32b", [128, T], F32)
        W32c = sb("W32c", [128, T], F32)


        OHT = sb("OHT", [64, T * 128], BF16)
        GBA = sb("GBA", [128, T * 64], F32)
        XSL = sb("XSL", [128, T * 32], F32)
        XNS = sb("XNS", [128, T * 32], F32)
        RED = sb("RED", [128, 64], F32)

        SEM = {n: st.enter_context(nc.semaphore(n)) for n in
               ["LD", "PD", "CC", "V", "A", "P", "G",
                "SP0", "SP1", "SP2", "SP3",
                "LX0", "LX1", "LX2", "LX3", "RL0", "RL1", "OS0", "OS1"]}
        cnt = {n: 0 for n in SEM}
        tick = {}

        def bump(inst, e, key=None):
            cnt[e] += 1
            inst.then_inc(SEM[e], 1)
            if key is not None:
                tick[key] = cnt[e]
            return cnt[e]

        def dbump(inst, e, key=None):
            cnt[e] += 16
            inst.then_inc(SEM[e], 16)
            if key is not None:
                tick[key] = cnt[e]
            return cnt[e]

        blk = st.enter_context(nc.Block())

        pst_free = [("V", 0)]

        if True:
            PS1 = [st.enter_context(nc.psum_tensor("PS1a", [128, 512], F32)),
                   st.enter_context(nc.psum_tensor("PS1b", [128, 512], F32))]
            PS2 = st.enter_context(nc.psum_tensor("PS2", [128, 2048], F32))
            PST = st.enter_context(nc.psum_tensor("PST", [128, 512], BF16))
            PSS = st.enter_context(nc.psum_tensor("PSS", [128, 512], F32))
            PSSb = PSS.ap().bitcast(BF16)
            sb1 = sb
            DD = sb1("DD", [128, 16 * 128], BF16)
            DCOL = sb1("DCOL", [128, 16], F32)
            WSS = sb1("WSS", [128, 16 * 64], BF16)
            THB = sb1("THB", [128, D], BF16)
            SID = sb1("SID", [128, 128], BF16)
            SIDF = sb1("SIDF", [128, 128], F32)
            ONE1 = sb1("ONE1", [1, 128], BF16)
            BIAR = sb1("BIAR", [1, 64], BF16)
            TIOT = sb1("TIOT", [128, 64], F32)
            CLB = sb1("CLB", [128, 64], F32)
            IOT128 = sb1("IOT128", [128, 128], F32)
            PIDX = sb1("PIDX", [128, 1], F32)
            SGAM = sb1("SGAM", [64, 32], F32)
            SBET = sb1("SBET", [64, 32], F32)
            GBH = sb1("GBH", [64, 64], BF16)
            GBL = sb1("GBL", [64, 64], BF16)
            W2BF = sb1("W2BF", [64, 32], BF16)
            BI64 = sb1("BI64", [64, 1], BF16)
            XAB = [sb("XAB0", [128, D], BF16)]
            SIGB = [sb("SIGB0", [128, D], BF16)]
            XAT = sb1("XAT", [128, SS * D], BF16)
            STT = sb1("STT", [128, 8 * 512], BF16)
            SGT = [sb("SGT0", [128, D], BF16)]
            CS8 = sb1("CS8", [128, 8], F32)
            MX8 = sb1("MX8", [128, 8], F32)
            IX8 = sb1("IX8", [128, 8], U32)
            TSC = sb1("TSC", [128, 64], F32)
            TSM = sb1("TSM", [128, 64], F32)
            MX8b = sb1("MX8b", [128, 8], F32)
            IX8b = sb1("IX8b", [128, 8], U32)
            BCF = sb1("BCF", [128, 1], F32)
            OHB = sb1("OHB", [128, T * 64], BF16)

            @blk.sync
            def _(sy):
                dbump(sy.dma_start(SGAM[:], gamd.ap()), "LD")
                dbump(sy.dma_start(SBET[:], betd.ap()), "LD")
                dbump(sy.dma_start(VMM[0:64, 0:512], basd.ap()), "LD")
                dbump(sy.dma_start(VMM[0:64, 512:1024], slod.ap()), "LD")
                dbump(sy.dma_start(SOSC[:], oscd.ap()), "LD")
                dbump(sy.dma_start(DCOL[:], dcold.ap()), "LD", "ldconst")

            @blk.gpsimd
            def _(g):
                g.memset(EPSC[:], EPS)
                g.memset(ONE1[:], 1.0)
                g.memset(WSS[:], 0)
                _gi = g.iota(TIOT[:], [[1, 64]], channel_multiplier=0,
                       allow_small_or_imprecise_dtypes=True)
                _gi = g.iota(CLB[:], [[1, 8], [0, 8]], channel_multiplier=0,
                       allow_small_or_imprecise_dtypes=True)
                _gi = g.iota(TOKI[:], [[128, T]], channel_multiplier=1,
                       allow_small_or_imprecise_dtypes=True)
                _gi = g.iota(PIDX[:], [[0, 1]], channel_multiplier=1,
                       allow_small_or_imprecise_dtypes=True)
                _gi = g.iota(IOT128[:], [[1, 128]], channel_multiplier=0,
                       allow_small_or_imprecise_dtypes=True)
                bump(_gi, "G", "iotas")
                dbump(g.dma_start(
                    BT[:], bass.AP(btd, 0, [[NS, 128], [128 * NS, 16], [1, NS]])),
                    "PD")
                dbump(g.dma_start(
                    CT[:], bass.AP(ctd, 0, [[D, 128], [128 * D, 8], [1, D]])),
                    "PD", "wload")
                g.wait_ge(SEM["PD"], cnt["PD"])

            @blk.vector
            def _(v):
                v.wait_ge(SEM["G"], tick["iotas"])
                bump(v.tensor_scalar(SID[:], IOT128[:], PIDX[:], None,
                                     OP.is_equal), "V", "sid")
                v.wait_ge(SEM["V"], tick["sid"])
                bump(v.tensor_copy(SIDF[:], SID[:]), "V", "sidf")
                v.wait_ge(SEM["LD"], tick["ldconst"])
                for j in range(16):
                    _vi = v.tensor_scalar(DD[:, 128 * j:128 * (j + 1)], SID[:],
                                          DCOL[:, j:j + 1], None, OP.mult)
                bump(_vi, "V", "ddiag")
                bump(v.reciprocal(VMM[0:64, 1300:1332], SGAM[:]), "V", "recg")
                v.wait_ge(SEM["V"], tick["recg"])
                bump(v.tensor_tensor(VMM[0:64, 1332:1364], VMM[0:64, 1300:1332],
                                     SBET[:], OP.mult), "V", "th0")
                v.wait_ge(SEM["V"], tick["th0"])
                bump(v.tensor_scalar(VMM[0:64, 1332:1364], VMM[0:64, 1332:1364],
                                     -1.0, None, OP.mult), "V", "theta")
                v.wait_ge(SEM["LD"], tick["ldconst"])
                bump(v.tensor_reduce(VMM[0:64, 1364:1396],
                                     VMM[0:64, 512:1024].rearrange(
                                         "p (a b) -> p a b", a=32),
                                     AX.X, OP.add), "V", "ssum")

            @blk.gpsimd
            def _(g):
                g.wait_ge(SEM["V"], tick["theta"])
                dbump(g.dma_start(th_d.ap(), VMM[0:64, 1332:1364]), "PD", "thdma")
                g.wait_ge(SEM["PD"], cnt["PD"])
                dbump(g.dma_start(THB[:], bass.AP(th_d, 0, [[0, 128], [1, D]])),
                      "PD", "thbcast")

            @blk.scalar
            def _(a):
                a.wait_ge(SEM["V"], tick["ssum"])
                bump(a.sign(VMM[0:64, 1396:1428], VMM[0:64, 1364:1396]),
                     "A", "ssgn")
                bump(a.sign(VMM[0:64, 1428:1460], SGAM[:]), "A", "sgam")

            @blk.vector
            def _(v):
                v.wait_ge(SEM["A"], tick["sgam"])
                bump(v.tensor_tensor(VMM[0:64, 1460:1492], VMM[0:64, 1396:1428],
                                     VMM[0:64, 1428:1460], OP.mult), "V", "w0")
                v.wait_ge(SEM["V"], tick["w0"])
                bump(v.tensor_reduce(VMM[0:64, 1492:1493],
                                     VMM[0:64, 1460:1492].unsqueeze(1), AX.X,
                                     OP.add), "V", "bsum")
                v.wait_ge(SEM["V"], tick["bsum"])
                bump(v.tensor_scalar(BI64[:], VMM[0:64, 1492:1493], -1.0, None,
                                     OP.mult), "V", "bi64")
                bump(v.tensor_scalar(W2BF[:], VMM[0:64, 1460:1492], 2.0, None,
                                     OP.mult), "V", "w2bf")
                v.wait_ge(SEM["LD"], tick["ldconst"])
                v.tensor_copy(VMM[0:64, 1024:1056], SGAM[:])
                bump(v.tensor_copy(VMM[0:64, 1056:1088], SBET[:]), "V", "gb0")
                v.wait_ge(SEM["V"], tick["gb0"])
                bump(v.tensor_copy(GBH[:], VMM[0:64, 1024:1088]), "V", "gbh")
                v.wait_ge(SEM["V"], tick["gbh"])
                bump(v.tensor_copy(VMM[0:64, 1088:1152], GBH[:]), "V", "gbb")
                v.wait_ge(SEM["V"], tick["gbb"])
                bump(v.tensor_tensor(VMM[0:64, 1152:1216], VMM[0:64, 1024:1088],
                                     VMM[0:64, 1088:1152], OP.subtract), "V", "gbl0")
                v.wait_ge(SEM["V"], tick["gbl0"])
                bump(v.tensor_copy(GBL[:], VMM[0:64, 1152:1216]), "V", "gbl")
                v.wait_ge(SEM["PD"], tick["thdma"])
                bump(v.tensor_copy(TBH[:], VMM[0:64, 0:1024]), "V", "tbh")
                v.wait_ge(SEM["V"], tick["tbh"])
                bump(v.tensor_copy(VMM[0:64, 1024:2048], TBH[:]), "V", "tbb")
                v.wait_ge(SEM["V"], tick["tbb"])
                bump(v.tensor_tensor(VMM[0:64, 1024:2048], VMM[0:64, 0:1024],
                                     VMM[0:64, 1024:2048], OP.subtract), "V", "tbl0")
                v.wait_ge(SEM["V"], tick["tbl0"])
                bump(v.tensor_copy(TBL[:], VMM[0:64, 1024:2048]), "V", "tbl")

            @blk.gpsimd
            def _(g):
                g.wait_ge(SEM["V"], tick["w2bf"])
                g.wait_ge(SEM["PD"], cnt["PD"])
                dbump(g.dma_start(w2_d.ap(), W2BF[:]), "PD")
                g.wait_ge(SEM["PD"], cnt["PD"])
                for b in range(4):
                    dbump(g.dma_start(
                        bass.AP(WSS, 32 * b * (16 * 64) + b, [[16 * 64, 32],
                                                              [68, 16]]),
                        bass.AP(w2_d, 32 * b, [[1, 32], [128, 16]]),
                        allow_slow_non_contiguous=True), "PD")
                g.wait_ge(SEM["V"], tick["bi64"])
                dbump(g.dma_start(bia_d.ap(), BI64[:]), "PD")
                g.wait_ge(SEM["PD"], cnt["PD"])
                dbump(g.dma_start(BIAR[:], bia_d.ap()), "PD")
                bump(g.wait_ge(SEM["PD"], cnt["PD"]), "G", "winit")



            def p1_load(k):
                @blk.sync
                def _(sy, k=k):
                    if k >= 4:
                        sy.wait_ge(SEM["V"], tick[("sig", k - 4)])
                        sy.wait_ge(SEM[f"SP{k % 4}"], tick[("sp", k - 4)])
                    dbump(sy.dma_start(
                        X0B[k % 4][:],
                        bass.AP(x0d, k * 128 * D, [[D, 128], [1, D]])),
                        f"LX{k % 4}", ("x0", k))

            def p1_pre(k):
                @blk.vector
                def _(v, k=k):
                    v.wait_ge(SEM[f"LX{k % 4}"], tick[("x0", k)])
                    for q in range(4):
                        _bv = v.bn_stats(BNS[:, 6 * q:6 * (q + 1)],
                                         X0B[k % 4][:, 512 * q:512 * (q + 1)])
                    bump(_bv, "V", None)
                    v.wait_ge(SEM["V"], cnt["V"])
                    bump(v.bn_aggr(MU0[:, k:k + 1].tensor[:, 2 * k:2 * k + 2]
                                   if False else
                                   bass.AP(MU0, 0, [[T, 128], [1, 1]])
                                   if False else MUV_[:, 2 * k:2 * k + 2],
                                   BNS[:]), "V", ("bn0", k))
                    v.wait_ge(SEM["V"], tick[("bn0", k)])
                    v.tensor_copy(MU0[:, k:k + 1], MUV_[:, 2 * k:2 * k + 1])
                    bump(v.tensor_copy(V0C[:, k:k + 1],
                                       MUV_[:, 2 * k + 1:2 * k + 2]),
                         "V", ("v0", k))

                @blk.scalar
                def _(a, k=k):
                    a.wait_ge(SEM["V"], tick[("v0", k)])
                    if k == 0:
                        a.wait_ge(SEM["V"], tick["tbl"])
                        a.wait_ge(SEM["PD"], tick["thdma"])
                    bump(a.activation(V0C[:, k:k + 1], V0C[:, k:k + 1], AF.Sqrt,
                                      bias=EPSC[:]), "A", ("ln0", k))

                @blk.vector
                def _(v, k=k):
                    v.wait_ge(SEM["A"], tick[("ln0", k)])
                    bump(v.reciprocal(R0A[:, k:k + 1], V0C[:, k:k + 1]),
                         "V", ("r0", k))
                    v.wait_ge(SEM["V"], tick[("r0", k)])
                    bump(v.scalar_tensor_tensor(B0A[:, k:k + 1],
                                                MU0[:, k:k + 1], -1.0,
                                                R0A[:, k:k + 1], OP.mult,
                                                OP.mult), "V", ("b0", k))

            def p1_xa(k):
                @blk.scalar
                def _(a, k=k):
                    a.wait_ge(SEM["V"], tick[("b0", k)])
                    if k >= 1:
                        a.wait_ge(SEM["P"], tick[("xat", k - 1)])
                    bump(a.activation(XAB[0][:], X0B[k % 4][:], AF.Identity,
                                      bias=B0A[:, k:k + 1],
                                      scale=R0A[:, k:k + 1]), "A", ("xa", k))

            def p1_txa(k):
                c = k % SS
                for grp in range(4):
                    @blk.tensor
                    def _(t, grp=grp, c=c, k=k):
                        if grp == 0:
                            t.wait_ge(SEM["A"], tick[("xa", k)])
                            t.wait_ge(SEM["V"], tick["sid"])
                        e, tk_ = pst_free[0]
                        t.wait_ge(SEM[e], tk_)
                        for j4 in range(4):
                            j = grp * 4 + j4
                            _mi = t.transpose(PST[:, 128 * j4:128 * (j4 + 1)],
                                        XAB[0][:, 128 * j:128 * (j + 1)],
                                        SID[:])
                        bump(_mi, "P", ("trx", k, grp))

                    @blk.scalar
                    def _(a, grp=grp, c=c, k=k):
                        a.wait_ge(SEM["P"], tick[("trx", k, grp)])
                        bump(a.copy(
                            XAT[:, D * c + 512 * grp:D * c + 512 * (grp + 1)],
                            PST[:]), "A", ("evT", k, grp))
                        pst_free[0] = ("A", tick[("evT", k, grp)])
                tick[("xat", k)] = tick[("trx", k, 3)]

            def p1_mm1(s):
                xat3 = XAT[:].rearrange("p (c f) -> p c f", c=SS)
                for i in range(8):
                    @blk.tensor
                    def _(t, i=i, s=s):
                        if i == 0:
                            t.wait_ge(SEM["A"], tick[("evT", s * SS + SS - 1, 3)])
                            if s == 0:
                                t.wait_ge(SEM["PD"], tick["wload"])
                        prev = (s, i - 2) if i >= 2 else \
                            ((s - 1, i + 6) if s > 0 else None)
                        if prev is not None:
                            t.wait_ge(SEM["A"], tick[("tanh",) + prev])
                        for j in range(16):
                            _mi = t.matmul(PS1[i % 2][:],
                                     BT[:, NS * j + 128 * i:
                                        NS * j + 128 * (i + 1)],
                                     xat3[:, :, 128 * j:128 * (j + 1)],
                                     start=(j == 0), stop=(j == 15))
                        bump(_mi, "P", ("mm1", s, i))

                    @blk.scalar
                    def _(a, i=i, s=s):
                        a.wait_ge(SEM["P"], tick[("mm1", s, i)])
                        bump(a.activation(STT[:, 512 * i:512 * (i + 1)],
                                          PS1[i % 2][:], AF.Tanh),
                             "A", ("tanh", s, i))

            def p1_mm2(k):
                s, c = k // SS, k % SS

                @blk.tensor
                def _(t, k=k, s=s, c=c):
                    if k >= 1:
                        t.wait_ge(SEM["V"], tick[("aev", k - 1)])
                    t.wait_ge(SEM["A"], tick[("tanh", s, 7)])
                    if k == 0:
                        t.wait_ge(SEM["V"], tick["ddiag"])
                    for j in range(16):
                        _mi = t.matmul(PS2[:, 128 * j:128 * (j + 1)],
                                 XAT[:, D * c + 128 * j:D * c + 128 * (j + 1)],
                                 DD[:, 128 * j:128 * (j + 1)],
                                 start=(j % 4 == 0), stop=False,
                                 skip_group_check=True)
                    for i in range(8):
                        for h in range(4):
                            _mi = t.matmul(
                                PS2[:, 512 * h:512 * (h + 1)],
                                STT[:, 512 * i + 128 * c:
                                    512 * i + 128 * (c + 1)],
                                CT[:, D * i + 512 * h:D * i + 512 * (h + 1)],
                                start=False, stop=(i == 7),
                                skip_group_check=True)
                    bump(_mi, "P", ("mm2", k))

            def p1_post(k):
                @blk.vector
                def _(v, k=k):
                    v.wait_ge(SEM["P"], tick[("mm2", k)])
                    if k >= 2:
                        v.wait_ge(SEM["V"], tick[("sig", k - 2)])
                    bump(v.tensor_copy(ATTNB[k % 2][:], PS2[:]),
                         "V", ("aev", k))
                    v.wait_ge(SEM["V"], tick[("aev", k)])
                    for q in range(4):
                        _bv = v.bn_stats(BNS[:, 6 * q:6 * (q + 1)],
                                         ATTNB[k % 2][:, 512 * q:512 * (q + 1)])
                    bump(_bv, "V", None)
                    v.wait_ge(SEM["V"], cnt["V"])
                    bump(v.bn_aggr(MUV_[:, 2 * k:2 * k + 2], BNS[:]),
                         "V", ("bna", k))
                    v.wait_ge(SEM["V"], tick[("bna", k)])
                    v.tensor_copy(MUA[:, k:k + 1], MUV_[:, 2 * k:2 * k + 1])
                    bump(v.tensor_copy(VAC[:, k:k + 1],
                                       MUV_[:, 2 * k + 1:2 * k + 2]),
                         "V", ("va", k))

                @blk.scalar
                def _(a, k=k):
                    a.wait_ge(SEM["V"], tick[("va", k)])
                    bump(a.activation(VAC[:, k:k + 1], VAC[:, k:k + 1], AF.Sqrt,
                                      bias=EPSC[:]), "A", ("lna", k))

                @blk.vector
                def _(v, k=k):
                    v.wait_ge(SEM["A"], tick[("lna", k)])
                    bump(v.reciprocal(RAA[:, k:k + 1], VAC[:, k:k + 1]),
                         "V", ("ba", k))

                @blk.vector
                def _(v, k=k):
                    v.wait_ge(SEM["V"], tick[("ba", k)])
                    if k == 0:
                        v.wait_ge(SEM["PD"], tick["thdma"])
                        v.wait_ge(SEM["V"], tick["tbl"])
                    bump(v.tensor_scalar(VMM[:], ATTNB[k % 2][:],
                                         MUA[:, k:k + 1], None, OP.subtract),
                         "V", ("tp", k))
                    v.wait_ge(SEM["V"], tick[("tp", k)])
                    bump(v.scalar_tensor_tensor(X0B[k % 4][:], VMM[:],
                                                RAA[:, k:k + 1],
                                                X0B[k % 4][:], OP.mult, OP.add,
                                                accum_out=SM1[:, k:k + 1]),
                         "V", ("x1", k))

                @blk.sync
                def _(sy, k=k):
                    sy.wait_ge(SEM["V"], tick[("x1", k)])
                    dbump(sy.dma_start(
                        bass.AP(x1sp, k * 128 * D, [[D, 128], [1, D]]),
                        X0B[k % 4][:]), f"SP{k % 4}", ("sp", k))

                @blk.vector
                def _(v, k=k):
                    v.wait_ge(SEM["V"], tick[("x1", k)])
                    for q in range(4):
                        _bv = v.bn_stats(BNS[:, 6 * q:6 * (q + 1)],
                                         X0B[k % 4][:, 512 * q:512 * (q + 1)])
                    bump(_bv, "V", None)
                    v.wait_ge(SEM["V"], cnt["V"])
                    bump(v.bn_aggr(MUV_[:, 2 * k:2 * k + 2], BNS[:]),
                         "V", ("bn1", k))
                    v.wait_ge(SEM["V"], tick[("bn1", k)])
                    v.tensor_copy(MU1[:, k:k + 1], MUV_[:, 2 * k:2 * k + 1])
                    bump(v.tensor_copy(V1A[:, k:k + 1],
                                       MUV_[:, 2 * k + 1:2 * k + 2]),
                         "V", ("v1", k))
                    tick[("sq1", k)] = tick[("v1", k)]

                @blk.scalar
                def _(a, k=k):
                    a.wait_ge(SEM["V"], tick[("v1", k)])
                    bump(a.activation(W32b[:, k:k + 1], V1A[:, k:k + 1],
                                      AF.Sqrt, bias=EPSC[:]), "A", ("ln1", k))

                @blk.vector
                def _(v, k=k):
                    v.wait_ge(SEM["A"], tick[("ln1", k)])
                    bump(v.reciprocal(R1A[:, k:k + 1], W32b[:, k:k + 1]),
                         "V", ("r1", k))
                    bump(v.tensor_scalar(V2C[:, k:k + 1], V1A[:, k:k + 1],
                                         1.0 + EPS, EPS * EPS, OP.mult,
                                         OP.add), "V", ("v2", k))

                @blk.scalar
                def _(a, k=k):
                    a.wait_ge(SEM["V"], tick[("v2", k)])
                    bump(a.activation(S12[:, k:k + 1], V2C[:, k:k + 1],
                                      AF.Sqrt), "A", ("ln2", k))

                @blk.vector
                def _(v, k=k):
                    v.wait_ge(SEM["A"], tick[("ln2", k)])
                    bump(v.reciprocal(R12A[:, k:k + 1], S12[:, k:k + 1]),
                         "V", ("s12", k))
                    v.wait_ge(SEM["V"], tick[("s12", k)])
                    if k == 0:
                        v.wait_ge(SEM["PD"], tick["thbcast"])
                    bump(v.tensor_scalar(ATTNB[k % 2][:], THB[:],
                                         S12[:, k:k + 1], MU1[:, k:k + 1],
                                         OP.mult, OP.add), "V", ("thr", k))
                    v.wait_ge(SEM["V"], tick[("thr", k)])
                    if k >= 1:
                        v.wait_ge(SEM["P"], tick[("sgt", k - 1)])
                    bump(v.tensor_tensor(SIGB[0][:], X0B[k % 4][:],
                                         ATTNB[k % 2][:], OP.is_gt),
                         "V", ("sig", k))

            def p1_tsg(k):
                for grp in range(4):
                    @blk.tensor
                    def _(t, grp=grp, k=k):
                        if grp == 0:
                            t.wait_ge(SEM["V"], tick[("sig", k)])
                        e, tk_ = pst_free[0]
                        t.wait_ge(SEM[e], tk_)
                        for j4 in range(4):
                            j = grp * 4 + j4
                            _mi = t.transpose(PST[:, 128 * j4:128 * (j4 + 1)],
                                        SIGB[0][:, 128 * j:128 * (j + 1)],
                                        SID[:])
                        bump(_mi, "P", ("trs", k, grp))

                    @blk.vector
                    def _(v, grp=grp, k=k):
                        v.wait_ge(SEM["P"], tick[("trs", k, grp)])
                        bump(v.tensor_copy(
                            SGT[0][:, 512 * grp:512 * (grp + 1)], PST[:]),
                            "V", ("evS", k, grp))
                        pst_free[0] = ("V", tick[("evS", k, grp)])
                tick[("sgt", k)] = tick[("trs", k, 3)]

            def p1_scores(k):
                @blk.tensor
                def _(t, k=k):
                    t.wait_ge(SEM["V"], tick[("evS", k, 3)])
                    if k == 0:
                        t.wait_ge(SEM["G"], tick["winit"])
                    else:
                        t.wait_ge(SEM["V"], tick[("gbev", k - 1)])
                    for j in range(16):
                        _mi = t.matmul(PSS[:, 0:64],
                                 SGT[0][:, 128 * j:128 * (j + 1)],
                                 WSS[:, 64 * j:64 * (j + 1)],
                                 start=(j == 0), stop=False,
                                 skip_group_check=True)
                    _mi = t.matmul(PSS[:, 0:64], ONE1[:], BIAR[:], start=False,
                             stop=True, skip_group_check=True)
                    bump(_mi, "P", ("sc", k))

                @blk.vector
                def _(v, k=k):
                    v.wait_ge(SEM["P"], tick[("sc", k)])
                    bump(v.tensor_reduce(
                        CS8[:], PSS[:, 0:64].rearrange("p (a b) -> p a b", a=8),
                        AX.X, OP.add), "V", ("cs8", k))
                    bump(v.tensor_copy(TSC[:], PSS[:, 0:64]), "V", ("tsev", k))
                    v.wait_ge(SEM["V"], tick[("cs8", k)])
                    bump(v.max(MX8[:], CS8[:]), "V", ("cmx", k))
                    v.wait_ge(SEM["V"], tick[("cmx", k)])
                    bump(v.max_index(IX8[:], MX8[:], CS8[:]), "V", ("cix", k))
                    v.wait_ge(SEM["V"], tick[("cix", k)])
                    bump(v.tensor_copy(BCF[:], IX8[:, 0:1]), "V", ("bcf", k))
                    v.wait_ge(SEM["V"], tick[("bcf", k)])
                    bump(v.tensor_scalar(TSM[:], CLB[:], BCF[:], None,
                                         OP.is_equal), "V", ("m64", k))
                    v.wait_ge(SEM["V"], tick[("m64", k)])
                    bump(v.scalar_tensor_tensor(TSM[:], TSC[:], BIG, TSM[:],
                                                OP.add, OP.mult),
                         "V", ("msk", k))
                    v.wait_ge(SEM["V"], tick[("msk", k)])
                    bump(v.max(MX8b[:], TSM[:]), "V", ("tmx", k))
                    v.wait_ge(SEM["V"], tick[("tmx", k)])
                    bump(v.max_index(IX8b[:], MX8b[:], TSM[:]),
                         "V", ("tix", k))
                    v.wait_ge(SEM["V"], tick[("tix", k)])
                    bump(v.tensor_copy(TIDA[:, k:k + 1], IX8b[:, 0:1]),
                         "V", ("tid", k))
                    v.wait_ge(SEM["V"], tick[("tid", k)])
                    bump(v.tensor_scalar(OHB[:, 64 * k:64 * (k + 1)], TIOT[:],
                                         TIDA[:, k:k + 1], None, OP.is_equal),
                         "V", ("oh", k))
                    bump(v.scalar_tensor_tensor(IDXF[:, k:k + 1],
                                                TOKI[:, k:k + 1], 64.0,
                                                TIDA[:, k:k + 1], OP.mult,
                                                OP.add), "V", ("idxf", k))
                    v.wait_ge(SEM["V"], tick[("idxf", k)])
                    bump(v.tensor_copy(IDXU[:, k:k + 1], IDXF[:, k:k + 1]),
                         "V", ("idxu", k))

                @blk.tensor
                def _(t, k=k):
                    t.wait_ge(SEM["V"], tick[("oh", k)])
                    t.wait_ge(SEM["V"], tick[("tsev", k)])
                    _mi = t.transpose(PSSb[0:64, 512:640],
                                OHB[:, 64 * k:64 * (k + 1)], SID[:])
                    bump(_mi, "P", ("oht", k))

                @blk.vector
                def _(v, k=k):
                    v.wait_ge(SEM["P"], tick[("oht", k)])
                    bump(v.tensor_copy(OHT[:, 128 * k:128 * (k + 1)],
                                       PSSb[0:64, 512:640]), "V", ("ohtev", k))

                @blk.tensor
                def _(t, k=k):
                    t.wait_ge(SEM["V"], tick[("ohtev", k)])
                    if k == 0:
                        t.wait_ge(SEM["V"], tick["gbl"])
                    _mi = t.matmul(PSS[:, 64:128], OHT[:, 128 * k:128 * (k + 1)],
                             GBH[:], start=True, stop=False,
                             skip_group_check=True)
                    _mi = t.matmul(PSS[:, 64:128], OHT[:, 128 * k:128 * (k + 1)],
                             GBL[:], start=False, stop=True,
                             skip_group_check=True)
                    bump(_mi, "P", ("gbmm", k))

                @blk.vector
                def _(v, k=k):
                    v.wait_ge(SEM["P"], tick[("gbmm", k)])
                    bump(v.tensor_copy(GBA[:, 64 * k:64 * (k + 1)],
                                       PSS[:, 64:128]), "V", ("gbev", k))


            for s in range(NSUP):
                for c0 in (0, 2):
                    for c in (c0, c0 + 1):
                        k = s * SS + c
                        p1_load(k)
                        p1_pre(k)
                    for c in (c0, c0 + 1):
                        k = s * SS + c
                        p1_xa(k)
                        p1_txa(k)
                p1_mm1(s)
                for c in range(SS):
                    k = s * SS + c
                    p1_mm2(k)
                    p1_post(k)
                    p1_tsg(k)
                    p1_scores(k)

            @blk.gpsimd
            def _(g):
                for q in range(4):
                    g.wait_ge(SEM[f"SP{q}"], tick[("sp", T - 4 + q)])
                g.wait_ge(SEM["V"], tick[("idxu", T - 1)])
                for k in range(T):
                    dbump(g.indirect_dma_start(
                        out=XSL[:, 32 * k:32 * (k + 1)], out_offset=None,
                        in_=x1sp.ap(),
                        in_offset=bass.IndirectOffsetOnAxis(
                            ap=IDXU[:, k:k + 1], axis=0)), "PD", ("xg", k))

            @blk.vector
            def _(v):
                v.wait_ge(SEM["PD"], tick[("xg", T - 1)])
                bump(v.tensor_tensor(XNS[:], XSL[:],
                                MU1[:].to_broadcast([128, T, 32]), OP.subtract), "V", "h1")
                v.wait_ge(SEM["V"], tick["h1"])
                bump(v.tensor_tensor(XNS[:], XNS[:],
                                R12A[:].to_broadcast([128, T, 32]), OP.mult), "V", "h3")
                v.wait_ge(SEM["V"], tick["h3"])
                bump(v.tensor_tensor(XNS[:], XNS[:],
                                bass.AP(GBA, 0, [[T * 64, 128], [64, T],
                                                 [1, 32]]), OP.mult), "V", "h4")
                v.wait_ge(SEM["V"], tick["h4"])
                bump(v.tensor_tensor(XNS[:], XNS[:],
                                bass.AP(GBA, 32, [[T * 64, 128], [64, T],
                                                  [1, 32]]), OP.add), "V", "xns")
                v.wait_ge(SEM["V"], tick["xns"])
                v.tensor_reduce(TMIN[:],
                                XNS[:].rearrange("p (a b) -> p a b", a=T),
                                AX.X, OP.min)
                bump(v.tensor_reduce(TMAX[:],
                                     XNS[:].rearrange("p (a b) -> p a b", a=T),
                                     AX.X, OP.max), "V", "tmm")
                v.wait_ge(SEM["V"], tick["tmm"])
                bump(v.tensor_scalar(
                    bass.AP(VMM, 0, [[2048, 128], [32, 64], [1, 32]]),
                    bass.AP(OHB, 0, [[T * 64, 128], [1, 64], [64, T]]),
                    -1e38, 1e38, OP.mult, OP.add), "V", "ms1")
                v.wait_ge(SEM["V"], tick["ms1"])
                bump(v.tensor_tensor(
                    bass.AP(VMM, 0, [[2048, 128], [32, 64], [1, 32]]),
                    bass.AP(VMM, 0, [[2048, 128], [32, 64], [1, 32]]),
                    bass.AP(TMIN, 0, [[T, 128], [0, 64], [1, T]]),
                    OP.add), "V", "vm1")
                v.wait_ge(SEM["V"], tick["vm1"])
                bump(v.tensor_reduce(
                    RED[:], VMM[:].rearrange("p (a b) -> p a b", a=64),
                    AX.X, OP.min), "V", "redmin")

            @blk.tensor
            def _(t):
                t.wait_ge(SEM["V"], tick["redmin"])
                t.wait_ge(SEM["V"], tick["sidf"])
                _mi = t.transpose(PSS[0:64, 128:256], RED[:], SIDF[:])
                bump(_mi, "P", "trmin")

            @blk.vector
            def _(v):
                v.wait_ge(SEM["P"], tick["trmin"])
                bump(v.tensor_reduce(PMM[:, 0:1],
                                     PSS[0:64, 128:256], AX.X, OP.min),
                     "V", "pmin")
                v.wait_ge(SEM["V"], tick["pmin"])
                bump(v.tensor_scalar(PMM[:, 0:1], PMM[:, 0:1], -1.0, None,
                                     OP.mult), "V", "pminneg")
                bump(v.tensor_scalar(
                    bass.AP(VMM, 0, [[2048, 128], [32, 64], [1, 32]]),
                    bass.AP(OHB, 0, [[T * 64, 128], [1, 64], [64, T]]),
                    1e38, -1e38, OP.mult, OP.add), "V", "ms2")
                v.wait_ge(SEM["V"], tick["ms2"])
                bump(v.tensor_tensor(
                    bass.AP(VMM, 0, [[2048, 128], [32, 64], [1, 32]]),
                    bass.AP(VMM, 0, [[2048, 128], [32, 64], [1, 32]]),
                    bass.AP(TMAX, 0, [[T, 128], [0, 64], [1, T]]),
                    OP.add), "V", "vx1")
                v.wait_ge(SEM["V"], tick["vx1"])
                bump(v.tensor_reduce(
                    RED[:], VMM[:].rearrange("p (a b) -> p a b", a=64),
                    AX.X, OP.max), "V", "redmax")

            @blk.tensor
            def _(t):
                t.wait_ge(SEM["V"], tick["redmax"])
                _mi = t.transpose(PSS[0:64, 256:384], RED[:], SIDF[:])
                bump(_mi, "P", "trmax")

            @blk.vector
            def _(v):
                v.wait_ge(SEM["P"], tick["trmax"])
                bump(v.tensor_reduce(PMM[:, 1:2], PSS[0:64, 256:384], AX.X,
                                     OP.max), "V", "pmax")

        @blk.gpsimd
        def _(g):
            g.wait_ge(SEM["V"], tick["pmax"])
            g.wait_ge(SEM["V"], tick["pminneg"])
            dbump(g.dma_start(bass.AP(cc_in, 0, [[2, 64], [1, 2]]), PMM[:]),
                  "PD")
            g.wait_ge(SEM["PD"], cnt["PD"])
            g.collective_compute(
                "AllReduce", OP.max, replica_groups=[list(range(NCORES))],
                ins=[cc_in.ap().opt()], outs=[cc_out.ap().opt()],
            ).then_inc(SEM["CC"], 1)
            cnt["CC"] += 1
            g.wait_ge(SEM["CC"], cnt["CC"])
            dbump(g.dma_start(GMM[:], bass.AP(cc_out, 0, [[2, 64], [1, 2]])),
                  "PD", "ccb")
            bump(g.wait_ge(SEM["PD"], cnt["PD"]), "G", "ccdone")

        if True:
            PSS2 = PSS
            PSP = [PS2.ap()[:, 0:1024], PS2.ap()[:, 1024:2048]]
            XATF = XAT.ap().bitcast(F32)
            XATB = XAT.ap()
            STTF = STT.ap().bitcast(F32)
            UAL = XATF[:, 0:1024]
            GIF = XATF[:, 1024:2048]
            DPP = XATF[:, 2048:3072]
            GIOT = XATB[:, 6656:7168]
            GMH = XATB[0:64, 7168:7171]
            GML = XATB[0:64, 7200:7203]
            MNX = STTF[:, 0:96]
            OHG = VMM[:, 1024:1536]
            GIU = bass.AP(VMM, 1536, [[T * 64, 128], [1, 512]]).bitcast(I32) \
                if False else None
            GIU = bass.AP(VMM, 1024, [[T * 64, 128], [1, 1024]]).bitcast(I32)
            XLC = UAL
            BSA = GBA

            @blk.gpsimd
            def _(g):
                _gi = g.iota(GIOT, [[0, 32], [1, 16]], channel_multiplier=0,
                       allow_small_or_imprecise_dtypes=True)
                bump(_gi, "G", "giot")

            @blk.vector
            def _(v):
                v.wait_ge(SEM["G"], tick["ccdone"])
                v.tensor_scalar(VMM[0:64, 0:1], GMM[:, 0:1], -1.0, None, OP.mult)
                v.tensor_copy(VMM[0:64, 1:2], GMM[:, 1:2])
                bump(v.tensor_copy(VMM[0:64, 2:3], SOSC[:]), "V", "gm0")
                v.wait_ge(SEM["V"], tick["gm0"])
                bump(v.tensor_copy(GMH, VMM[0:64, 0:3]), "V", "gmh")
                v.wait_ge(SEM["V"], tick["gmh"])
                bump(v.tensor_copy(VMM[0:64, 3:6], GMH), "V", "gmb")
                v.wait_ge(SEM["V"], tick["gmb"])
                bump(v.tensor_tensor(VMM[0:64, 6:9], VMM[0:64, 0:3], VMM[0:64, 3:6],
                                     OP.subtract), "V", "gml0")
                v.wait_ge(SEM["V"], tick["gml0"])
                bump(v.tensor_copy(GML, VMM[0:64, 6:9]), "V", "gml")

            for k in range(T):
                @blk.tensor
                def _(t, k=k):
                    if k == 0:
                        t.wait_ge(SEM["V"], tick["gml"])
                    _mi = t.matmul(PSS2[:, 3 * k:3 * (k + 1)],
                             OHT[:, 128 * k:128 * (k + 1)], GMH,
                             start=(k == 0), stop=False, skip_group_check=True)
                    _mi = t.matmul(PSS2[:, 3 * k:3 * (k + 1)],
                             OHT[:, 128 * k:128 * (k + 1)], GML,
                             start=False, stop=True, skip_group_check=True)
                    if k == T - 1:
                        bump(_mi, "P", "mnmx")

            @blk.vector
            def _(v):
                v.wait_ge(SEM["P"], tick["mnmx"])
                bump(v.tensor_copy(MNX, PSS2[:, 0:96]), "V", "mnxev")
                v.wait_ge(SEM["V"], tick["mnxev"])
                m3 = MNX.rearrange("p (a b) -> p a b", a=T)
                mnv = m3[:, :, 0:1]
                mxv = m3[:, :, 1:2]
                bump(v.tensor_tensor(W32a[:], mxv, mnv, OP.subtract),
                     "V", "den0")
                v.wait_ge(SEM["V"], tick["den0"])
                bump(v.tensor_scalar(W32a[:], W32a[:], 1e-8, None, OP.add),
                     "V", "den1")
                v.wait_ge(SEM["V"], tick["den1"])
                bump(v.reciprocal(INV[:], W32a[:]), "V", "inv")
                v.wait_ge(SEM["V"], tick["inv"])
                bump(v.tensor_tensor(UAL, XNS[:],
                                m3[:, :, 0:1].to_broadcast([128, T, 1, 32]),
                                OP.subtract), "V", "u0")
                v.wait_ge(SEM["V"], tick["u0"])
                bump(v.tensor_tensor(UAL, UAL,
                                INV[:].to_broadcast([128, T, 32]), OP.mult), "V", "u1")
                v.wait_ge(SEM["V"], tick["u1"])
                bump(v.tensor_scalar(UAL, UAL, 0.0, 1.0 - 1e-6, OP.max,
                                OP.min), "V", "u2")
                v.wait_ge(SEM["V"], tick["u2"])
                bump(v.tensor_scalar(GIF, UAL, 16.0, None, OP.mult), "V", "g16")
                v.wait_ge(SEM["V"], tick["g16"])
                bump(v.tensor_copy(GIU, GIF), "V", "giu")
                v.wait_ge(SEM["V"], tick["giu"])
                bump(v.tensor_copy(GIF, GIU), "V", "gif")
                v.wait_ge(SEM["V"], tick["gif"])
                bump(v.scalar_tensor_tensor(XLC[:], UAL, 16.0, GIF, OP.mult,
                                       OP.subtract), "V", "xlc")

            for k in range(T):
                @blk.tensor
                def _(t, k=k):
                    if k == 0:
                        t.wait_ge(SEM["V"], tick["tbl"])
                        t.wait_ge(SEM["V"], tick["xlc"])
                    if k >= 2:
                        t.wait_ge(SEM["V"], tick[("bs", k - 2)])
                    for h in range(2):
                        _mi = t.matmul(PSP[k % 2][:, 512 * h:512 * (h + 1)],
                                 OHT[:, 128 * k:128 * (k + 1)],
                                 TBH[:, 512 * h:512 * (h + 1)],
                                 start=True, stop=False, skip_group_check=True)
                        _mi = t.matmul(PSP[k % 2][:, 512 * h:512 * (h + 1)],
                                 OHT[:, 128 * k:128 * (k + 1)],
                                 TBL[:, 512 * h:512 * (h + 1)],
                                 start=False, stop=True, skip_group_check=True)
                    bump(_mi, "P", ("pg", k))

                @blk.vector
                def _(v, k=k):
                    v.wait_ge(SEM["P"], tick[("pg", k)])
                    if k == 0:
                        v.wait_ge(SEM["G"], tick["giot"])
                    bump(v.tensor_tensor(
                        OHG, GIF[:, 32 * k:32 * (k + 1)]
                        .to_broadcast([128, 32, 16]), GIOT, OP.is_equal), "V", ("ohg", k))
                    v.wait_ge(SEM["V"], tick[("ohg", k)])
                    bump(v.tensor_tensor(VMM[:, 0:1024], bass.AP(
                        VMM, 1024, [[T * 64, 128], [0, 2], [1, 512]]),
                        PSP[k % 2], OP.mult), "V", ("prod", k))
                    v.wait_ge(SEM["V"], tick[("prod", k)])
                    bump(v.tensor_reduce(
                        BSA[:, 64 * k:64 * (k + 1)],
                        VMM[:, 0:1024].rearrange("p (a b) -> p a b", a=64),
                        AX.X, OP.add), "V", ("bs", k))

            @blk.vector
            def _(v):
                v.wait_ge(SEM["V"], tick[("bs", T - 1)])
                bview = bass.AP(BSA, 0, [[T * 64, 128], [64, T], [1, 32]])
                sview = bass.AP(BSA, 32, [[T * 64, 128], [64, T], [1, 32]])
                oview = MNX.rearrange("p (a b) -> p a b", a=T)[:, :, 2:3] \
                    .to_broadcast([128, T, 1, 32])
                bump(v.tensor_tensor(DPP, sview, XLC[:], OP.mult), "V", "d1")
                v.wait_ge(SEM["V"], tick["d1"])
                bump(v.tensor_tensor(DPP, DPP, bview, OP.add), "V", "d2")
                v.wait_ge(SEM["V"], tick["d2"])
                bump(v.tensor_tensor(DPP, DPP, oview, OP.mult), "V", "delta")
                v.wait_ge(SEM["V"], tick["delta"])
                bump(v.tensor_reduce(W32a[:],
                                DPP.rearrange("p (a b) -> p a b", a=T),
                                AX.X, OP.add), "V", "sd")
                v.wait_ge(SEM["V"], tick["sd"])
                bump(v.tensor_scalar(MUG[:], W32a[:], 1.0 / D, None, OP.mult), "V", "mug")
                bump(v.tensor_tensor(VMM[:, 0:T * 32], XSL[:],
                                     MU1[:].to_broadcast([128, T, 32]),
                                     OP.subtract), "V", "hn0")
                v.wait_ge(SEM["V"], tick["hn0"])
                bump(v.tensor_tensor(VMM[:, 0:T * 32], VMM[:, 0:T * 32],
                                     R1A[:].to_broadcast([128, T, 32]),
                                     OP.mult), "V", "hn1")
                v.wait_ge(SEM["V"], tick["hn1"])
                bump(v.tensor_tensor(VMM[:, T * 32:T * 64], VMM[:, 0:T * 32],
                                     DPP, OP.mult), "V", "cr0")
                v.wait_ge(SEM["V"], tick["cr0"])
                bump(v.tensor_reduce(W32b[:],
                                VMM[:, T * 32:T * 64].rearrange(
                                    "p (a b) -> p a b", a=T), AX.X, OP.add), "V", "cr")
                v.wait_ge(SEM["V"], tick["cr"])
                bump(v.tensor_tensor(VMM[:, T * 32:T * 64], DPP, DPP, OP.mult), "V", "dd0")
                v.wait_ge(SEM["V"], tick["dd0"])
                bump(v.tensor_reduce(W32c[:],
                                VMM[:, T * 32:T * 64].rearrange(
                                    "p (a b) -> p a b", a=T), AX.X, OP.add), "V", "ddq")
                v.wait_ge(SEM["V"], tick["ddq"])
                bump(v.tensor_tensor(W32a[:], R1A[:], R1A[:], OP.mult), "V", "q1")
                v.wait_ge(SEM["V"], tick["q1"])
                bump(v.tensor_tensor(W32a[:], W32a[:], V1A[:], OP.mult), "V", "q2")
                v.wait_ge(SEM["V"], tick["q2"])
                bump(v.scalar_tensor_tensor(W32a[:], W32a[:], float(D), W32b[:],
                                       OP.mult, OP.add), "V", "q3")
                v.wait_ge(SEM["V"], tick["q3"])
                bump(v.tensor_tensor(W32a[:], W32a[:], W32b[:], OP.add), "V", "q4")
                v.wait_ge(SEM["V"], tick["q4"])
                bump(v.tensor_tensor(W32a[:], W32a[:], W32c[:], OP.add), "V", "q5")
                v.wait_ge(SEM["V"], tick["q5"])
                bump(v.tensor_tensor(W32b[:], MUG[:], MUG[:], OP.mult), "V", "q6")
                v.wait_ge(SEM["V"], tick["q6"])
                bump(v.scalar_tensor_tensor(W32a[:], W32a[:], 1.0 / D, W32b[:],
                                       OP.mult, OP.subtract), "V", "vg")

            @blk.scalar
            def _(a):
                a.wait_ge(SEM["V"], tick["vg"])
                bump(a.activation(W32a[:], W32a[:], AF.Sqrt, bias=EPSC[:]),
                     "A", "lng")

            @blk.vector
            def _(v):
                v.wait_ge(SEM["A"], tick["lng"])
                bump(v.reciprocal(RGA[:], W32a[:]), "V", "rg")
                v.wait_ge(SEM["V"], tick["rg"])
                bump(v.tensor_tensor(W32a[:], R1A[:], RGA[:], OP.mult), "V", "s20")
                v.wait_ge(SEM["V"], tick["s20"])
                bump(v.tensor_scalar(S2A[:], W32a[:], 1.0, None, OP.add),
                     "V", "s2")
                bump(v.tensor_tensor(W32b[:], MU1[:], R1A[:], OP.mult), "V", "b20")
                v.wait_ge(SEM["V"], tick["b20"])
                bump(v.tensor_tensor(W32b[:], W32b[:], MUG[:], OP.add), "V", "b21")
                v.wait_ge(SEM["V"], tick["b21"])
                bump(v.tensor_tensor(W32b[:], W32b[:], RGA[:], OP.mult), "V", "b22")
                v.wait_ge(SEM["V"], tick["b22"])
                bump(v.tensor_scalar(B2A[:], W32b[:], -1.0, None, OP.mult),
                     "V", "b2")
                bump(v.tensor_tensor(DPP, DPP, VMM[:, 0:T * 32], OP.add), "V", "e1")
                v.wait_ge(SEM["V"], tick["e1"])
                bump(v.tensor_tensor(DPP, DPP,
                                MUG[:].to_broadcast([128, T, 32]), OP.subtract), "V", "e2")
                v.wait_ge(SEM["V"], tick["e2"])
                bump(v.tensor_tensor(DPP, DPP,
                                RGA[:].to_broadcast([128, T, 32]), OP.mult), "V", "e3")
                v.wait_ge(SEM["V"], tick["e3"])
                bump(v.tensor_tensor(DPP, DPP, XSL[:], OP.add), "V", "dpp")

            for k in range(T):
                @blk.sync
                def _(sy, k=k):
                    if k == 0:
                        for q in range(4):
                            sy.wait_ge(SEM[f"SP{q}"], tick[("sp", T - 4 + q)])
                        sy.wait_ge(SEM["V"], tick[("sig", T - 1)])
                    if k >= 2:
                        sy.wait_ge(SEM[f"OS{k % 2}"], tick[("st", k - 2)])
                    dbump(sy.dma_start(
                        X0B[k % 2][:],
                        bass.AP(x1sp, k * 128 * D, [[D, 128], [1, D]])),
                        f"RL{k % 2}", ("rl", k))

                @blk.scalar
                def _(a, k=k):
                    a.wait_ge(SEM[f"RL{k % 2}"], tick[("rl", k)])
                    if k == 0:
                        a.wait_ge(SEM["V"], tick["b2"])
                    bump(a.activation(X0B[k % 2][:], X0B[k % 2][:],
                                      AF.Identity, bias=B2A[:, k:k + 1],
                                      scale=S2A[:, k:k + 1]), "A", ("t2", k))

                @blk.sync
                def _(sy, k=k):
                    sy.wait_ge(SEM["A"], tick[("t2", k)])
                    dbump(sy.dma_start(
                        bass.AP(outd, k * 128 * D, [[D, 128], [1, D]]),
                        X0B[k % 2][:]), f"OS{k % 2}", ("st", k))

                @blk.gpsimd
                def _(g, k=k):
                    g.wait_ge(SEM[f"OS{k % 2}"], tick[("st", k)])
                    if k == 0:
                        g.wait_ge(SEM["V"], tick["dpp"])
                    dbump(g.indirect_dma_start(
                        out=bass.AP(outd, 0, [[32, TOK * 64], [1, 32]]),
                        out_offset=bass.IndirectOffsetOnAxis(
                            ap=IDXU[:, k:k + 1], axis=0),
                        in_=DPP[:, 32 * k:32 * (k + 1)], in_offset=None),
                        "PD", ("scat", k))

            if debug:
                @blk.sync
                def _(sy):
                    sy.wait_ge(SEM["V"], tick["dpp"])
                    dbump(sy.dma_start(dbg1d.ap(), XNS[:, 0:D]), "OS0")
                    dbump(sy.dma_start(dbg2d.ap(), BSA[:, 0:2048]), "OS0")

            @blk.sync
            def _(sy):
                sy.wait_ge(SEM["OS0"], cnt["OS0"])
                sy.wait_ge(SEM["OS1"], cnt["OS1"])
                sy.wait_ge(SEM["PD"], cnt["PD"])

    return nc


_PROG_CACHE = {}


def _get_prog(debug=False):
    key = bool(debug)
    if key not in _PROG_CACHE:
        _PROG_CACHE[key] = build_program(debug=key)
    return _PROG_CACHE[key]


def make_in_maps(L0_attn_x, L0_attn_B, L0_attn_C, L0_attn_D, L0_ffn_ln_gamma,
                 L0_ffn_ln_beta, L0_ffn_spline_bases, L0_ffn_spline_slopes,
                 L0_ffn_output_scale):
    x = np.ascontiguousarray(np.asarray(L0_attn_x, dtype=np.float32)
                             .reshape(NCORES * TOK, D))
    bt = np.ascontiguousarray(np.asarray(L0_attn_B, dtype=np.float32).T)
    ct = np.ascontiguousarray(np.asarray(L0_attn_C, dtype=np.float32).T)
    dcol = np.ascontiguousarray(
        np.asarray(L0_attn_D, dtype=np.float32).reshape(16, 128).T)
    gam = np.ascontiguousarray(
        np.asarray(L0_ffn_ln_gamma, dtype=np.float32).reshape(64, 32))
    bet = np.ascontiguousarray(
        np.asarray(L0_ffn_ln_beta, dtype=np.float32).reshape(64, 32))
    bas = np.ascontiguousarray(
        np.asarray(L0_ffn_spline_bases, dtype=np.float32).reshape(64, 512))
    slo = np.ascontiguousarray(
        np.asarray(L0_ffn_spline_slopes, dtype=np.float32).reshape(64, 512))
    osc = np.ascontiguousarray(
        np.asarray(L0_ffn_output_scale, dtype=np.float32).reshape(64, 1))
    return [{"x0": x[core * TOK:(core + 1) * TOK],
             "bt": bt, "ct": ct, "dcol": dcol, "gam": gam, "bet": bet,
             "bas": bas, "slo": slo, "osc": osc} for core in range(NCORES)]


def kernel(**inputs):
    in_maps = make_in_maps(**inputs)
    nc = _get_prog(debug=False)
    res = run_bass_kernel_spmd(nc, in_maps, core_ids=list(range(NCORES)))
    out = np.concatenate([r["out"] for r in res.results], axis=0)
    return out.reshape(2, 16384, D).astype(np.float32)


# version 05
# speedup vs baseline: 1.1196x; 1.0200x over previous
import numpy as np
from contextlib import ExitStack

import concourse.bass as bass
import concourse.mybir as mybir
from concourse.alu_op_type import AluOpType as OP
from concourse.bass_utils import run_bass_kernel_spmd

F32 = mybir.dt.float32
BF16 = mybir.dt.bfloat16
U32 = mybir.dt.uint32
I32 = mybir.dt.int32
AF = mybir.ActivationFunctionType
AX = mybir.AxisListType

NCORES = 8
TOK = 4096
T = 32
SS = 4
NSUP = T // SS
D = 2048
NS = 1024
EPS = 1e-5
BIG = 1000.0


def build_program(debug=False):
    nc = bass.Bass("TRN2", target_bir_lowering=False)

    x0d = nc.dram_tensor("x0", [TOK, D], F32, kind="ExternalInput")
    btd = nc.dram_tensor("bt", [D, NS], F32, kind="ExternalInput")
    ctd = nc.dram_tensor("ct", [NS, D], F32, kind="ExternalInput")
    dcold = nc.dram_tensor("dcol", [128, 16], F32, kind="ExternalInput")
    gamd = nc.dram_tensor("gam", [64, 32], F32, kind="ExternalInput")
    betd = nc.dram_tensor("bet", [64, 32], F32, kind="ExternalInput")
    basd = nc.dram_tensor("bas", [64, 512], F32, kind="ExternalInput")
    slod = nc.dram_tensor("slo", [64, 512], F32, kind="ExternalInput")
    oscd = nc.dram_tensor("osc", [64, 1], F32, kind="ExternalInput")
    outd = nc.dram_tensor("out", [TOK, D], F32, kind="ExternalOutput")
    if debug:
        dbg1d = nc.dram_tensor("dbg1", [128, D], F32, kind="ExternalOutput")
        dbg2d = nc.dram_tensor("dbg2", [128, 2048], F32, kind="ExternalOutput")

    x1sp = nc.dram_tensor("x1sp", [TOK * 64, 32], F32)
    th_d = nc.dram_tensor("th_d", [1, D], F32)
    w2_d = nc.dram_tensor("w2_d", [64, 32], BF16)
    bia_d = nc.dram_tensor("bia_d", [1, 64], BF16)
    cc_in = nc.dram_tensor("cc_in", [1, 128], F32)
    cc_out = nc.dram_tensor("cc_out", [1, 128], F32)

    with ExitStack() as st:
        def sb(name, shape, dt):
            return st.enter_context(nc.sbuf_tensor(name, shape, dt))

        BT = sb("BT", [128, 16 * NS], BF16)
        CT = sb("CT", [128, 8 * D], BF16)
        TOKI = sb("TOKI", [128, T], F32)
        EPSC = sb("EPSC", [128, 1], F32)
        VMM = sb("VMM", [128, T * 64], F32)
        SOSC = sb("SOSC", [64, 1], F32)
        TBH = sb("TBH", [64, 1024], BF16)
        TBL = sb("TBL", [64, 1024], BF16)
        GMM = sb("GMM", [64, 2], F32)
        PMM = sb("PMM", [64, 2], F32)

        X0B = [sb(f"X0B{i}", [128, D], F32) for i in range(4)]
        ATTNB = [sb(f"ATTNB{i}", [128, D], F32) for i in range(2)]

        MU0 = sb("MU0", [128, T], F32)
        SM0 = sb("SM0", [128, T], F32)
        SQ0 = sb("SQ0", [128, T], F32)
        V0C = sb("V0C", [128, T], F32)
        R0A = sb("R0A", [128, T], F32)
        B0A = sb("B0A", [128, T], F32)
        SMA = sb("SMA", [128, T], F32)
        SQA = sb("SQA", [128, T], F32)
        MUA = sb("MUA", [128, T], F32)
        VAC = sb("VAC", [128, T], F32)
        RAA = sb("RAA", [128, T], F32)
        BAA = sb("BAA", [128, T], F32)
        SM1 = sb("SM1", [128, T], F32)
        SQ1 = sb("SQ1", [128, T], F32)
        MU1 = sb("MU1", [128, T], F32)
        V1A = sb("V1A", [128, T], F32)
        R1A = sb("R1A", [128, T], F32)
        V2C = sb("V2C", [128, T], F32)
        S12 = sb("S12", [128, T], F32)
        TIDA = sb("TIDA", [128, T], F32)
        IDXF = sb("IDXF", [128, T], F32)
        IDXU = sb("IDXU", [128, T], U32)
        R12A = sb("R12A", [128, T], F32)
        TMIN = sb("TMIN", [128, T], F32)
        TMAX = sb("TMAX", [128, T], F32)
        MUG = sb("MUG", [128, T], F32)
        RGA = sb("RGA", [128, T], F32)
        S2A = sb("S2A", [128, T], F32)
        B2A = sb("B2A", [128, T], F32)
        INV = sb("INV", [128, T], F32)
        W32a = sb("W32a", [128, T], F32)
        W32b = sb("W32b", [128, T], F32)
        W32c = sb("W32c", [128, T], F32)


        OHT = sb("OHT", [64, T * 128], BF16)
        GBA = sb("GBA", [128, T * 64], F32)
        XSL = sb("XSL", [128, T * 32], F32)
        XNS = sb("XNS", [128, T * 32], F32)
        RED = sb("RED", [128, 64], F32)

        SEM = {n: st.enter_context(nc.semaphore(n)) for n in
               ["LD", "PD", "CC", "V", "A", "P", "G",
                "SP0", "SP1", "SP2", "SP3",
                "LX0", "LX1", "LX2", "LX3", "RL0", "RL1", "OS0", "OS1"]}
        cnt = {n: 0 for n in SEM}
        tick = {}

        def bump(inst, e, key=None):
            cnt[e] += 1
            inst.then_inc(SEM[e], 1)
            if key is not None:
                tick[key] = cnt[e]
            return cnt[e]

        def dbump(inst, e, key=None):
            cnt[e] += 16
            inst.then_inc(SEM[e], 16)
            if key is not None:
                tick[key] = cnt[e]
            return cnt[e]

        blk = st.enter_context(nc.Block())

        pst_free = [("V", 0)]

        if True:
            PS1 = [st.enter_context(nc.psum_tensor("PS1a", [128, 512], F32)),
                   st.enter_context(nc.psum_tensor("PS1b", [128, 512], F32))]
            PS2 = st.enter_context(nc.psum_tensor("PS2", [128, 2048], F32))
            PST = st.enter_context(nc.psum_tensor("PST", [128, 512], BF16))
            PSS = st.enter_context(nc.psum_tensor("PSS", [128, 512], F32))
            PSSb = PSS.ap().bitcast(BF16)
            sb1 = sb
            DD = sb1("DD", [128, 16 * 128], BF16)
            DCOL = sb1("DCOL", [128, 16], F32)
            WSS = sb1("WSS", [128, 16 * 64], BF16)
            THB = sb1("THB", [128, D], BF16)
            SID = sb1("SID", [128, 128], BF16)
            SIDF = sb1("SIDF", [128, 128], F32)
            ONE1 = sb1("ONE1", [1, 128], BF16)
            BIAR = sb1("BIAR", [1, 64], BF16)
            TIOT = sb1("TIOT", [128, 64], F32)
            CLB = sb1("CLB", [128, 64], F32)
            IOT128 = sb1("IOT128", [128, 128], F32)
            PIDX = sb1("PIDX", [128, 1], F32)
            SGAM = sb1("SGAM", [64, 32], F32)
            SBET = sb1("SBET", [64, 32], F32)
            GBH = sb1("GBH", [64, 64], BF16)
            GBL = sb1("GBL", [64, 64], BF16)
            W2BF = sb1("W2BF", [64, 32], BF16)
            BI64 = sb1("BI64", [64, 1], BF16)
            XAB = [sb("XAB0", [128, D], BF16)]
            SIGB = [sb("SIGB0", [128, D], BF16)]
            XAT = sb1("XAT", [128, SS * D], BF16)
            STT = sb1("STT", [128, 8 * 512], BF16)
            SGT = [sb("SGT0", [128, D], BF16)]
            CS8 = sb1("CS8", [128, 8], F32)
            MX8 = sb1("MX8", [128, 8], F32)
            IX8 = sb1("IX8", [128, 8], U32)
            TSC = sb1("TSC", [128, 64], F32)
            TSM = sb1("TSM", [128, 64], F32)
            MX8b = sb1("MX8b", [128, 8], F32)
            IX8b = sb1("IX8b", [128, 8], U32)
            BCF = sb1("BCF", [128, 1], F32)
            OHB = sb1("OHB", [128, T * 64], BF16)

            @blk.sync
            def _(sy):
                dbump(sy.dma_start(SGAM[:], gamd.ap()), "LD")
                dbump(sy.dma_start(SBET[:], betd.ap()), "LD")
                dbump(sy.dma_start(VMM[0:64, 0:512], basd.ap()), "LD")
                dbump(sy.dma_start(VMM[0:64, 512:1024], slod.ap()), "LD")
                dbump(sy.dma_start(SOSC[:], oscd.ap()), "LD")
                dbump(sy.dma_start(DCOL[:], dcold.ap()), "LD", "ldconst")

            @blk.gpsimd
            def _(g):
                g.memset(EPSC[:], EPS)
                g.memset(ONE1[:], 1.0)
                g.memset(WSS[:], 0)
                _gi = g.iota(TIOT[:], [[1, 64]], channel_multiplier=0,
                       allow_small_or_imprecise_dtypes=True)
                _gi = g.iota(CLB[:], [[1, 8], [0, 8]], channel_multiplier=0,
                       allow_small_or_imprecise_dtypes=True)
                _gi = g.iota(TOKI[:], [[128, T]], channel_multiplier=1,
                       allow_small_or_imprecise_dtypes=True)
                _gi = g.iota(PIDX[:], [[0, 1]], channel_multiplier=1,
                       allow_small_or_imprecise_dtypes=True)
                _gi = g.iota(IOT128[:], [[1, 128]], channel_multiplier=0,
                       allow_small_or_imprecise_dtypes=True)
                bump(_gi, "G", "iotas")
                dbump(g.dma_start(
                    BT[:], bass.AP(btd, 0, [[NS, 128], [128 * NS, 16], [1, NS]])),
                    "PD")
                dbump(g.dma_start(
                    CT[:], bass.AP(ctd, 0, [[D, 128], [128 * D, 8], [1, D]])),
                    "PD", "wload")
                g.wait_ge(SEM["PD"], cnt["PD"])

            @blk.vector
            def _(v):
                v.wait_ge(SEM["G"], tick["iotas"])
                bump(v.tensor_scalar(SID[:], IOT128[:], PIDX[:], None,
                                     OP.is_equal), "V", "sid")
                v.wait_ge(SEM["V"], tick["sid"])
                bump(v.tensor_copy(SIDF[:], SID[:]), "V", "sidf")
                v.wait_ge(SEM["LD"], tick["ldconst"])
                for j in range(16):
                    _vi = v.tensor_scalar(DD[:, 128 * j:128 * (j + 1)], SID[:],
                                          DCOL[:, j:j + 1], None, OP.mult)
                bump(_vi, "V", "ddiag")
                bump(v.reciprocal(VMM[0:64, 1300:1332], SGAM[:]), "V", "recg")
                v.wait_ge(SEM["V"], tick["recg"])
                bump(v.tensor_tensor(VMM[0:64, 1332:1364], VMM[0:64, 1300:1332],
                                     SBET[:], OP.mult), "V", "th0")
                v.wait_ge(SEM["V"], tick["th0"])
                bump(v.tensor_scalar(VMM[0:64, 1332:1364], VMM[0:64, 1332:1364],
                                     -1.0, None, OP.mult), "V", "theta")
                v.wait_ge(SEM["LD"], tick["ldconst"])
                bump(v.tensor_reduce(VMM[0:64, 1364:1396],
                                     VMM[0:64, 512:1024].rearrange(
                                         "p (a b) -> p a b", a=32),
                                     AX.X, OP.add), "V", "ssum")

            @blk.gpsimd
            def _(g):
                g.wait_ge(SEM["V"], tick["theta"])
                dbump(g.dma_start(th_d.ap(), VMM[0:64, 1332:1364]), "PD", "thdma")
                g.wait_ge(SEM["PD"], cnt["PD"])
                dbump(g.dma_start(THB[:], bass.AP(th_d, 0, [[0, 128], [1, D]])),
                      "PD", "thbcast")

            @blk.scalar
            def _(a):
                a.wait_ge(SEM["V"], tick["ssum"])
                bump(a.sign(VMM[0:64, 1396:1428], VMM[0:64, 1364:1396]),
                     "A", "ssgn")
                bump(a.sign(VMM[0:64, 1428:1460], SGAM[:]), "A", "sgam")

            @blk.vector
            def _(v):
                v.wait_ge(SEM["A"], tick["sgam"])
                bump(v.tensor_tensor(VMM[0:64, 1460:1492], VMM[0:64, 1396:1428],
                                     VMM[0:64, 1428:1460], OP.mult), "V", "w0")
                v.wait_ge(SEM["V"], tick["w0"])
                bump(v.tensor_reduce(VMM[0:64, 1492:1493],
                                     VMM[0:64, 1460:1492].unsqueeze(1), AX.X,
                                     OP.add), "V", "bsum")
                v.wait_ge(SEM["V"], tick["bsum"])
                bump(v.tensor_scalar(BI64[:], VMM[0:64, 1492:1493], -1.0, None,
                                     OP.mult), "V", "bi64")
                bump(v.tensor_scalar(W2BF[:], VMM[0:64, 1460:1492], 2.0, None,
                                     OP.mult), "V", "w2bf")
                v.wait_ge(SEM["LD"], tick["ldconst"])
                v.tensor_copy(VMM[0:64, 1024:1056], SGAM[:])
                bump(v.tensor_copy(VMM[0:64, 1056:1088], SBET[:]), "V", "gb0")
                v.wait_ge(SEM["V"], tick["gb0"])
                bump(v.tensor_copy(GBH[:], VMM[0:64, 1024:1088]), "V", "gbh")
                v.wait_ge(SEM["V"], tick["gbh"])
                bump(v.tensor_copy(VMM[0:64, 1088:1152], GBH[:]), "V", "gbb")
                v.wait_ge(SEM["V"], tick["gbb"])
                bump(v.tensor_tensor(VMM[0:64, 1152:1216], VMM[0:64, 1024:1088],
                                     VMM[0:64, 1088:1152], OP.subtract), "V", "gbl0")
                v.wait_ge(SEM["V"], tick["gbl0"])
                bump(v.tensor_copy(GBL[:], VMM[0:64, 1152:1216]), "V", "gbl")
                v.wait_ge(SEM["PD"], tick["thdma"])
                bump(v.tensor_copy(TBH[:], VMM[0:64, 0:1024]), "V", "tbh")
                v.wait_ge(SEM["V"], tick["tbh"])
                bump(v.tensor_copy(VMM[0:64, 1024:2048], TBH[:]), "V", "tbb")
                v.wait_ge(SEM["V"], tick["tbb"])
                bump(v.tensor_tensor(VMM[0:64, 1024:2048], VMM[0:64, 0:1024],
                                     VMM[0:64, 1024:2048], OP.subtract), "V", "tbl0")
                v.wait_ge(SEM["V"], tick["tbl0"])
                bump(v.tensor_copy(TBL[:], VMM[0:64, 1024:2048]), "V", "tbl")

            @blk.gpsimd
            def _(g):
                g.wait_ge(SEM["V"], tick["w2bf"])
                g.wait_ge(SEM["PD"], cnt["PD"])
                dbump(g.dma_start(w2_d.ap(), W2BF[:]), "PD")
                g.wait_ge(SEM["PD"], cnt["PD"])
                for b in range(4):
                    dbump(g.dma_start(
                        bass.AP(WSS, 32 * b * (16 * 64) + b, [[16 * 64, 32],
                                                              [68, 16]]),
                        bass.AP(w2_d, 32 * b, [[1, 32], [128, 16]]),
                        allow_slow_non_contiguous=True), "PD")
                g.wait_ge(SEM["V"], tick["bi64"])
                dbump(g.dma_start(bia_d.ap(), BI64[:]), "PD")
                g.wait_ge(SEM["PD"], cnt["PD"])
                dbump(g.dma_start(BIAR[:], bia_d.ap()), "PD")
                bump(g.wait_ge(SEM["PD"], cnt["PD"]), "G", "winit")



            def p1_load(k):
                @blk.sync
                def _(sy, k=k):
                    if k >= 4:
                        sy.wait_ge(SEM["V"], tick[("sig", k - 4)])
                        sy.wait_ge(SEM[f"SP{k % 4}"], tick[("sp", k - 4)])
                    dbump(sy.dma_start(
                        X0B[k % 4][:],
                        bass.AP(x0d, k * 128 * D, [[D, 128], [1, D]])),
                        f"LX{k % 4}", ("x0", k))

            def p1_pre(k):
                @blk.vector
                def _(v, k=k):
                    v.wait_ge(SEM[f"LX{k % 4}"], tick[("x0", k)])
                    bump(v.tensor_reduce(SM0[:, k:k + 1], X0B[k % 4][:],
                                         AX.X, OP.add), "V", ("sum0", k))
                    v.wait_ge(SEM["V"], tick[("sum0", k)])
                    bump(v.tensor_scalar(MU0[:, k:k + 1], SM0[:, k:k + 1],
                                         1.0 / D, None, OP.mult),
                         "V", ("mu0", k))

                @blk.scalar
                def _(a, k=k):
                    a.wait_ge(SEM[f"LX{k % 4}"], tick[("x0", k)])
                    if k == 0:
                        a.wait_ge(SEM["V"], tick["tbl"])
                        a.wait_ge(SEM["PD"], tick["thdma"])
                    if k >= 4:
                        a.wait_ge(SEM["V"], tick[("x1", 4 * (k // 4) - 1)])
                    bump(a.activation(VMM[:], X0B[k % 4][:], AF.Square,
                                      accum_out=SQ0[:, k:k + 1]),
                         "A", ("sq0", k))

                @blk.vector
                def _(v, k=k):
                    v.wait_ge(SEM["V"], tick[("mu0", k)])
                    v.wait_ge(SEM["A"], tick[("sq0", k)])
                    bump(v.tensor_tensor(W32a[:, k:k + 1], MU0[:, k:k + 1],
                                         MU0[:, k:k + 1], OP.mult),
                         "V", ("m20", k))
                    v.wait_ge(SEM["V"], tick[("m20", k)])
                    bump(v.scalar_tensor_tensor(V0C[:, k:k + 1],
                                                SQ0[:, k:k + 1], 1.0 / D,
                                                W32a[:, k:k + 1], OP.mult,
                                                OP.subtract), "V", ("v0", k))

                @blk.scalar
                def _(a, k=k):
                    a.wait_ge(SEM["V"], tick[("v0", k)])
                    bump(a.activation(V0C[:, k:k + 1], V0C[:, k:k + 1], AF.Sqrt,
                                      bias=EPSC[:]), "A", ("ln0", k))

                @blk.vector
                def _(v, k=k):
                    v.wait_ge(SEM["A"], tick[("ln0", k)])
                    bump(v.reciprocal(R0A[:, k:k + 1], V0C[:, k:k + 1]),
                         "V", ("r0", k))
                    v.wait_ge(SEM["V"], tick[("r0", k)])
                    bump(v.scalar_tensor_tensor(B0A[:, k:k + 1],
                                                MU0[:, k:k + 1], -1.0,
                                                R0A[:, k:k + 1], OP.mult,
                                                OP.mult), "V", ("b0", k))

            def p1_xa(k):
                @blk.vector
                def _(v, k=k):
                    v.wait_ge(SEM["V"], tick[("b0", k)])
                    if k >= 1:
                        v.wait_ge(SEM["P"], tick[("xat", k - 1)])
                    bump(v.tensor_scalar(XAB[0][:], X0B[k % 4][:],
                                         R0A[:, k:k + 1], B0A[:, k:k + 1],
                                         OP.mult, OP.add), "V", ("xa", k))

            def p1_txa(k):
                c = k % SS
                for grp in range(4):
                    @blk.tensor
                    def _(t, grp=grp, c=c, k=k):
                        if grp == 0:
                            t.wait_ge(SEM["V"], tick[("xa", k)])
                            t.wait_ge(SEM["V"], tick["sid"])
                        e, tk_ = pst_free[0]
                        t.wait_ge(SEM[e], tk_)
                        for j4 in range(4):
                            j = grp * 4 + j4
                            _mi = t.transpose(PST[:, 128 * j4:128 * (j4 + 1)],
                                        XAB[0][:, 128 * j:128 * (j + 1)],
                                        SID[:])
                        bump(_mi, "P", ("trx", k, grp))

                    @blk.scalar
                    def _(a, grp=grp, c=c, k=k):
                        a.wait_ge(SEM["P"], tick[("trx", k, grp)])
                        bump(a.copy(
                            XAT[:, D * c + 512 * grp:D * c + 512 * (grp + 1)],
                            PST[:]), "A", ("evT", k, grp))
                        pst_free[0] = ("A", tick[("evT", k, grp)])
                tick[("xat", k)] = tick[("trx", k, 3)]

            def p1_mm1(s):
                xat3 = XAT[:].rearrange("p (c f) -> p c f", c=SS)
                for i in range(8):
                    @blk.tensor
                    def _(t, i=i, s=s):
                        if i == 0:
                            t.wait_ge(SEM["A"], tick[("evT", s * SS + SS - 1, 3)])
                            if s == 0:
                                t.wait_ge(SEM["PD"], tick["wload"])
                        prev = (s, i - 2) if i >= 2 else \
                            ((s - 1, i + 6) if s > 0 else None)
                        if prev is not None:
                            t.wait_ge(SEM["A"], tick[("tanh",) + prev])
                        for j in range(16):
                            _mi = t.matmul(PS1[i % 2][:],
                                     BT[:, NS * j + 128 * i:
                                        NS * j + 128 * (i + 1)],
                                     xat3[:, :, 128 * j:128 * (j + 1)],
                                     start=(j == 0), stop=(j == 15))
                        bump(_mi, "P", ("mm1", s, i))

                    @blk.scalar
                    def _(a, i=i, s=s):
                        a.wait_ge(SEM["P"], tick[("mm1", s, i)])
                        bump(a.activation(STT[:, 512 * i:512 * (i + 1)],
                                          PS1[i % 2][:], AF.Tanh),
                             "A", ("tanh", s, i))

            def p1_mm2(k):
                s, c = k // SS, k % SS

                @blk.tensor
                def _(t, k=k, s=s, c=c):
                    if k >= 1:
                        t.wait_ge(SEM["V"], tick[("aev", k - 1)])
                    t.wait_ge(SEM["A"], tick[("tanh", s, 7)])
                    if k == 0:
                        t.wait_ge(SEM["V"], tick["ddiag"])
                    for j in range(16):
                        _mi = t.matmul(PS2[:, 128 * j:128 * (j + 1)],
                                 XAT[:, D * c + 128 * j:D * c + 128 * (j + 1)],
                                 DD[:, 128 * j:128 * (j + 1)],
                                 start=(j % 4 == 0), stop=False,
                                 skip_group_check=True)
                    for i in range(8):
                        for h in range(4):
                            _mi = t.matmul(
                                PS2[:, 512 * h:512 * (h + 1)],
                                STT[:, 512 * i + 128 * c:
                                    512 * i + 128 * (c + 1)],
                                CT[:, D * i + 512 * h:D * i + 512 * (h + 1)],
                                start=False, stop=(i == 7),
                                skip_group_check=True)
                    bump(_mi, "P", ("mm2", k))

            def p1_post(k):
                @blk.vector
                def _(v, k=k):
                    v.wait_ge(SEM["P"], tick[("mm2", k)])
                    if k >= 2:
                        v.wait_ge(SEM["V"], tick[("sig", k - 2)])
                    bump(v.tensor_scalar(ATTNB[k % 2][:], PS2[:], 0.0, 0.0,
                                         OP.add, OP.add,
                                         accum_out=SMA[:, k:k + 1]),
                         "V", ("aev", k))
                    v.wait_ge(SEM["V"], tick[("aev", k)])
                    bump(v.tensor_scalar(MUA[:, k:k + 1], SMA[:, k:k + 1],
                                         1.0 / D, None, OP.mult),
                         "V", ("mua", k))

                @blk.scalar
                def _(a, k=k):
                    a.wait_ge(SEM["V"], tick[("aev", k)])
                    bump(a.activation(VMM[:], ATTNB[k % 2][:], AF.Square,
                                      accum_out=SQA[:, k:k + 1]),
                         "A", ("sqa", k))

                @blk.vector
                def _(v, k=k):
                    v.wait_ge(SEM["V"], tick[("mua", k)])
                    v.wait_ge(SEM["A"], tick[("sqa", k)])
                    bump(v.tensor_tensor(W32a[:, k:k + 1], MUA[:, k:k + 1],
                                         MUA[:, k:k + 1], OP.mult),
                         "V", ("m2a", k))
                    v.wait_ge(SEM["V"], tick[("m2a", k)])
                    bump(v.scalar_tensor_tensor(VAC[:, k:k + 1],
                                                SQA[:, k:k + 1], 1.0 / D,
                                                W32a[:, k:k + 1], OP.mult,
                                                OP.subtract), "V", ("va", k))

                @blk.scalar
                def _(a, k=k):
                    a.wait_ge(SEM["V"], tick[("va", k)])
                    bump(a.activation(VAC[:, k:k + 1], VAC[:, k:k + 1], AF.Sqrt,
                                      bias=EPSC[:]), "A", ("lna", k))

                @blk.vector
                def _(v, k=k):
                    v.wait_ge(SEM["A"], tick[("lna", k)])
                    bump(v.reciprocal(RAA[:, k:k + 1], VAC[:, k:k + 1]),
                         "V", ("ba", k))

                @blk.vector
                def _(v, k=k):
                    v.wait_ge(SEM["V"], tick[("ba", k)])
                    v.wait_ge(SEM["A"], tick[("sqa", k)])
                    if k == 0:
                        v.wait_ge(SEM["PD"], tick["thdma"])
                        v.wait_ge(SEM["V"], tick["tbl"])
                    bump(v.tensor_scalar(VMM[:], ATTNB[k % 2][:],
                                         MUA[:, k:k + 1], None, OP.subtract),
                         "V", ("tp", k))
                    v.wait_ge(SEM["V"], tick[("tp", k)])
                    bump(v.scalar_tensor_tensor(X0B[k % 4][:], VMM[:],
                                                RAA[:, k:k + 1],
                                                X0B[k % 4][:], OP.mult, OP.add,
                                                accum_out=SM1[:, k:k + 1]),
                         "V", ("x1", k))

                @blk.sync
                def _(sy, k=k):
                    sy.wait_ge(SEM["V"], tick[("x1", k)])
                    dbump(sy.dma_start(
                        bass.AP(x1sp, k * 128 * D, [[D, 128], [1, D]]),
                        X0B[k % 4][:]), f"SP{k % 4}", ("sp", k))

                @blk.scalar
                def _(a, k=k):
                    a.wait_ge(SEM["V"], tick[("x1", k)])
                    bump(a.activation(VMM[:], X0B[k % 4][:], AF.Square,
                                      accum_out=SQ1[:, k:k + 1]),
                         "A", ("sq1", k))

                @blk.vector
                def _(v, k=k):
                    v.wait_ge(SEM["V"], tick[("x1", k)])
                    bump(v.tensor_scalar(MU1[:, k:k + 1], SM1[:, k:k + 1],
                                         1.0 / D, None, OP.mult),
                         "V", ("mu1", k))
                    v.wait_ge(SEM["V"], tick[("mu1", k)])
                    v.wait_ge(SEM["A"], tick[("sq1", k)])
                    bump(v.tensor_tensor(W32a[:, k:k + 1], MU1[:, k:k + 1],
                                         MU1[:, k:k + 1], OP.mult),
                         "V", ("m21", k))
                    v.wait_ge(SEM["V"], tick[("m21", k)])
                    bump(v.scalar_tensor_tensor(V1A[:, k:k + 1],
                                                SQ1[:, k:k + 1], 1.0 / D,
                                                W32a[:, k:k + 1], OP.mult,
                                                OP.subtract), "V", ("v1", k))

                @blk.scalar
                def _(a, k=k):
                    a.wait_ge(SEM["V"], tick[("v1", k)])
                    bump(a.activation(W32b[:, k:k + 1], V1A[:, k:k + 1],
                                      AF.Sqrt, bias=EPSC[:]), "A", ("ln1", k))

                @blk.vector
                def _(v, k=k):
                    v.wait_ge(SEM["A"], tick[("ln1", k)])
                    bump(v.reciprocal(R1A[:, k:k + 1], W32b[:, k:k + 1]),
                         "V", ("r1", k))
                    bump(v.tensor_scalar(V2C[:, k:k + 1], V1A[:, k:k + 1],
                                         1.0 + EPS, EPS * EPS, OP.mult,
                                         OP.add), "V", ("v2", k))

                @blk.scalar
                def _(a, k=k):
                    a.wait_ge(SEM["V"], tick[("v2", k)])
                    bump(a.activation(S12[:, k:k + 1], V2C[:, k:k + 1],
                                      AF.Sqrt), "A", ("ln2", k))

                @blk.vector
                def _(v, k=k):
                    v.wait_ge(SEM["A"], tick[("ln2", k)])
                    bump(v.reciprocal(R12A[:, k:k + 1], S12[:, k:k + 1]),
                         "V", ("s12", k))
                    v.wait_ge(SEM["V"], tick[("s12", k)])
                    if k == 0:
                        v.wait_ge(SEM["PD"], tick["thbcast"])
                    bump(v.tensor_scalar(ATTNB[k % 2][:], THB[:],
                                         S12[:, k:k + 1], MU1[:, k:k + 1],
                                         OP.mult, OP.add), "V", ("thr", k))
                    v.wait_ge(SEM["V"], tick[("thr", k)])
                    if k >= 1:
                        v.wait_ge(SEM["P"], tick[("sgt", k - 1)])
                    bump(v.tensor_tensor(SIGB[0][:], X0B[k % 4][:],
                                         ATTNB[k % 2][:], OP.is_gt),
                         "V", ("sig", k))

            def p1_tsg(k):
                for grp in range(4):
                    @blk.tensor
                    def _(t, grp=grp, k=k):
                        if grp == 0:
                            t.wait_ge(SEM["V"], tick[("sig", k)])
                        e, tk_ = pst_free[0]
                        t.wait_ge(SEM[e], tk_)
                        for j4 in range(4):
                            j = grp * 4 + j4
                            _mi = t.transpose(PST[:, 128 * j4:128 * (j4 + 1)],
                                        SIGB[0][:, 128 * j:128 * (j + 1)],
                                        SID[:])
                        bump(_mi, "P", ("trs", k, grp))

                    @blk.vector
                    def _(v, grp=grp, k=k):
                        v.wait_ge(SEM["P"], tick[("trs", k, grp)])
                        bump(v.tensor_copy(
                            SGT[0][:, 512 * grp:512 * (grp + 1)], PST[:]),
                            "V", ("evS", k, grp))
                        pst_free[0] = ("V", tick[("evS", k, grp)])
                tick[("sgt", k)] = tick[("trs", k, 3)]

            def p1_scores(k):
                @blk.tensor
                def _(t, k=k):
                    t.wait_ge(SEM["V"], tick[("evS", k, 3)])
                    if k == 0:
                        t.wait_ge(SEM["G"], tick["winit"])
                    else:
                        t.wait_ge(SEM["V"], tick[("gbev", k - 1)])
                    for j in range(16):
                        _mi = t.matmul(PSS[:, 0:64],
                                 SGT[0][:, 128 * j:128 * (j + 1)],
                                 WSS[:, 64 * j:64 * (j + 1)],
                                 start=(j == 0), stop=False,
                                 skip_group_check=True)
                    _mi = t.matmul(PSS[:, 0:64], ONE1[:], BIAR[:], start=False,
                             stop=True, skip_group_check=True)
                    bump(_mi, "P", ("sc", k))

                @blk.vector
                def _(v, k=k):
                    v.wait_ge(SEM["P"], tick[("sc", k)])
                    bump(v.tensor_reduce(
                        CS8[:], PSS[:, 0:64].rearrange("p (a b) -> p a b", a=8),
                        AX.X, OP.add), "V", ("cs8", k))
                    bump(v.tensor_copy(TSC[:], PSS[:, 0:64]), "V", ("tsev", k))
                    v.wait_ge(SEM["V"], tick[("cs8", k)])
                    bump(v.max(MX8[:], CS8[:]), "V", ("cmx", k))
                    v.wait_ge(SEM["V"], tick[("cmx", k)])
                    bump(v.max_index(IX8[:], MX8[:], CS8[:]), "V", ("cix", k))
                    v.wait_ge(SEM["V"], tick[("cix", k)])
                    bump(v.tensor_copy(BCF[:], IX8[:, 0:1]), "V", ("bcf", k))
                    v.wait_ge(SEM["V"], tick[("bcf", k)])
                    bump(v.tensor_scalar(TSM[:], CLB[:], BCF[:], None,
                                         OP.is_equal), "V", ("m64", k))
                    v.wait_ge(SEM["V"], tick[("m64", k)])
                    bump(v.scalar_tensor_tensor(TSM[:], TSC[:], BIG, TSM[:],
                                                OP.add, OP.mult),
                         "V", ("msk", k))
                    v.wait_ge(SEM["V"], tick[("msk", k)])
                    bump(v.max(MX8b[:], TSM[:]), "V", ("tmx", k))
                    v.wait_ge(SEM["V"], tick[("tmx", k)])
                    bump(v.max_index(IX8b[:], MX8b[:], TSM[:]),
                         "V", ("tix", k))
                    v.wait_ge(SEM["V"], tick[("tix", k)])
                    bump(v.tensor_copy(TIDA[:, k:k + 1], IX8b[:, 0:1]),
                         "V", ("tid", k))
                    v.wait_ge(SEM["V"], tick[("tid", k)])
                    bump(v.tensor_scalar(OHB[:, 64 * k:64 * (k + 1)], TIOT[:],
                                         TIDA[:, k:k + 1], None, OP.is_equal),
                         "V", ("oh", k))
                    bump(v.scalar_tensor_tensor(IDXF[:, k:k + 1],
                                                TOKI[:, k:k + 1], 64.0,
                                                TIDA[:, k:k + 1], OP.mult,
                                                OP.add), "V", ("idxf", k))
                    v.wait_ge(SEM["V"], tick[("idxf", k)])
                    bump(v.tensor_copy(IDXU[:, k:k + 1], IDXF[:, k:k + 1]),
                         "V", ("idxu", k))

                @blk.tensor
                def _(t, k=k):
                    t.wait_ge(SEM["V"], tick[("oh", k)])
                    t.wait_ge(SEM["V"], tick[("tsev", k)])
                    _mi = t.transpose(PSSb[0:64, 512:640],
                                OHB[:, 64 * k:64 * (k + 1)], SID[:])
                    bump(_mi, "P", ("oht", k))

                @blk.vector
                def _(v, k=k):
                    v.wait_ge(SEM["P"], tick[("oht", k)])
                    bump(v.tensor_copy(OHT[:, 128 * k:128 * (k + 1)],
                                       PSSb[0:64, 512:640]), "V", ("ohtev", k))

                @blk.tensor
                def _(t, k=k):
                    t.wait_ge(SEM["V"], tick[("ohtev", k)])
                    if k == 0:
                        t.wait_ge(SEM["V"], tick["gbl"])
                    _mi = t.matmul(PSS[:, 64:128], OHT[:, 128 * k:128 * (k + 1)],
                             GBH[:], start=True, stop=False,
                             skip_group_check=True)
                    _mi = t.matmul(PSS[:, 64:128], OHT[:, 128 * k:128 * (k + 1)],
                             GBL[:], start=False, stop=True,
                             skip_group_check=True)
                    bump(_mi, "P", ("gbmm", k))

                @blk.vector
                def _(v, k=k):
                    v.wait_ge(SEM["P"], tick[("gbmm", k)])
                    bump(v.tensor_copy(GBA[:, 64 * k:64 * (k + 1)],
                                       PSS[:, 64:128]), "V", ("gbev", k))


            for s in range(NSUP):
                for c0 in (0, 2):
                    for c in (c0, c0 + 1):
                        k = s * SS + c
                        p1_load(k)
                        p1_pre(k)
                    for c in (c0, c0 + 1):
                        k = s * SS + c
                        p1_xa(k)
                        p1_txa(k)
                p1_mm1(s)
                for c in range(SS):
                    k = s * SS + c
                    p1_mm2(k)
                    p1_post(k)
                    p1_tsg(k)
                    p1_scores(k)

            @blk.gpsimd
            def _(g):
                for q in range(4):
                    g.wait_ge(SEM[f"SP{q}"], tick[("sp", T - 4 + q)])
                g.wait_ge(SEM["V"], tick[("idxu", T - 1)])
                for k in range(T):
                    dbump(g.indirect_dma_start(
                        out=XSL[:, 32 * k:32 * (k + 1)], out_offset=None,
                        in_=x1sp.ap(),
                        in_offset=bass.IndirectOffsetOnAxis(
                            ap=IDXU[:, k:k + 1], axis=0)), "PD", ("xg", k))

            @blk.vector
            def _(v):
                v.wait_ge(SEM["PD"], tick[("xg", T - 1)])
                bump(v.tensor_tensor(XNS[:], XSL[:],
                                MU1[:].to_broadcast([128, T, 32]), OP.subtract), "V", "h1")
                v.wait_ge(SEM["V"], tick["h1"])
                bump(v.tensor_tensor(XNS[:], XNS[:],
                                R12A[:].to_broadcast([128, T, 32]), OP.mult), "V", "h3")
                v.wait_ge(SEM["V"], tick["h3"])
                bump(v.tensor_tensor(XNS[:], XNS[:],
                                bass.AP(GBA, 0, [[T * 64, 128], [64, T],
                                                 [1, 32]]), OP.mult), "V", "h4")
                v.wait_ge(SEM["V"], tick["h4"])
                bump(v.tensor_tensor(XNS[:], XNS[:],
                                bass.AP(GBA, 32, [[T * 64, 128], [64, T],
                                                  [1, 32]]), OP.add), "V", "xns")
                v.wait_ge(SEM["V"], tick["xns"])
                v.tensor_reduce(TMIN[:],
                                XNS[:].rearrange("p (a b) -> p a b", a=T),
                                AX.X, OP.min)
                bump(v.tensor_reduce(TMAX[:],
                                     XNS[:].rearrange("p (a b) -> p a b", a=T),
                                     AX.X, OP.max), "V", "tmm")
                v.wait_ge(SEM["V"], tick["tmm"])
                v.wait_ge(SEM["A"], tick[("sq1", T - 1)])
                bump(v.tensor_scalar(
                    bass.AP(VMM, 0, [[2048, 128], [32, 64], [1, 32]]),
                    bass.AP(OHB, 0, [[T * 64, 128], [1, 64], [64, T]]),
                    -1e38, 1e38, OP.mult, OP.add), "V", "ms1")
                v.wait_ge(SEM["V"], tick["ms1"])
                bump(v.tensor_tensor(
                    bass.AP(VMM, 0, [[2048, 128], [32, 64], [1, 32]]),
                    bass.AP(VMM, 0, [[2048, 128], [32, 64], [1, 32]]),
                    bass.AP(TMIN, 0, [[T, 128], [0, 64], [1, T]]),
                    OP.add), "V", "vm1")
                v.wait_ge(SEM["V"], tick["vm1"])
                bump(v.tensor_reduce(
                    RED[:], VMM[:].rearrange("p (a b) -> p a b", a=64),
                    AX.X, OP.min), "V", "redmin")

            @blk.tensor
            def _(t):
                t.wait_ge(SEM["V"], tick["redmin"])
                t.wait_ge(SEM["V"], tick["sidf"])
                _mi = t.transpose(PSS[0:64, 128:256], RED[:], SIDF[:])
                bump(_mi, "P", "trmin")

            @blk.vector
            def _(v):
                v.wait_ge(SEM["P"], tick["trmin"])
                bump(v.tensor_reduce(PMM[:, 0:1],
                                     PSS[0:64, 128:256], AX.X, OP.min),
                     "V", "pmin")
                v.wait_ge(SEM["V"], tick["pmin"])
                bump(v.tensor_scalar(PMM[:, 0:1], PMM[:, 0:1], -1.0, None,
                                     OP.mult), "V", "pminneg")
                bump(v.tensor_scalar(
                    bass.AP(VMM, 0, [[2048, 128], [32, 64], [1, 32]]),
                    bass.AP(OHB, 0, [[T * 64, 128], [1, 64], [64, T]]),
                    1e38, -1e38, OP.mult, OP.add), "V", "ms2")
                v.wait_ge(SEM["V"], tick["ms2"])
                bump(v.tensor_tensor(
                    bass.AP(VMM, 0, [[2048, 128], [32, 64], [1, 32]]),
                    bass.AP(VMM, 0, [[2048, 128], [32, 64], [1, 32]]),
                    bass.AP(TMAX, 0, [[T, 128], [0, 64], [1, T]]),
                    OP.add), "V", "vx1")
                v.wait_ge(SEM["V"], tick["vx1"])
                bump(v.tensor_reduce(
                    RED[:], VMM[:].rearrange("p (a b) -> p a b", a=64),
                    AX.X, OP.max), "V", "redmax")

            @blk.tensor
            def _(t):
                t.wait_ge(SEM["V"], tick["redmax"])
                _mi = t.transpose(PSS[0:64, 256:384], RED[:], SIDF[:])
                bump(_mi, "P", "trmax")

            @blk.vector
            def _(v):
                v.wait_ge(SEM["P"], tick["trmax"])
                bump(v.tensor_reduce(PMM[:, 1:2], PSS[0:64, 256:384], AX.X,
                                     OP.max), "V", "pmax")

        @blk.gpsimd
        def _(g):
            g.wait_ge(SEM["V"], tick["pmax"])
            g.wait_ge(SEM["V"], tick["pminneg"])
            dbump(g.dma_start(bass.AP(cc_in, 0, [[2, 64], [1, 2]]), PMM[:]),
                  "PD")
            g.wait_ge(SEM["PD"], cnt["PD"])
            g.collective_compute(
                "AllReduce", OP.max, replica_groups=[list(range(NCORES))],
                ins=[cc_in.ap().opt()], outs=[cc_out.ap().opt()],
            ).then_inc(SEM["CC"], 1)
            cnt["CC"] += 1
            g.wait_ge(SEM["CC"], cnt["CC"])
            dbump(g.dma_start(GMM[:], bass.AP(cc_out, 0, [[2, 64], [1, 2]])),
                  "PD", "ccb")
            bump(g.wait_ge(SEM["PD"], cnt["PD"]), "G", "ccdone")

        if True:
            PSS2 = PSS
            PSP = [PS2.ap()[:, 0:1024], PS2.ap()[:, 1024:2048]]
            XATF = XAT.ap().bitcast(F32)
            XATB = XAT.ap()
            STTF = STT.ap().bitcast(F32)
            UAL = XATF[:, 0:1024]
            GIF = XATF[:, 1024:2048]
            DPP = XATF[:, 2048:3072]
            GIOT = XATB[:, 6656:7168]
            GMH = XATB[0:64, 7168:7171]
            GML = XATB[0:64, 7200:7203]
            MNX = STTF[:, 0:96]
            OHG = VMM[:, 1024:1536]
            GIU = bass.AP(VMM, 1536, [[T * 64, 128], [1, 512]]).bitcast(I32) \
                if False else None
            GIU = bass.AP(VMM, 1024, [[T * 64, 128], [1, 1024]]).bitcast(I32)
            XLC = UAL
            BSA = GBA

            @blk.gpsimd
            def _(g):
                _gi = g.iota(GIOT, [[0, 32], [1, 16]], channel_multiplier=0,
                       allow_small_or_imprecise_dtypes=True)
                bump(_gi, "G", "giot")

            @blk.vector
            def _(v):
                v.wait_ge(SEM["G"], tick["ccdone"])
                v.tensor_scalar(VMM[0:64, 0:1], GMM[:, 0:1], -1.0, None, OP.mult)
                v.tensor_copy(VMM[0:64, 1:2], GMM[:, 1:2])
                bump(v.tensor_copy(VMM[0:64, 2:3], SOSC[:]), "V", "gm0")
                v.wait_ge(SEM["V"], tick["gm0"])
                bump(v.tensor_copy(GMH, VMM[0:64, 0:3]), "V", "gmh")
                v.wait_ge(SEM["V"], tick["gmh"])
                bump(v.tensor_copy(VMM[0:64, 3:6], GMH), "V", "gmb")
                v.wait_ge(SEM["V"], tick["gmb"])
                bump(v.tensor_tensor(VMM[0:64, 6:9], VMM[0:64, 0:3], VMM[0:64, 3:6],
                                     OP.subtract), "V", "gml0")
                v.wait_ge(SEM["V"], tick["gml0"])
                bump(v.tensor_copy(GML, VMM[0:64, 6:9]), "V", "gml")

            for k in range(T):
                @blk.tensor
                def _(t, k=k):
                    if k == 0:
                        t.wait_ge(SEM["V"], tick["gml"])
                    _mi = t.matmul(PSS2[:, 3 * k:3 * (k + 1)],
                             OHT[:, 128 * k:128 * (k + 1)], GMH,
                             start=(k == 0), stop=False, skip_group_check=True)
                    _mi = t.matmul(PSS2[:, 3 * k:3 * (k + 1)],
                             OHT[:, 128 * k:128 * (k + 1)], GML,
                             start=False, stop=True, skip_group_check=True)
                    if k == T - 1:
                        bump(_mi, "P", "mnmx")

            @blk.vector
            def _(v):
                v.wait_ge(SEM["P"], tick["mnmx"])
                bump(v.tensor_copy(MNX, PSS2[:, 0:96]), "V", "mnxev")
                v.wait_ge(SEM["V"], tick["mnxev"])
                m3 = MNX.rearrange("p (a b) -> p a b", a=T)
                mnv = m3[:, :, 0:1]
                mxv = m3[:, :, 1:2]
                bump(v.tensor_tensor(W32a[:], mxv, mnv, OP.subtract),
                     "V", "den0")
                v.wait_ge(SEM["V"], tick["den0"])
                bump(v.tensor_scalar(W32a[:], W32a[:], 1e-8, None, OP.add),
                     "V", "den1")
                v.wait_ge(SEM["V"], tick["den1"])
                bump(v.reciprocal(INV[:], W32a[:]), "V", "inv")
                v.wait_ge(SEM["V"], tick["inv"])
                bump(v.tensor_tensor(UAL, XNS[:],
                                m3[:, :, 0:1].to_broadcast([128, T, 1, 32]),
                                OP.subtract), "V", "u0")
                v.wait_ge(SEM["V"], tick["u0"])
                bump(v.tensor_tensor(UAL, UAL,
                                INV[:].to_broadcast([128, T, 32]), OP.mult), "V", "u1")
                v.wait_ge(SEM["V"], tick["u1"])
                bump(v.tensor_scalar(UAL, UAL, 0.0, 1.0 - 1e-6, OP.max,
                                OP.min), "V", "u2")
                v.wait_ge(SEM["V"], tick["u2"])
                bump(v.tensor_scalar(GIF, UAL, 16.0, None, OP.mult), "V", "g16")
                v.wait_ge(SEM["V"], tick["g16"])
                bump(v.tensor_copy(GIU, GIF), "V", "giu")
                v.wait_ge(SEM["V"], tick["giu"])
                bump(v.tensor_copy(GIF, GIU), "V", "gif")
                v.wait_ge(SEM["V"], tick["gif"])
                bump(v.scalar_tensor_tensor(XLC[:], UAL, 16.0, GIF, OP.mult,
                                       OP.subtract), "V", "xlc")

            for k in range(T):
                @blk.tensor
                def _(t, k=k):
                    if k == 0:
                        t.wait_ge(SEM["V"], tick["tbl"])
                        t.wait_ge(SEM["V"], tick["xlc"])
                    if k >= 2:
                        t.wait_ge(SEM["V"], tick[("bs", k - 2)])
                    for h in range(2):
                        _mi = t.matmul(PSP[k % 2][:, 512 * h:512 * (h + 1)],
                                 OHT[:, 128 * k:128 * (k + 1)],
                                 TBH[:, 512 * h:512 * (h + 1)],
                                 start=True, stop=False, skip_group_check=True)
                        _mi = t.matmul(PSP[k % 2][:, 512 * h:512 * (h + 1)],
                                 OHT[:, 128 * k:128 * (k + 1)],
                                 TBL[:, 512 * h:512 * (h + 1)],
                                 start=False, stop=True, skip_group_check=True)
                    bump(_mi, "P", ("pg", k))

                @blk.vector
                def _(v, k=k):
                    v.wait_ge(SEM["P"], tick[("pg", k)])
                    if k == 0:
                        v.wait_ge(SEM["G"], tick["giot"])
                    bump(v.tensor_tensor(
                        OHG, GIF[:, 32 * k:32 * (k + 1)]
                        .to_broadcast([128, 32, 16]), GIOT, OP.is_equal), "V", ("ohg", k))
                    v.wait_ge(SEM["V"], tick[("ohg", k)])
                    bump(v.tensor_tensor(VMM[:, 0:1024], bass.AP(
                        VMM, 1024, [[T * 64, 128], [0, 2], [1, 512]]),
                        PSP[k % 2], OP.mult), "V", ("prod", k))
                    v.wait_ge(SEM["V"], tick[("prod", k)])
                    bump(v.tensor_reduce(
                        BSA[:, 64 * k:64 * (k + 1)],
                        VMM[:, 0:1024].rearrange("p (a b) -> p a b", a=64),
                        AX.X, OP.add), "V", ("bs", k))

            @blk.vector
            def _(v):
                v.wait_ge(SEM["V"], tick[("bs", T - 1)])
                bview = bass.AP(BSA, 0, [[T * 64, 128], [64, T], [1, 32]])
                sview = bass.AP(BSA, 32, [[T * 64, 128], [64, T], [1, 32]])
                oview = MNX.rearrange("p (a b) -> p a b", a=T)[:, :, 2:3] \
                    .to_broadcast([128, T, 1, 32])
                bump(v.tensor_tensor(DPP, sview, XLC[:], OP.mult), "V", "d1")
                v.wait_ge(SEM["V"], tick["d1"])
                bump(v.tensor_tensor(DPP, DPP, bview, OP.add), "V", "d2")
                v.wait_ge(SEM["V"], tick["d2"])
                bump(v.tensor_tensor(DPP, DPP, oview, OP.mult), "V", "delta")
                v.wait_ge(SEM["V"], tick["delta"])
                bump(v.tensor_reduce(W32a[:],
                                DPP.rearrange("p (a b) -> p a b", a=T),
                                AX.X, OP.add), "V", "sd")
                v.wait_ge(SEM["V"], tick["sd"])
                bump(v.tensor_scalar(MUG[:], W32a[:], 1.0 / D, None, OP.mult), "V", "mug")
                bump(v.tensor_tensor(VMM[:, 0:T * 32], XSL[:],
                                     MU1[:].to_broadcast([128, T, 32]),
                                     OP.subtract), "V", "hn0")
                v.wait_ge(SEM["V"], tick["hn0"])
                bump(v.tensor_tensor(VMM[:, 0:T * 32], VMM[:, 0:T * 32],
                                     R1A[:].to_broadcast([128, T, 32]),
                                     OP.mult), "V", "hn1")
                v.wait_ge(SEM["V"], tick["hn1"])
                bump(v.tensor_tensor(VMM[:, T * 32:T * 64], VMM[:, 0:T * 32],
                                     DPP, OP.mult), "V", "cr0")
                v.wait_ge(SEM["V"], tick["cr0"])
                bump(v.tensor_reduce(W32b[:],
                                VMM[:, T * 32:T * 64].rearrange(
                                    "p (a b) -> p a b", a=T), AX.X, OP.add), "V", "cr")
                v.wait_ge(SEM["V"], tick["cr"])
                bump(v.tensor_tensor(VMM[:, T * 32:T * 64], DPP, DPP, OP.mult), "V", "dd0")
                v.wait_ge(SEM["V"], tick["dd0"])
                bump(v.tensor_reduce(W32c[:],
                                VMM[:, T * 32:T * 64].rearrange(
                                    "p (a b) -> p a b", a=T), AX.X, OP.add), "V", "ddq")
                v.wait_ge(SEM["V"], tick["ddq"])
                bump(v.tensor_tensor(W32a[:], R1A[:], R1A[:], OP.mult), "V", "q1")
                v.wait_ge(SEM["V"], tick["q1"])
                bump(v.tensor_tensor(W32a[:], W32a[:], V1A[:], OP.mult), "V", "q2")
                v.wait_ge(SEM["V"], tick["q2"])
                bump(v.scalar_tensor_tensor(W32a[:], W32a[:], float(D), W32b[:],
                                       OP.mult, OP.add), "V", "q3")
                v.wait_ge(SEM["V"], tick["q3"])
                bump(v.tensor_tensor(W32a[:], W32a[:], W32b[:], OP.add), "V", "q4")
                v.wait_ge(SEM["V"], tick["q4"])
                bump(v.tensor_tensor(W32a[:], W32a[:], W32c[:], OP.add), "V", "q5")
                v.wait_ge(SEM["V"], tick["q5"])
                bump(v.tensor_tensor(W32b[:], MUG[:], MUG[:], OP.mult), "V", "q6")
                v.wait_ge(SEM["V"], tick["q6"])
                bump(v.scalar_tensor_tensor(W32a[:], W32a[:], 1.0 / D, W32b[:],
                                       OP.mult, OP.subtract), "V", "vg")

            @blk.scalar
            def _(a):
                a.wait_ge(SEM["V"], tick["vg"])
                bump(a.activation(W32a[:], W32a[:], AF.Sqrt, bias=EPSC[:]),
                     "A", "lng")

            @blk.vector
            def _(v):
                v.wait_ge(SEM["A"], tick["lng"])
                bump(v.reciprocal(RGA[:], W32a[:]), "V", "rg")
                v.wait_ge(SEM["V"], tick["rg"])
                bump(v.tensor_tensor(W32a[:], R1A[:], RGA[:], OP.mult), "V", "s20")
                v.wait_ge(SEM["V"], tick["s20"])
                bump(v.tensor_scalar(S2A[:], W32a[:], 1.0, None, OP.add),
                     "V", "s2")
                bump(v.tensor_tensor(W32b[:], MU1[:], R1A[:], OP.mult), "V", "b20")
                v.wait_ge(SEM["V"], tick["b20"])
                bump(v.tensor_tensor(W32b[:], W32b[:], MUG[:], OP.add), "V", "b21")
                v.wait_ge(SEM["V"], tick["b21"])
                bump(v.tensor_tensor(W32b[:], W32b[:], RGA[:], OP.mult), "V", "b22")
                v.wait_ge(SEM["V"], tick["b22"])
                bump(v.tensor_scalar(B2A[:], W32b[:], -1.0, None, OP.mult),
                     "V", "b2")
                bump(v.tensor_tensor(DPP, DPP, VMM[:, 0:T * 32], OP.add), "V", "e1")
                v.wait_ge(SEM["V"], tick["e1"])
                bump(v.tensor_tensor(DPP, DPP,
                                MUG[:].to_broadcast([128, T, 32]), OP.subtract), "V", "e2")
                v.wait_ge(SEM["V"], tick["e2"])
                bump(v.tensor_tensor(DPP, DPP,
                                RGA[:].to_broadcast([128, T, 32]), OP.mult), "V", "e3")
                v.wait_ge(SEM["V"], tick["e3"])
                bump(v.tensor_tensor(DPP, DPP, XSL[:], OP.add), "V", "dpp")

            for k in range(T):
                @blk.sync
                def _(sy, k=k):
                    if k == 0:
                        for q in range(4):
                            sy.wait_ge(SEM[f"SP{q}"], tick[("sp", T - 4 + q)])
                        sy.wait_ge(SEM["V"], tick[("sig", T - 1)])
                    if k >= 2:
                        sy.wait_ge(SEM[f"OS{k % 2}"], tick[("st", k - 2)])
                    dbump(sy.dma_start(
                        X0B[k % 2][:],
                        bass.AP(x1sp, k * 128 * D, [[D, 128], [1, D]])),
                        f"RL{k % 2}", ("rl", k))

                @blk.vector
                def _(v, k=k):
                    v.wait_ge(SEM[f"RL{k % 2}"], tick[("rl", k)])
                    if k == 0:
                        v.wait_ge(SEM["V"], tick["b2"])
                    bump(v.tensor_scalar(X0B[k % 2][:], X0B[k % 2][:],
                                         S2A[:, k:k + 1], B2A[:, k:k + 1],
                                         OP.mult, OP.add), "V", ("t2", k))

                @blk.sync
                def _(sy, k=k):
                    sy.wait_ge(SEM["V"], tick[("t2", k)])
                    dbump(sy.dma_start(
                        bass.AP(outd, k * 128 * D, [[D, 128], [1, D]]),
                        X0B[k % 2][:]), f"OS{k % 2}", ("st", k))

                @blk.gpsimd
                def _(g, k=k):
                    g.wait_ge(SEM[f"OS{k % 2}"], tick[("st", k)])
                    if k == 0:
                        g.wait_ge(SEM["V"], tick["dpp"])
                    dbump(g.indirect_dma_start(
                        out=bass.AP(outd, 0, [[32, TOK * 64], [1, 32]]),
                        out_offset=bass.IndirectOffsetOnAxis(
                            ap=IDXU[:, k:k + 1], axis=0),
                        in_=DPP[:, 32 * k:32 * (k + 1)], in_offset=None),
                        "PD", ("scat", k))

            if debug:
                @blk.sync
                def _(sy):
                    sy.wait_ge(SEM["V"], tick["dpp"])
                    dbump(sy.dma_start(dbg1d.ap(), XNS[:, 0:D]), "OS0")
                    dbump(sy.dma_start(dbg2d.ap(), BSA[:, 0:2048]), "OS0")

            @blk.sync
            def _(sy):
                sy.wait_ge(SEM["OS0"], cnt["OS0"])
                sy.wait_ge(SEM["OS1"], cnt["OS1"])
                sy.wait_ge(SEM["PD"], cnt["PD"])

    return nc


_PROG_CACHE = {}


def _get_prog(debug=False):
    key = bool(debug)
    if key not in _PROG_CACHE:
        _PROG_CACHE[key] = build_program(debug=key)
    return _PROG_CACHE[key]


def make_in_maps(L0_attn_x, L0_attn_B, L0_attn_C, L0_attn_D, L0_ffn_ln_gamma,
                 L0_ffn_ln_beta, L0_ffn_spline_bases, L0_ffn_spline_slopes,
                 L0_ffn_output_scale):
    x = np.ascontiguousarray(np.asarray(L0_attn_x, dtype=np.float32)
                             .reshape(NCORES * TOK, D))
    bt = np.ascontiguousarray(np.asarray(L0_attn_B, dtype=np.float32).T)
    ct = np.ascontiguousarray(np.asarray(L0_attn_C, dtype=np.float32).T)
    dcol = np.ascontiguousarray(
        np.asarray(L0_attn_D, dtype=np.float32).reshape(16, 128).T)
    gam = np.ascontiguousarray(
        np.asarray(L0_ffn_ln_gamma, dtype=np.float32).reshape(64, 32))
    bet = np.ascontiguousarray(
        np.asarray(L0_ffn_ln_beta, dtype=np.float32).reshape(64, 32))
    bas = np.ascontiguousarray(
        np.asarray(L0_ffn_spline_bases, dtype=np.float32).reshape(64, 512))
    slo = np.ascontiguousarray(
        np.asarray(L0_ffn_spline_slopes, dtype=np.float32).reshape(64, 512))
    osc = np.ascontiguousarray(
        np.asarray(L0_ffn_output_scale, dtype=np.float32).reshape(64, 1))
    return [{"x0": x[core * TOK:(core + 1) * TOK],
             "bt": bt, "ct": ct, "dcol": dcol, "gam": gam, "bet": bet,
             "bas": bas, "slo": slo, "osc": osc} for core in range(NCORES)]


def kernel(**inputs):
    in_maps = make_in_maps(**inputs)
    nc = _get_prog(debug=False)
    res = run_bass_kernel_spmd(nc, in_maps, core_ids=list(range(NCORES)))
    out = np.concatenate([r["out"] for r in res.results], axis=0)
    return out.reshape(2, 16384, D).astype(np.float32)


# version 06
# speedup vs baseline: 1.2938x; 1.1557x over previous
import numpy as np
from contextlib import ExitStack

import concourse.bass as bass
import concourse.mybir as mybir
from concourse.alu_op_type import AluOpType as OP
from concourse.bass_utils import run_bass_kernel_spmd

F32 = mybir.dt.float32
BF16 = mybir.dt.bfloat16
U32 = mybir.dt.uint32
I32 = mybir.dt.int32
AF = mybir.ActivationFunctionType
AX = mybir.AxisListType

NCORES = 8
TOK = 4096
T = 32
SS = 4
NSUP = T // SS
D = 2048
NS = 1024
EPS = 1e-5
BIG = 1000.0


def build_program(debug=False):
    nc = bass.Bass("TRN2", target_bir_lowering=False)

    x0d = nc.dram_tensor("x0", [TOK, D], F32, kind="ExternalInput")
    btd = nc.dram_tensor("bt", [D, NS], F32, kind="ExternalInput")
    ctd = nc.dram_tensor("ct", [NS, D], F32, kind="ExternalInput")
    dcold = nc.dram_tensor("dcol", [128, 16], F32, kind="ExternalInput")
    gamd = nc.dram_tensor("gam", [64, 32], F32, kind="ExternalInput")
    betd = nc.dram_tensor("bet", [64, 32], F32, kind="ExternalInput")
    basd = nc.dram_tensor("bas", [64, 512], F32, kind="ExternalInput")
    slod = nc.dram_tensor("slo", [64, 512], F32, kind="ExternalInput")
    oscd = nc.dram_tensor("osc", [64, 1], F32, kind="ExternalInput")
    outd = nc.dram_tensor("out", [TOK, D], F32, kind="ExternalOutput")
    if debug:
        dbg1d = nc.dram_tensor("dbg1", [128, D], F32, kind="ExternalOutput")
        dbg2d = nc.dram_tensor("dbg2", [128, 2048], F32, kind="ExternalOutput")

    x1sp = nc.dram_tensor("x1sp", [TOK * 64, 32], F32)
    th_d = nc.dram_tensor("th_d", [1, D], F32)
    w2_d = nc.dram_tensor("w2_d", [64, 32], BF16)
    bia_d = nc.dram_tensor("bia_d", [1, 64], BF16)
    cc_in = nc.dram_tensor("cc_in", [1, 128], F32)
    cc_out = nc.dram_tensor("cc_out", [1, 128], F32)

    with ExitStack() as st:
        def sb(name, shape, dt):
            return st.enter_context(nc.sbuf_tensor(name, shape, dt))

        BT = sb("BT", [128, 16 * NS], BF16)
        CT = sb("CT", [128, 8 * D], BF16)
        TOKI = sb("TOKI", [128, T], F32)
        EPSC = sb("EPSC", [128, 1], F32)
        VMM = sb("VMM", [128, T * 64], F32)
        SOSC = sb("SOSC", [64, 1], F32)
        TBH = sb("TBH", [64, 1024], BF16)
        TBL = sb("TBL", [64, 1024], BF16)
        GMM = sb("GMM", [64, 2], F32)
        PMM = sb("PMM", [64, 2], F32)

        X0B = [sb(f"X0B{i}", [128, D], F32) for i in range(4)]
        ATTNB = [sb(f"ATTNB{i}", [128, D], F32) for i in range(2)]

        MU0 = sb("MU0", [128, T], F32)
        SM0 = sb("SM0", [128, T], F32)
        SQ0 = sb("SQ0", [128, T], F32)
        V0C = sb("V0C", [128, T], F32)
        R0A = sb("R0A", [128, T], F32)
        B0A = sb("B0A", [128, T], F32)
        SMA = sb("SMA", [128, T], F32)
        SQA = sb("SQA", [128, T], F32)
        MUA = sb("MUA", [128, T], F32)
        VAC = sb("VAC", [128, T], F32)
        RAA = sb("RAA", [128, T], F32)
        BAA = sb("BAA", [128, T], F32)
        SM1 = sb("SM1", [128, T], F32)
        SQ1 = sb("SQ1", [128, T], F32)
        MU1 = sb("MU1", [128, T], F32)
        V1A = sb("V1A", [128, T], F32)
        R1A = sb("R1A", [128, T], F32)
        V2C = sb("V2C", [128, T], F32)
        S12 = sb("S12", [128, T], F32)
        TIDA = sb("TIDA", [128, T], F32)
        IDXF = sb("IDXF", [128, T], F32)
        IDXU = sb("IDXU", [128, T], U32)
        R12A = sb("R12A", [128, T], F32)
        TMIN = sb("TMIN", [128, T], F32)
        TMAX = sb("TMAX", [128, T], F32)
        MUG = sb("MUG", [128, T], F32)
        RGA = sb("RGA", [128, T], F32)
        S2A = sb("S2A", [128, T], F32)
        B2A = sb("B2A", [128, T], F32)
        INV = sb("INV", [128, T], F32)
        W32a = sb("W32a", [128, T], F32)
        W32b = sb("W32b", [128, T], F32)
        W32c = sb("W32c", [128, T], F32)


        OHT = sb("OHT", [64, T * 128], BF16)
        GBA = sb("GBA", [128, T * 64], F32)
        XSL = sb("XSL", [128, T * 32], F32)
        XNS = sb("XNS", [128, T * 32], F32)
        RED = sb("RED", [128, 64], F32)

        SEM = {n: st.enter_context(nc.semaphore(n)) for n in
               ["LD", "PD", "CC", "V", "A", "P", "G",
                "SP0", "SP1", "SP2", "SP3",
                "LX0", "LX1", "LX2", "LX3", "RL0", "RL1", "OS0", "OS1"]}
        cnt = {n: 0 for n in SEM}
        tick = {}

        def bump(inst, e, key=None):
            cnt[e] += 1
            inst.then_inc(SEM[e], 1)
            if key is not None:
                tick[key] = cnt[e]
            return cnt[e]

        def dbump(inst, e, key=None):
            cnt[e] += 16
            inst.then_inc(SEM[e], 16)
            if key is not None:
                tick[key] = cnt[e]
            return cnt[e]

        blk = st.enter_context(nc.Block())

        pst_free = [("V", 0)]

        if True:
            PS1 = [st.enter_context(nc.psum_tensor("PS1a", [128, 512], F32)),
                   st.enter_context(nc.psum_tensor("PS1b", [128, 512], F32))]
            PS2 = st.enter_context(nc.psum_tensor("PS2", [128, 2048], F32))
            PST = st.enter_context(nc.psum_tensor("PST", [128, 512], BF16))
            PSS = st.enter_context(nc.psum_tensor("PSS", [128, 512], F32))
            PSSb = PSS.ap().bitcast(BF16)
            sb1 = sb
            DD = sb1("DD", [128, 16 * 128], BF16)
            DCOL = sb1("DCOL", [128, 16], F32)
            WSS = sb1("WSS", [128, 16 * 64], BF16)
            THB = sb1("THB", [128, D], BF16)
            SID = sb1("SID", [128, 128], BF16)
            SIDF = sb1("SIDF", [128, 128], F32)
            ONE1 = sb1("ONE1", [1, 128], BF16)
            BIAR = sb1("BIAR", [1, 64], BF16)
            TIOT = sb1("TIOT", [128, 64], F32)
            CLB = sb1("CLB", [128, 64], F32)
            IOT128 = sb1("IOT128", [128, 128], F32)
            PIDX = sb1("PIDX", [128, 1], F32)
            SGAM = sb1("SGAM", [64, 32], F32)
            SBET = sb1("SBET", [64, 32], F32)
            GBH = sb1("GBH", [64, 64], BF16)
            GBL = sb1("GBL", [64, 64], BF16)
            W2BF = sb1("W2BF", [64, 32], BF16)
            BI64 = sb1("BI64", [64, 1], BF16)
            XAB = [sb("XAB0", [128, D], BF16)]
            SIGB = [sb("SIGB0", [128, D], BF16)]
            XAT = sb1("XAT", [128, SS * D], BF16)
            STT = sb1("STT", [128, 8 * 512], BF16)
            SGT = [sb("SGT0", [128, D], BF16)]
            CS8 = sb1("CS8", [128, 8], F32)
            MX8 = sb1("MX8", [128, 8], F32)
            IX8 = sb1("IX8", [128, 8], U32)
            TSC = sb1("TSC", [128, 64], F32)
            TSM = sb1("TSM", [128, 64], F32)
            MX8b = sb1("MX8b", [128, 8], F32)
            IX8b = sb1("IX8b", [128, 8], U32)
            BCF = sb1("BCF", [128, 1], F32)
            OHB = sb1("OHB", [128, T * 64], BF16)

            @blk.sync
            def _(sy):
                dbump(sy.dma_start(SGAM[:], gamd.ap()), "LD")
                dbump(sy.dma_start(SBET[:], betd.ap()), "LD")
                dbump(sy.dma_start(VMM[0:64, 0:512], basd.ap()), "LD")
                dbump(sy.dma_start(VMM[0:64, 512:1024], slod.ap()), "LD")
                dbump(sy.dma_start(SOSC[:], oscd.ap()), "LD")
                dbump(sy.dma_start(DCOL[:], dcold.ap()), "LD", "ldconst")

            @blk.gpsimd
            def _(g):
                g.memset(EPSC[:], EPS)
                g.memset(ONE1[:], 1.0)
                g.memset(WSS[:], 0)
                _gi = g.iota(TIOT[:], [[1, 64]], channel_multiplier=0,
                       allow_small_or_imprecise_dtypes=True)
                _gi = g.iota(CLB[:], [[1, 8], [0, 8]], channel_multiplier=0,
                       allow_small_or_imprecise_dtypes=True)
                _gi = g.iota(TOKI[:], [[128, T]], channel_multiplier=1,
                       allow_small_or_imprecise_dtypes=True)
                _gi = g.iota(PIDX[:], [[0, 1]], channel_multiplier=1,
                       allow_small_or_imprecise_dtypes=True)
                _gi = g.iota(IOT128[:], [[1, 128]], channel_multiplier=0,
                       allow_small_or_imprecise_dtypes=True)
                bump(_gi, "G", "iotas")
                dbump(g.dma_start(
                    BT[:], bass.AP(btd, 0, [[NS, 128], [128 * NS, 16], [1, NS]])),
                    "PD")
                dbump(g.dma_start(
                    CT[:], bass.AP(ctd, 0, [[D, 128], [128 * D, 8], [1, D]])),
                    "PD", "wload")
                g.wait_ge(SEM["PD"], cnt["PD"])

            @blk.vector
            def _(v):
                v.wait_ge(SEM["G"], tick["iotas"])
                bump(v.tensor_scalar(SID[:], IOT128[:], PIDX[:], None,
                                     OP.is_equal), "V", "sid")
                v.wait_ge(SEM["V"], tick["sid"])
                bump(v.tensor_copy(SIDF[:], SID[:]), "V", "sidf")
                v.wait_ge(SEM["LD"], tick["ldconst"])
                for j in range(16):
                    _vi = v.tensor_scalar(DD[:, 128 * j:128 * (j + 1)], SID[:],
                                          DCOL[:, j:j + 1], None, OP.mult)
                bump(_vi, "V", "ddiag")
                bump(v.reciprocal(VMM[0:64, 1300:1332], SGAM[:]), "V", "recg")
                v.wait_ge(SEM["V"], tick["recg"])
                bump(v.tensor_tensor(VMM[0:64, 1332:1364], VMM[0:64, 1300:1332],
                                     SBET[:], OP.mult), "V", "th0")
                v.wait_ge(SEM["V"], tick["th0"])
                bump(v.tensor_scalar(VMM[0:64, 1332:1364], VMM[0:64, 1332:1364],
                                     -1.0, None, OP.mult), "V", "theta")
                v.wait_ge(SEM["LD"], tick["ldconst"])
                bump(v.tensor_reduce(VMM[0:64, 1364:1396],
                                     VMM[0:64, 512:1024].rearrange(
                                         "p (a b) -> p a b", a=32),
                                     AX.X, OP.add), "V", "ssum")

            @blk.gpsimd
            def _(g):
                g.wait_ge(SEM["V"], tick["theta"])
                dbump(g.dma_start(th_d.ap(), VMM[0:64, 1332:1364]), "PD", "thdma")
                g.wait_ge(SEM["PD"], cnt["PD"])
                dbump(g.dma_start(THB[:], bass.AP(th_d, 0, [[0, 128], [1, D]])),
                      "PD", "thbcast")

            @blk.scalar
            def _(a):
                a.wait_ge(SEM["V"], tick["ssum"])
                bump(a.sign(VMM[0:64, 1396:1428], VMM[0:64, 1364:1396]),
                     "A", "ssgn")
                bump(a.sign(VMM[0:64, 1428:1460], SGAM[:]), "A", "sgam")

            @blk.vector
            def _(v):
                v.wait_ge(SEM["A"], tick["sgam"])
                bump(v.tensor_tensor(VMM[0:64, 1460:1492], VMM[0:64, 1396:1428],
                                     VMM[0:64, 1428:1460], OP.mult), "V", "w0")
                v.wait_ge(SEM["V"], tick["w0"])
                bump(v.tensor_reduce(VMM[0:64, 1492:1493],
                                     VMM[0:64, 1460:1492].unsqueeze(1), AX.X,
                                     OP.add), "V", "bsum")
                v.wait_ge(SEM["V"], tick["bsum"])
                bump(v.tensor_scalar(BI64[:], VMM[0:64, 1492:1493], -1.0, None,
                                     OP.mult), "V", "bi64")
                bump(v.tensor_scalar(W2BF[:], VMM[0:64, 1460:1492], 2.0, None,
                                     OP.mult), "V", "w2bf")
                v.wait_ge(SEM["LD"], tick["ldconst"])
                v.tensor_copy(VMM[0:64, 1024:1056], SGAM[:])
                bump(v.tensor_copy(VMM[0:64, 1056:1088], SBET[:]), "V", "gb0")
                v.wait_ge(SEM["V"], tick["gb0"])
                bump(v.tensor_copy(GBH[:], VMM[0:64, 1024:1088]), "V", "gbh")
                v.wait_ge(SEM["V"], tick["gbh"])
                bump(v.tensor_copy(VMM[0:64, 1088:1152], GBH[:]), "V", "gbb")
                v.wait_ge(SEM["V"], tick["gbb"])
                bump(v.tensor_tensor(VMM[0:64, 1152:1216], VMM[0:64, 1024:1088],
                                     VMM[0:64, 1088:1152], OP.subtract), "V", "gbl0")
                v.wait_ge(SEM["V"], tick["gbl0"])
                bump(v.tensor_copy(GBL[:], VMM[0:64, 1152:1216]), "V", "gbl")
                v.wait_ge(SEM["PD"], tick["thdma"])
                bump(v.tensor_copy(TBH[:], VMM[0:64, 0:1024]), "V", "tbh")
                v.wait_ge(SEM["V"], tick["tbh"])
                bump(v.tensor_copy(VMM[0:64, 1024:2048], TBH[:]), "V", "tbb")
                v.wait_ge(SEM["V"], tick["tbb"])
                bump(v.tensor_tensor(VMM[0:64, 1024:2048], VMM[0:64, 0:1024],
                                     VMM[0:64, 1024:2048], OP.subtract), "V", "tbl0")
                v.wait_ge(SEM["V"], tick["tbl0"])
                bump(v.tensor_copy(TBL[:], VMM[0:64, 1024:2048]), "V", "tbl")

            @blk.gpsimd
            def _(g):
                g.wait_ge(SEM["V"], tick["w2bf"])
                g.wait_ge(SEM["PD"], cnt["PD"])
                dbump(g.dma_start(w2_d.ap(), W2BF[:]), "PD")
                g.wait_ge(SEM["PD"], cnt["PD"])
                for b in range(4):
                    dbump(g.dma_start(
                        bass.AP(WSS, 32 * b * (16 * 64) + b, [[16 * 64, 32],
                                                              [68, 16]]),
                        bass.AP(w2_d, 32 * b, [[1, 32], [128, 16]]),
                        allow_slow_non_contiguous=True), "PD")
                g.wait_ge(SEM["V"], tick["bi64"])
                dbump(g.dma_start(bia_d.ap(), BI64[:]), "PD")
                g.wait_ge(SEM["PD"], cnt["PD"])
                dbump(g.dma_start(BIAR[:], bia_d.ap()), "PD")
                bump(g.wait_ge(SEM["PD"], cnt["PD"]), "G", "winit")



            a_defer = []

            def defer_a(keys, fn):
                for key in keys:
                    cnt["A"] += 1
                    tick[key] = cnt["A"]
                a_defer.append(fn)

            def flush_a():
                fns = list(a_defer)
                a_defer.clear()

                @blk.scalar
                def _(a):
                    for fn in fns:
                        fn(a)

            def p1_load(k):
                @blk.sync
                def _(sy, k=k):
                    if k >= 4:
                        sy.wait_ge(SEM["V"], tick[("sig", k - 4)])
                        sy.wait_ge(SEM[f"SP{k % 4}"], tick[("sp", k - 4)])
                    dbump(sy.dma_start(
                        X0B[k % 4][:],
                        bass.AP(x0d, k * 128 * D, [[D, 128], [1, D]])),
                        f"LX{k % 4}", ("x0", k))

            def p1_pre(k):
                @blk.vector
                def _(v, k=k):
                    v.wait_ge(SEM[f"LX{k % 4}"], tick[("x0", k)])
                    bump(v.tensor_reduce(SM0[:, k:k + 1], X0B[k % 4][:],
                                         AX.X, OP.add), "V", ("sum0", k))
                    v.wait_ge(SEM["V"], tick[("sum0", k)])
                    bump(v.tensor_scalar(MU0[:, k:k + 1], SM0[:, k:k + 1],
                                         1.0 / D, None, OP.mult),
                         "V", ("mu0", k))

                def _sq0(a, k=k):
                    a.wait_ge(SEM[f"LX{k % 4}"], tick[("x0", k)])
                    if k == 0:
                        a.wait_ge(SEM["V"], tick["tbl"])
                        a.wait_ge(SEM["PD"], tick["thdma"])
                    if k >= 4:
                        a.wait_ge(SEM["V"], tick[("x1", 4 * (k // 4) - 1)])
                    a.activation(VMM[:], X0B[k % 4][:], AF.Square,
                                 accum_out=SQ0[:, k:k + 1]
                                 ).then_inc(SEM["A"], 1)
                defer_a([("sq0", k)], _sq0)

                @blk.vector
                def _(v, k=k):
                    v.wait_ge(SEM["V"], tick[("mu0", k)])
                    v.wait_ge(SEM["A"], tick[("sq0", k)])
                    bump(v.tensor_tensor(W32a[:, k:k + 1], MU0[:, k:k + 1],
                                         MU0[:, k:k + 1], OP.mult),
                         "V", ("m20", k))
                    v.wait_ge(SEM["V"], tick[("m20", k)])
                    bump(v.scalar_tensor_tensor(V0C[:, k:k + 1],
                                                SQ0[:, k:k + 1], 1.0 / D,
                                                W32a[:, k:k + 1], OP.mult,
                                                OP.subtract), "V", ("v0", k))

                def _ln0(a, k=k):
                    a.wait_ge(SEM["V"], tick[("v0", k)])
                    a.activation(V0C[:, k:k + 1], V0C[:, k:k + 1], AF.Sqrt,
                                 bias=EPSC[:]).then_inc(SEM["A"], 1)
                defer_a([("ln0", k)], _ln0)

                @blk.vector
                def _(v, k=k):
                    v.wait_ge(SEM["A"], tick[("ln0", k)])
                    bump(v.reciprocal(R0A[:, k:k + 1], V0C[:, k:k + 1]),
                         "V", ("r0", k))
                    v.wait_ge(SEM["V"], tick[("r0", k)])
                    bump(v.scalar_tensor_tensor(B0A[:, k:k + 1],
                                                MU0[:, k:k + 1], -1.0,
                                                R0A[:, k:k + 1], OP.mult,
                                                OP.mult), "V", ("b0", k))

            def p1_xa(k):
                @blk.vector
                def _(v, k=k):
                    v.wait_ge(SEM["V"], tick[("b0", k)])
                    if k >= 1:
                        v.wait_ge(SEM["P"], tick[("xat", k - 1)])
                    bump(v.tensor_scalar(XAB[0][:], X0B[k % 4][:],
                                         R0A[:, k:k + 1], B0A[:, k:k + 1],
                                         OP.mult, OP.add), "V", ("xa", k))

            def p1_txa(k):
                c = k % SS
                for grp in range(4):
                    @blk.tensor
                    def _(t, grp=grp, c=c, k=k):
                        if grp == 0:
                            t.wait_ge(SEM["V"], tick[("xa", k)])
                            t.wait_ge(SEM["V"], tick["sid"])
                        e, tk_ = pst_free[0]
                        t.wait_ge(SEM[e], tk_)
                        for j4 in range(4):
                            j = grp * 4 + j4
                            _mi = t.transpose(PST[:, 128 * j4:128 * (j4 + 1)],
                                        XAB[0][:, 128 * j:128 * (j + 1)],
                                        SID[:])
                        bump(_mi, "P", ("trx", k, grp))

                    def _ev(a, grp=grp, c=c, k=k):
                        a.wait_ge(SEM["P"], tick[("trx", k, grp)])
                        a.copy(
                            XAT[:, D * c + 512 * grp:D * c + 512 * (grp + 1)],
                            PST[:]).then_inc(SEM["A"], 1)
                    defer_a([("evT", k, grp)], _ev)
                    pst_free[0] = ("A", tick[("evT", k, grp)])
                tick[("xat", k)] = tick[("trx", k, 3)]

            def p1_mm1(s):
                xat3 = XAT[:].rearrange("p (c f) -> p c f", c=SS)
                for i in range(8):
                    @blk.tensor
                    def _(t, i=i, s=s):
                        if i == 0:
                            t.wait_ge(SEM["A"], tick[("evT", s * SS + SS - 1, 3)])
                            if s == 0:
                                t.wait_ge(SEM["PD"], tick["wload"])
                        prev = (s, i - 2) if i >= 2 else \
                            ((s - 1, i + 6) if s > 0 else None)
                        if prev is not None:
                            t.wait_ge(SEM["A"], tick[("tanh",) + prev])
                        for j in range(16):
                            _mi = t.matmul(PS1[i % 2][:],
                                     BT[:, NS * j + 128 * i:
                                        NS * j + 128 * (i + 1)],
                                     xat3[:, :, 128 * j:128 * (j + 1)],
                                     start=(j == 0), stop=(j == 15))
                        bump(_mi, "P", ("mm1", s, i))

                    def _th(a, i=i, s=s):
                        a.wait_ge(SEM["P"], tick[("mm1", s, i)])
                        a.activation(STT[:, 512 * i:512 * (i + 1)],
                                     PS1[i % 2][:], AF.Tanh
                                     ).then_inc(SEM["A"], 1)
                    defer_a([("tanh", s, i)], _th)

            def p1_mm2(k):
                s, c = k // SS, k % SS

                @blk.tensor
                def _(t, k=k, s=s, c=c):
                    if k >= 1:
                        t.wait_ge(SEM["V"], tick[("aev", k - 1)])
                    t.wait_ge(SEM["A"], tick[("tanh", s, 7)])
                    if k == 0:
                        t.wait_ge(SEM["V"], tick["ddiag"])
                    for j in range(16):
                        _mi = t.matmul(PS2[:, 128 * j:128 * (j + 1)],
                                 XAT[:, D * c + 128 * j:D * c + 128 * (j + 1)],
                                 DD[:, 128 * j:128 * (j + 1)],
                                 start=(j % 4 == 0), stop=False,
                                 skip_group_check=True)
                    for i in range(8):
                        for h in range(4):
                            _mi = t.matmul(
                                PS2[:, 512 * h:512 * (h + 1)],
                                STT[:, 512 * i + 128 * c:
                                    512 * i + 128 * (c + 1)],
                                CT[:, D * i + 512 * h:D * i + 512 * (h + 1)],
                                start=False, stop=(i == 7),
                                skip_group_check=True)
                    bump(_mi, "P", ("mm2", k))

            def p1_post(k):
                @blk.vector
                def _(v, k=k):
                    v.wait_ge(SEM["P"], tick[("mm2", k)])
                    if k >= 2:
                        v.wait_ge(SEM["V"], tick[("sig", k - 2)])
                    bump(v.tensor_scalar(ATTNB[k % 2][:], PS2[:], 0.0, 0.0,
                                         OP.add, OP.add,
                                         accum_out=SMA[:, k:k + 1]),
                         "V", ("aev", k))
                    v.wait_ge(SEM["V"], tick[("aev", k)])
                    bump(v.tensor_scalar(MUA[:, k:k + 1], SMA[:, k:k + 1],
                                         1.0 / D, None, OP.mult),
                         "V", ("mua", k))

                def _sqa(a, k=k):
                    a.wait_ge(SEM["V"], tick[("aev", k)])
                    a.activation(VMM[:], ATTNB[k % 2][:], AF.Square,
                                 accum_out=SQA[:, k:k + 1]
                                 ).then_inc(SEM["A"], 1)
                defer_a([("sqa", k)], _sqa)

                @blk.vector
                def _(v, k=k):
                    v.wait_ge(SEM["V"], tick[("mua", k)])
                    v.wait_ge(SEM["A"], tick[("sqa", k)])
                    bump(v.tensor_tensor(W32a[:, k:k + 1], MUA[:, k:k + 1],
                                         MUA[:, k:k + 1], OP.mult),
                         "V", ("m2a", k))
                    v.wait_ge(SEM["V"], tick[("m2a", k)])
                    bump(v.scalar_tensor_tensor(VAC[:, k:k + 1],
                                                SQA[:, k:k + 1], 1.0 / D,
                                                W32a[:, k:k + 1], OP.mult,
                                                OP.subtract), "V", ("va", k))

                def _lna(a, k=k):
                    a.wait_ge(SEM["V"], tick[("va", k)])
                    a.activation(VAC[:, k:k + 1], VAC[:, k:k + 1], AF.Sqrt,
                                 bias=EPSC[:]).then_inc(SEM["A"], 1)
                defer_a([("lna", k)], _lna)

                @blk.vector
                def _(v, k=k):
                    v.wait_ge(SEM["A"], tick[("lna", k)])
                    bump(v.reciprocal(RAA[:, k:k + 1], VAC[:, k:k + 1]),
                         "V", ("ba", k))

                @blk.vector
                def _(v, k=k):
                    v.wait_ge(SEM["V"], tick[("ba", k)])
                    v.wait_ge(SEM["A"], tick[("sqa", k)])
                    if k == 0:
                        v.wait_ge(SEM["PD"], tick["thdma"])
                        v.wait_ge(SEM["V"], tick["tbl"])
                    bump(v.tensor_scalar(VMM[:], ATTNB[k % 2][:],
                                         MUA[:, k:k + 1], None, OP.subtract),
                         "V", ("tp", k))
                    v.wait_ge(SEM["V"], tick[("tp", k)])
                    bump(v.scalar_tensor_tensor(X0B[k % 4][:], VMM[:],
                                                RAA[:, k:k + 1],
                                                X0B[k % 4][:], OP.mult, OP.add,
                                                accum_out=SM1[:, k:k + 1]),
                         "V", ("x1", k))

                @blk.sync
                def _(sy, k=k):
                    sy.wait_ge(SEM["V"], tick[("x1", k)])
                    dbump(sy.dma_start(
                        bass.AP(x1sp, k * 128 * D, [[D, 128], [1, D]]),
                        X0B[k % 4][:]), f"SP{k % 4}", ("sp", k))

                def _sq1(a, k=k):
                    a.wait_ge(SEM["V"], tick[("x1", k)])
                    a.activation(VMM[:], X0B[k % 4][:], AF.Square,
                                 accum_out=SQ1[:, k:k + 1]
                                 ).then_inc(SEM["A"], 1)
                defer_a([("sq1", k)], _sq1)

                @blk.vector
                def _(v, k=k):
                    v.wait_ge(SEM["V"], tick[("x1", k)])
                    bump(v.tensor_scalar(MU1[:, k:k + 1], SM1[:, k:k + 1],
                                         1.0 / D, None, OP.mult),
                         "V", ("mu1", k))
                    v.wait_ge(SEM["V"], tick[("mu1", k)])
                    v.wait_ge(SEM["A"], tick[("sq1", k)])
                    bump(v.tensor_tensor(W32a[:, k:k + 1], MU1[:, k:k + 1],
                                         MU1[:, k:k + 1], OP.mult),
                         "V", ("m21", k))
                    v.wait_ge(SEM["V"], tick[("m21", k)])
                    bump(v.scalar_tensor_tensor(V1A[:, k:k + 1],
                                                SQ1[:, k:k + 1], 1.0 / D,
                                                W32a[:, k:k + 1], OP.mult,
                                                OP.subtract), "V", ("v1", k))

                def _ln1(a, k=k):
                    a.wait_ge(SEM["V"], tick[("v1", k)])
                    a.activation(W32b[:, k:k + 1], V1A[:, k:k + 1],
                                 AF.Sqrt, bias=EPSC[:]).then_inc(SEM["A"], 1)
                defer_a([("ln1", k)], _ln1)

                @blk.vector
                def _(v, k=k):
                    v.wait_ge(SEM["A"], tick[("ln1", k)])
                    bump(v.reciprocal(R1A[:, k:k + 1], W32b[:, k:k + 1]),
                         "V", ("r1", k))
                    bump(v.tensor_scalar(V2C[:, k:k + 1], V1A[:, k:k + 1],
                                         1.0 + EPS, EPS * EPS, OP.mult,
                                         OP.add), "V", ("v2", k))

                def _ln2(a, k=k):
                    a.wait_ge(SEM["V"], tick[("v2", k)])
                    a.activation(S12[:, k:k + 1], V2C[:, k:k + 1],
                                 AF.Sqrt).then_inc(SEM["A"], 1)
                defer_a([("ln2", k)], _ln2)

                @blk.vector
                def _(v, k=k):
                    v.wait_ge(SEM["A"], tick[("ln2", k)])
                    bump(v.reciprocal(R12A[:, k:k + 1], S12[:, k:k + 1]),
                         "V", ("s12", k))
                    v.wait_ge(SEM["V"], tick[("s12", k)])
                    if k == 0:
                        v.wait_ge(SEM["PD"], tick["thbcast"])
                    bump(v.tensor_scalar(ATTNB[k % 2][:], THB[:],
                                         S12[:, k:k + 1], MU1[:, k:k + 1],
                                         OP.mult, OP.add), "V", ("thr", k))
                    v.wait_ge(SEM["V"], tick[("thr", k)])
                    if k >= 1:
                        v.wait_ge(SEM["P"], tick[("sgt", k - 1)])
                    bump(v.tensor_tensor(SIGB[0][:], X0B[k % 4][:],
                                         ATTNB[k % 2][:], OP.is_gt),
                         "V", ("sig", k))

            def p1_tsg(k):
                for grp in range(4):
                    @blk.tensor
                    def _(t, grp=grp, k=k):
                        if grp == 0:
                            t.wait_ge(SEM["V"], tick[("sig", k)])
                        e, tk_ = pst_free[0]
                        t.wait_ge(SEM[e], tk_)
                        for j4 in range(4):
                            j = grp * 4 + j4
                            _mi = t.transpose(PST[:, 128 * j4:128 * (j4 + 1)],
                                        SIGB[0][:, 128 * j:128 * (j + 1)],
                                        SID[:])
                        bump(_mi, "P", ("trs", k, grp))

                    @blk.vector
                    def _(v, grp=grp, k=k):
                        v.wait_ge(SEM["P"], tick[("trs", k, grp)])
                        bump(v.tensor_copy(
                            SGT[0][:, 512 * grp:512 * (grp + 1)], PST[:]),
                            "V", ("evS", k, grp))
                        pst_free[0] = ("V", tick[("evS", k, grp)])
                tick[("sgt", k)] = tick[("trs", k, 3)]

            def p1_scores(k):
                @blk.tensor
                def _(t, k=k):
                    t.wait_ge(SEM["V"], tick[("evS", k, 3)])
                    if k == 0:
                        t.wait_ge(SEM["G"], tick["winit"])
                    else:
                        t.wait_ge(SEM["V"], tick[("gbev", k - 1)])
                    for j in range(16):
                        _mi = t.matmul(PSS[:, 0:64],
                                 SGT[0][:, 128 * j:128 * (j + 1)],
                                 WSS[:, 64 * j:64 * (j + 1)],
                                 start=(j == 0), stop=False,
                                 skip_group_check=True)
                    _mi = t.matmul(PSS[:, 0:64], ONE1[:], BIAR[:], start=False,
                             stop=True, skip_group_check=True)
                    bump(_mi, "P", ("sc", k))

                @blk.vector
                def _(v, k=k):
                    v.wait_ge(SEM["P"], tick[("sc", k)])
                    bump(v.tensor_reduce(
                        CS8[:], PSS[:, 0:64].rearrange("p (a b) -> p a b", a=8),
                        AX.X, OP.add), "V", ("cs8", k))
                    bump(v.tensor_copy(TSC[:], PSS[:, 0:64]), "V", ("tsev", k))
                    v.wait_ge(SEM["V"], tick[("cs8", k)])
                    bump(v.max(MX8[:], CS8[:]), "V", ("cmx", k))
                    v.wait_ge(SEM["V"], tick[("cmx", k)])
                    bump(v.max_index(IX8[:], MX8[:], CS8[:]), "V", ("cix", k))
                    v.wait_ge(SEM["V"], tick[("cix", k)])
                    bump(v.tensor_copy(BCF[:], IX8[:, 0:1]), "V", ("bcf", k))
                    v.wait_ge(SEM["V"], tick[("bcf", k)])
                    bump(v.tensor_scalar(TSM[:], CLB[:], BCF[:], None,
                                         OP.is_equal), "V", ("m64", k))
                    v.wait_ge(SEM["V"], tick[("m64", k)])
                    bump(v.scalar_tensor_tensor(TSM[:], TSC[:], BIG, TSM[:],
                                                OP.add, OP.mult),
                         "V", ("msk", k))
                    v.wait_ge(SEM["V"], tick[("msk", k)])
                    bump(v.max(MX8b[:], TSM[:]), "V", ("tmx", k))
                    v.wait_ge(SEM["V"], tick[("tmx", k)])
                    bump(v.max_index(IX8b[:], MX8b[:], TSM[:]),
                         "V", ("tix", k))
                    v.wait_ge(SEM["V"], tick[("tix", k)])
                    bump(v.tensor_copy(TIDA[:, k:k + 1], IX8b[:, 0:1]),
                         "V", ("tid", k))
                    v.wait_ge(SEM["V"], tick[("tid", k)])
                    bump(v.tensor_scalar(OHB[:, 64 * k:64 * (k + 1)], TIOT[:],
                                         TIDA[:, k:k + 1], None, OP.is_equal),
                         "V", ("oh", k))
                    bump(v.scalar_tensor_tensor(IDXF[:, k:k + 1],
                                                TOKI[:, k:k + 1], 64.0,
                                                TIDA[:, k:k + 1], OP.mult,
                                                OP.add), "V", ("idxf", k))
                    v.wait_ge(SEM["V"], tick[("idxf", k)])
                    bump(v.tensor_copy(IDXU[:, k:k + 1], IDXF[:, k:k + 1]),
                         "V", ("idxu", k))

                @blk.tensor
                def _(t, k=k):
                    t.wait_ge(SEM["V"], tick[("oh", k)])
                    t.wait_ge(SEM["V"], tick[("tsev", k)])
                    _mi = t.transpose(PSSb[0:64, 512:640],
                                OHB[:, 64 * k:64 * (k + 1)], SID[:])
                    bump(_mi, "P", ("oht", k))

                @blk.vector
                def _(v, k=k):
                    v.wait_ge(SEM["P"], tick[("oht", k)])
                    bump(v.tensor_copy(OHT[:, 128 * k:128 * (k + 1)],
                                       PSSb[0:64, 512:640]), "V", ("ohtev", k))

                @blk.tensor
                def _(t, k=k):
                    t.wait_ge(SEM["V"], tick[("ohtev", k)])
                    if k == 0:
                        t.wait_ge(SEM["V"], tick["gbl"])
                    _mi = t.matmul(PSS[:, 64:128], OHT[:, 128 * k:128 * (k + 1)],
                             GBH[:], start=True, stop=False,
                             skip_group_check=True)
                    _mi = t.matmul(PSS[:, 64:128], OHT[:, 128 * k:128 * (k + 1)],
                             GBL[:], start=False, stop=True,
                             skip_group_check=True)
                    bump(_mi, "P", ("gbmm", k))

                @blk.vector
                def _(v, k=k):
                    v.wait_ge(SEM["P"], tick[("gbmm", k)])
                    bump(v.tensor_copy(GBA[:, 64 * k:64 * (k + 1)],
                                       PSS[:, 64:128]), "V", ("gbev", k))


            for s in range(NSUP):
                for c0 in (0, 2):
                    for c in (c0, c0 + 1):
                        k = s * SS + c
                        p1_load(k)
                        p1_pre(k)
                    for c in (c0, c0 + 1):
                        k = s * SS + c
                        p1_xa(k)
                        p1_txa(k)
                p1_mm1(s)
                for c in range(SS):
                    k = s * SS + c
                    p1_mm2(k)
                    p1_post(k)
                    p1_tsg(k)
                    p1_scores(k)
                flush_a()

            @blk.gpsimd
            def _(g):
                for q in range(4):
                    g.wait_ge(SEM[f"SP{q}"], tick[("sp", T - 4 + q)])
                g.wait_ge(SEM["V"], tick[("idxu", T - 1)])
                for k in range(T):
                    dbump(g.indirect_dma_start(
                        out=XSL[:, 32 * k:32 * (k + 1)], out_offset=None,
                        in_=x1sp.ap(),
                        in_offset=bass.IndirectOffsetOnAxis(
                            ap=IDXU[:, k:k + 1], axis=0)), "PD", ("xg", k))

            @blk.vector
            def _(v):
                v.wait_ge(SEM["PD"], tick[("xg", T - 1)])
                bump(v.tensor_tensor(XNS[:], XSL[:],
                                MU1[:].to_broadcast([128, T, 32]), OP.subtract), "V", "h1")
                v.wait_ge(SEM["V"], tick["h1"])
                bump(v.tensor_tensor(XNS[:], XNS[:],
                                R12A[:].to_broadcast([128, T, 32]), OP.mult), "V", "h3")
                v.wait_ge(SEM["V"], tick["h3"])
                bump(v.tensor_tensor(XNS[:], XNS[:],
                                bass.AP(GBA, 0, [[T * 64, 128], [64, T],
                                                 [1, 32]]), OP.mult), "V", "h4")
                v.wait_ge(SEM["V"], tick["h4"])
                bump(v.tensor_tensor(XNS[:], XNS[:],
                                bass.AP(GBA, 32, [[T * 64, 128], [64, T],
                                                  [1, 32]]), OP.add), "V", "xns")
                v.wait_ge(SEM["V"], tick["xns"])
                v.tensor_reduce(TMIN[:],
                                XNS[:].rearrange("p (a b) -> p a b", a=T),
                                AX.X, OP.min)
                bump(v.tensor_reduce(TMAX[:],
                                     XNS[:].rearrange("p (a b) -> p a b", a=T),
                                     AX.X, OP.max), "V", "tmm")
                v.wait_ge(SEM["V"], tick["tmm"])
                v.wait_ge(SEM["A"], tick[("sq1", T - 1)])
                bump(v.tensor_scalar(
                    bass.AP(VMM, 0, [[2048, 128], [32, 64], [1, 32]]),
                    bass.AP(OHB, 0, [[T * 64, 128], [1, 64], [64, T]]),
                    -1e38, 1e38, OP.mult, OP.add), "V", "ms1")
                v.wait_ge(SEM["V"], tick["ms1"])
                bump(v.tensor_tensor(
                    bass.AP(VMM, 0, [[2048, 128], [32, 64], [1, 32]]),
                    bass.AP(VMM, 0, [[2048, 128], [32, 64], [1, 32]]),
                    bass.AP(TMIN, 0, [[T, 128], [0, 64], [1, T]]),
                    OP.add), "V", "vm1")
                v.wait_ge(SEM["V"], tick["vm1"])
                bump(v.tensor_reduce(
                    RED[:], VMM[:].rearrange("p (a b) -> p a b", a=64),
                    AX.X, OP.min), "V", "redmin")

            @blk.tensor
            def _(t):
                t.wait_ge(SEM["V"], tick["redmin"])
                t.wait_ge(SEM["V"], tick["sidf"])
                _mi = t.transpose(PSS[0:64, 128:256], RED[:], SIDF[:])
                bump(_mi, "P", "trmin")

            @blk.vector
            def _(v):
                v.wait_ge(SEM["P"], tick["trmin"])
                bump(v.tensor_reduce(PMM[:, 0:1],
                                     PSS[0:64, 128:256], AX.X, OP.min),
                     "V", "pmin")
                v.wait_ge(SEM["V"], tick["pmin"])
                bump(v.tensor_scalar(PMM[:, 0:1], PMM[:, 0:1], -1.0, None,
                                     OP.mult), "V", "pminneg")
                bump(v.tensor_scalar(
                    bass.AP(VMM, 0, [[2048, 128], [32, 64], [1, 32]]),
                    bass.AP(OHB, 0, [[T * 64, 128], [1, 64], [64, T]]),
                    1e38, -1e38, OP.mult, OP.add), "V", "ms2")
                v.wait_ge(SEM["V"], tick["ms2"])
                bump(v.tensor_tensor(
                    bass.AP(VMM, 0, [[2048, 128], [32, 64], [1, 32]]),
                    bass.AP(VMM, 0, [[2048, 128], [32, 64], [1, 32]]),
                    bass.AP(TMAX, 0, [[T, 128], [0, 64], [1, T]]),
                    OP.add), "V", "vx1")
                v.wait_ge(SEM["V"], tick["vx1"])
                bump(v.tensor_reduce(
                    RED[:], VMM[:].rearrange("p (a b) -> p a b", a=64),
                    AX.X, OP.max), "V", "redmax")

            @blk.tensor
            def _(t):
                t.wait_ge(SEM["V"], tick["redmax"])
                _mi = t.transpose(PSS[0:64, 256:384], RED[:], SIDF[:])
                bump(_mi, "P", "trmax")

            @blk.vector
            def _(v):
                v.wait_ge(SEM["P"], tick["trmax"])
                bump(v.tensor_reduce(PMM[:, 1:2], PSS[0:64, 256:384], AX.X,
                                     OP.max), "V", "pmax")

        @blk.gpsimd
        def _(g):
            g.wait_ge(SEM["V"], tick["pmax"])
            g.wait_ge(SEM["V"], tick["pminneg"])
            dbump(g.dma_start(bass.AP(cc_in, 0, [[2, 64], [1, 2]]), PMM[:]),
                  "PD")
            g.wait_ge(SEM["PD"], cnt["PD"])
            g.collective_compute(
                "AllReduce", OP.max, replica_groups=[list(range(NCORES))],
                ins=[cc_in.ap().opt()], outs=[cc_out.ap().opt()],
            ).then_inc(SEM["CC"], 1)
            cnt["CC"] += 1
            g.wait_ge(SEM["CC"], cnt["CC"])
            dbump(g.dma_start(GMM[:], bass.AP(cc_out, 0, [[2, 64], [1, 2]])),
                  "PD", "ccb")
            bump(g.wait_ge(SEM["PD"], cnt["PD"]), "G", "ccdone")

        if True:
            PSS2 = PSS
            PSP = [PS2.ap()[:, 0:1024], PS2.ap()[:, 1024:2048]]
            XATF = XAT.ap().bitcast(F32)
            XATB = XAT.ap()
            STTF = STT.ap().bitcast(F32)
            UAL = XATF[:, 0:1024]
            GIF = XATF[:, 1024:2048]
            DPP = XATF[:, 2048:3072]
            GIOT = XATB[:, 6656:7168]
            GMH = XATB[0:64, 7168:7171]
            GML = XATB[0:64, 7200:7203]
            MNX = STTF[:, 0:96]
            OHG = VMM[:, 1024:1536]
            GIU = bass.AP(VMM, 1536, [[T * 64, 128], [1, 512]]).bitcast(I32) \
                if False else None
            GIU = bass.AP(VMM, 1024, [[T * 64, 128], [1, 1024]]).bitcast(I32)
            XLC = UAL
            BSA = GBA

            @blk.gpsimd
            def _(g):
                _gi = g.iota(GIOT, [[0, 32], [1, 16]], channel_multiplier=0,
                       allow_small_or_imprecise_dtypes=True)
                bump(_gi, "G", "giot")

            @blk.vector
            def _(v):
                v.wait_ge(SEM["G"], tick["ccdone"])
                v.tensor_scalar(VMM[0:64, 0:1], GMM[:, 0:1], -1.0, None, OP.mult)
                v.tensor_copy(VMM[0:64, 1:2], GMM[:, 1:2])
                bump(v.tensor_copy(VMM[0:64, 2:3], SOSC[:]), "V", "gm0")
                v.wait_ge(SEM["V"], tick["gm0"])
                bump(v.tensor_copy(GMH, VMM[0:64, 0:3]), "V", "gmh")
                v.wait_ge(SEM["V"], tick["gmh"])
                bump(v.tensor_copy(VMM[0:64, 3:6], GMH), "V", "gmb")
                v.wait_ge(SEM["V"], tick["gmb"])
                bump(v.tensor_tensor(VMM[0:64, 6:9], VMM[0:64, 0:3], VMM[0:64, 3:6],
                                     OP.subtract), "V", "gml0")
                v.wait_ge(SEM["V"], tick["gml0"])
                bump(v.tensor_copy(GML, VMM[0:64, 6:9]), "V", "gml")

            for k in range(T):
                @blk.tensor
                def _(t, k=k):
                    if k == 0:
                        t.wait_ge(SEM["V"], tick["gml"])
                    _mi = t.matmul(PSS2[:, 3 * k:3 * (k + 1)],
                             OHT[:, 128 * k:128 * (k + 1)], GMH,
                             start=(k == 0), stop=False, skip_group_check=True)
                    _mi = t.matmul(PSS2[:, 3 * k:3 * (k + 1)],
                             OHT[:, 128 * k:128 * (k + 1)], GML,
                             start=False, stop=True, skip_group_check=True)
                    if k == T - 1:
                        bump(_mi, "P", "mnmx")

            @blk.vector
            def _(v):
                v.wait_ge(SEM["P"], tick["mnmx"])
                bump(v.tensor_copy(MNX, PSS2[:, 0:96]), "V", "mnxev")
                v.wait_ge(SEM["V"], tick["mnxev"])
                m3 = MNX.rearrange("p (a b) -> p a b", a=T)
                mnv = m3[:, :, 0:1]
                mxv = m3[:, :, 1:2]
                bump(v.tensor_tensor(W32a[:], mxv, mnv, OP.subtract),
                     "V", "den0")
                v.wait_ge(SEM["V"], tick["den0"])
                bump(v.tensor_scalar(W32a[:], W32a[:], 1e-8, None, OP.add),
                     "V", "den1")
                v.wait_ge(SEM["V"], tick["den1"])
                bump(v.reciprocal(INV[:], W32a[:]), "V", "inv")
                v.wait_ge(SEM["V"], tick["inv"])
                bump(v.tensor_tensor(UAL, XNS[:],
                                m3[:, :, 0:1].to_broadcast([128, T, 1, 32]),
                                OP.subtract), "V", "u0")
                v.wait_ge(SEM["V"], tick["u0"])
                bump(v.tensor_tensor(UAL, UAL,
                                INV[:].to_broadcast([128, T, 32]), OP.mult), "V", "u1")
                v.wait_ge(SEM["V"], tick["u1"])
                bump(v.tensor_scalar(UAL, UAL, 0.0, 1.0 - 1e-6, OP.max,
                                OP.min), "V", "u2")
                v.wait_ge(SEM["V"], tick["u2"])
                bump(v.tensor_scalar(GIF, UAL, 16.0, None, OP.mult), "V", "g16")
                v.wait_ge(SEM["V"], tick["g16"])
                bump(v.tensor_copy(GIU, GIF), "V", "giu")
                v.wait_ge(SEM["V"], tick["giu"])
                bump(v.tensor_copy(GIF, GIU), "V", "gif")
                v.wait_ge(SEM["V"], tick["gif"])
                bump(v.scalar_tensor_tensor(XLC[:], UAL, 16.0, GIF, OP.mult,
                                       OP.subtract), "V", "xlc")

            for k in range(T):
                @blk.tensor
                def _(t, k=k):
                    if k == 0:
                        t.wait_ge(SEM["V"], tick["tbl"])
                        t.wait_ge(SEM["V"], tick["xlc"])
                    if k >= 2:
                        t.wait_ge(SEM["V"], tick[("bs", k - 2)])
                    for h in range(2):
                        _mi = t.matmul(PSP[k % 2][:, 512 * h:512 * (h + 1)],
                                 OHT[:, 128 * k:128 * (k + 1)],
                                 TBH[:, 512 * h:512 * (h + 1)],
                                 start=True, stop=False, skip_group_check=True)
                        _mi = t.matmul(PSP[k % 2][:, 512 * h:512 * (h + 1)],
                                 OHT[:, 128 * k:128 * (k + 1)],
                                 TBL[:, 512 * h:512 * (h + 1)],
                                 start=False, stop=True, skip_group_check=True)
                    bump(_mi, "P", ("pg", k))

                @blk.vector
                def _(v, k=k):
                    v.wait_ge(SEM["P"], tick[("pg", k)])
                    if k == 0:
                        v.wait_ge(SEM["G"], tick["giot"])
                    bump(v.tensor_tensor(
                        OHG, GIF[:, 32 * k:32 * (k + 1)]
                        .to_broadcast([128, 32, 16]), GIOT, OP.is_equal), "V", ("ohg", k))
                    v.wait_ge(SEM["V"], tick[("ohg", k)])
                    bump(v.tensor_tensor(VMM[:, 0:1024], bass.AP(
                        VMM, 1024, [[T * 64, 128], [0, 2], [1, 512]]),
                        PSP[k % 2], OP.mult), "V", ("prod", k))
                    v.wait_ge(SEM["V"], tick[("prod", k)])
                    bump(v.tensor_reduce(
                        BSA[:, 64 * k:64 * (k + 1)],
                        VMM[:, 0:1024].rearrange("p (a b) -> p a b", a=64),
                        AX.X, OP.add), "V", ("bs", k))

            @blk.vector
            def _(v):
                v.wait_ge(SEM["V"], tick[("bs", T - 1)])
                bview = bass.AP(BSA, 0, [[T * 64, 128], [64, T], [1, 32]])
                sview = bass.AP(BSA, 32, [[T * 64, 128], [64, T], [1, 32]])
                oview = MNX.rearrange("p (a b) -> p a b", a=T)[:, :, 2:3] \
                    .to_broadcast([128, T, 1, 32])
                bump(v.tensor_tensor(DPP, sview, XLC[:], OP.mult), "V", "d1")
                v.wait_ge(SEM["V"], tick["d1"])
                bump(v.tensor_tensor(DPP, DPP, bview, OP.add), "V", "d2")
                v.wait_ge(SEM["V"], tick["d2"])
                bump(v.tensor_tensor(DPP, DPP, oview, OP.mult), "V", "delta")
                v.wait_ge(SEM["V"], tick["delta"])
                bump(v.tensor_reduce(W32a[:],
                                DPP.rearrange("p (a b) -> p a b", a=T),
                                AX.X, OP.add), "V", "sd")
                v.wait_ge(SEM["V"], tick["sd"])
                bump(v.tensor_scalar(MUG[:], W32a[:], 1.0 / D, None, OP.mult), "V", "mug")
                bump(v.tensor_tensor(VMM[:, 0:T * 32], XSL[:],
                                     MU1[:].to_broadcast([128, T, 32]),
                                     OP.subtract), "V", "hn0")
                v.wait_ge(SEM["V"], tick["hn0"])
                bump(v.tensor_tensor(VMM[:, 0:T * 32], VMM[:, 0:T * 32],
                                     R1A[:].to_broadcast([128, T, 32]),
                                     OP.mult), "V", "hn1")
                v.wait_ge(SEM["V"], tick["hn1"])
                bump(v.tensor_tensor(VMM[:, T * 32:T * 64], VMM[:, 0:T * 32],
                                     DPP, OP.mult), "V", "cr0")
                v.wait_ge(SEM["V"], tick["cr0"])
                bump(v.tensor_reduce(W32b[:],
                                VMM[:, T * 32:T * 64].rearrange(
                                    "p (a b) -> p a b", a=T), AX.X, OP.add), "V", "cr")
                v.wait_ge(SEM["V"], tick["cr"])
                bump(v.tensor_tensor(VMM[:, T * 32:T * 64], DPP, DPP, OP.mult), "V", "dd0")
                v.wait_ge(SEM["V"], tick["dd0"])
                bump(v.tensor_reduce(W32c[:],
                                VMM[:, T * 32:T * 64].rearrange(
                                    "p (a b) -> p a b", a=T), AX.X, OP.add), "V", "ddq")
                v.wait_ge(SEM["V"], tick["ddq"])
                bump(v.tensor_tensor(W32a[:], R1A[:], R1A[:], OP.mult), "V", "q1")
                v.wait_ge(SEM["V"], tick["q1"])
                bump(v.tensor_tensor(W32a[:], W32a[:], V1A[:], OP.mult), "V", "q2")
                v.wait_ge(SEM["V"], tick["q2"])
                bump(v.scalar_tensor_tensor(W32a[:], W32a[:], float(D), W32b[:],
                                       OP.mult, OP.add), "V", "q3")
                v.wait_ge(SEM["V"], tick["q3"])
                bump(v.tensor_tensor(W32a[:], W32a[:], W32b[:], OP.add), "V", "q4")
                v.wait_ge(SEM["V"], tick["q4"])
                bump(v.tensor_tensor(W32a[:], W32a[:], W32c[:], OP.add), "V", "q5")
                v.wait_ge(SEM["V"], tick["q5"])
                bump(v.tensor_tensor(W32b[:], MUG[:], MUG[:], OP.mult), "V", "q6")
                v.wait_ge(SEM["V"], tick["q6"])
                bump(v.scalar_tensor_tensor(W32a[:], W32a[:], 1.0 / D, W32b[:],
                                       OP.mult, OP.subtract), "V", "vg")

            @blk.scalar
            def _(a):
                a.wait_ge(SEM["V"], tick["vg"])
                bump(a.activation(W32a[:], W32a[:], AF.Sqrt, bias=EPSC[:]),
                     "A", "lng")

            @blk.vector
            def _(v):
                v.wait_ge(SEM["A"], tick["lng"])
                bump(v.reciprocal(RGA[:], W32a[:]), "V", "rg")
                v.wait_ge(SEM["V"], tick["rg"])
                bump(v.tensor_tensor(W32a[:], R1A[:], RGA[:], OP.mult), "V", "s20")
                v.wait_ge(SEM["V"], tick["s20"])
                bump(v.tensor_scalar(S2A[:], W32a[:], 1.0, None, OP.add),
                     "V", "s2")
                bump(v.tensor_tensor(W32b[:], MU1[:], R1A[:], OP.mult), "V", "b20")
                v.wait_ge(SEM["V"], tick["b20"])
                bump(v.tensor_tensor(W32b[:], W32b[:], MUG[:], OP.add), "V", "b21")
                v.wait_ge(SEM["V"], tick["b21"])
                bump(v.tensor_tensor(W32b[:], W32b[:], RGA[:], OP.mult), "V", "b22")
                v.wait_ge(SEM["V"], tick["b22"])
                bump(v.tensor_scalar(B2A[:], W32b[:], -1.0, None, OP.mult),
                     "V", "b2")
                bump(v.tensor_tensor(DPP, DPP, VMM[:, 0:T * 32], OP.add), "V", "e1")
                v.wait_ge(SEM["V"], tick["e1"])
                bump(v.tensor_tensor(DPP, DPP,
                                MUG[:].to_broadcast([128, T, 32]), OP.subtract), "V", "e2")
                v.wait_ge(SEM["V"], tick["e2"])
                bump(v.tensor_tensor(DPP, DPP,
                                RGA[:].to_broadcast([128, T, 32]), OP.mult), "V", "e3")
                v.wait_ge(SEM["V"], tick["e3"])
                bump(v.tensor_tensor(DPP, DPP, XSL[:], OP.add), "V", "dpp")

            for k in range(T):
                @blk.sync
                def _(sy, k=k):
                    if k == 0:
                        for q in range(4):
                            sy.wait_ge(SEM[f"SP{q}"], tick[("sp", T - 4 + q)])
                        sy.wait_ge(SEM["V"], tick[("sig", T - 1)])
                    if k >= 2:
                        sy.wait_ge(SEM[f"OS{k % 2}"], tick[("st", k - 2)])
                    dbump(sy.dma_start(
                        X0B[k % 2][:],
                        bass.AP(x1sp, k * 128 * D, [[D, 128], [1, D]])),
                        f"RL{k % 2}", ("rl", k))

                @blk.vector
                def _(v, k=k):
                    v.wait_ge(SEM[f"RL{k % 2}"], tick[("rl", k)])
                    if k == 0:
                        v.wait_ge(SEM["V"], tick["b2"])
                    bump(v.tensor_scalar(X0B[k % 2][:], X0B[k % 2][:],
                                         S2A[:, k:k + 1], B2A[:, k:k + 1],
                                         OP.mult, OP.add), "V", ("t2", k))

                @blk.sync
                def _(sy, k=k):
                    sy.wait_ge(SEM["V"], tick[("t2", k)])
                    dbump(sy.dma_start(
                        bass.AP(outd, k * 128 * D, [[D, 128], [1, D]]),
                        X0B[k % 2][:]), f"OS{k % 2}", ("st", k))

                @blk.gpsimd
                def _(g, k=k):
                    g.wait_ge(SEM[f"OS{k % 2}"], tick[("st", k)])
                    if k == 0:
                        g.wait_ge(SEM["V"], tick["dpp"])
                    dbump(g.indirect_dma_start(
                        out=bass.AP(outd, 0, [[32, TOK * 64], [1, 32]]),
                        out_offset=bass.IndirectOffsetOnAxis(
                            ap=IDXU[:, k:k + 1], axis=0),
                        in_=DPP[:, 32 * k:32 * (k + 1)], in_offset=None),
                        "PD", ("scat", k))

            if debug:
                @blk.sync
                def _(sy):
                    sy.wait_ge(SEM["V"], tick["dpp"])
                    dbump(sy.dma_start(dbg1d.ap(), XNS[:, 0:D]), "OS0")
                    dbump(sy.dma_start(dbg2d.ap(), BSA[:, 0:2048]), "OS0")

            @blk.sync
            def _(sy):
                sy.wait_ge(SEM["OS0"], cnt["OS0"])
                sy.wait_ge(SEM["OS1"], cnt["OS1"])
                sy.wait_ge(SEM["PD"], cnt["PD"])

    return nc


_PROG_CACHE = {}


def _get_prog(debug=False):
    key = bool(debug)
    if key not in _PROG_CACHE:
        _PROG_CACHE[key] = build_program(debug=key)
    return _PROG_CACHE[key]


def make_in_maps(L0_attn_x, L0_attn_B, L0_attn_C, L0_attn_D, L0_ffn_ln_gamma,
                 L0_ffn_ln_beta, L0_ffn_spline_bases, L0_ffn_spline_slopes,
                 L0_ffn_output_scale):
    x = np.ascontiguousarray(np.asarray(L0_attn_x, dtype=np.float32)
                             .reshape(NCORES * TOK, D))
    bt = np.ascontiguousarray(np.asarray(L0_attn_B, dtype=np.float32).T)
    ct = np.ascontiguousarray(np.asarray(L0_attn_C, dtype=np.float32).T)
    dcol = np.ascontiguousarray(
        np.asarray(L0_attn_D, dtype=np.float32).reshape(16, 128).T)
    gam = np.ascontiguousarray(
        np.asarray(L0_ffn_ln_gamma, dtype=np.float32).reshape(64, 32))
    bet = np.ascontiguousarray(
        np.asarray(L0_ffn_ln_beta, dtype=np.float32).reshape(64, 32))
    bas = np.ascontiguousarray(
        np.asarray(L0_ffn_spline_bases, dtype=np.float32).reshape(64, 512))
    slo = np.ascontiguousarray(
        np.asarray(L0_ffn_spline_slopes, dtype=np.float32).reshape(64, 512))
    osc = np.ascontiguousarray(
        np.asarray(L0_ffn_output_scale, dtype=np.float32).reshape(64, 1))
    return [{"x0": x[core * TOK:(core + 1) * TOK],
             "bt": bt, "ct": ct, "dcol": dcol, "gam": gam, "bet": bet,
             "bas": bas, "slo": slo, "osc": osc} for core in range(NCORES)]


def kernel(**inputs):
    in_maps = make_in_maps(**inputs)
    nc = _get_prog(debug=False)
    res = run_bass_kernel_spmd(nc, in_maps, core_ids=list(range(NCORES)))
    out = np.concatenate([r["out"] for r in res.results], axis=0)
    return out.reshape(2, 16384, D).astype(np.float32)


# version 07 (final)
# speedup vs baseline: 1.3545x; 1.0469x over previous
import numpy as np
from contextlib import ExitStack

import concourse.bass as bass
import concourse.mybir as mybir
from concourse.alu_op_type import AluOpType as OP
from concourse.bass_utils import run_bass_kernel_spmd

F32 = mybir.dt.float32
BF16 = mybir.dt.bfloat16
U32 = mybir.dt.uint32
I32 = mybir.dt.int32
AF = mybir.ActivationFunctionType
AX = mybir.AxisListType

NCORES = 8
TOK = 4096
T = 32
SS = 4
NSUP = T // SS
D = 2048
NS = 1024
EPS = 1e-5
BIG = 1000.0


def build_program(debug=False):
    nc = bass.Bass("TRN2", target_bir_lowering=False)

    x0d = nc.dram_tensor("x0", [TOK, D], F32, kind="ExternalInput")
    btd = nc.dram_tensor("bt", [D, NS], F32, kind="ExternalInput")
    ctd = nc.dram_tensor("ct", [NS, D], F32, kind="ExternalInput")
    dcold = nc.dram_tensor("dcol", [128, 16], F32, kind="ExternalInput")
    gamd = nc.dram_tensor("gam", [64, 32], F32, kind="ExternalInput")
    betd = nc.dram_tensor("bet", [64, 32], F32, kind="ExternalInput")
    basd = nc.dram_tensor("bas", [64, 512], F32, kind="ExternalInput")
    slod = nc.dram_tensor("slo", [64, 512], F32, kind="ExternalInput")
    oscd = nc.dram_tensor("osc", [64, 1], F32, kind="ExternalInput")
    outd = nc.dram_tensor("out", [TOK, D], F32, kind="ExternalOutput")
    if debug:
        dbg1d = nc.dram_tensor("dbg1", [128, D], F32, kind="ExternalOutput")
        dbg2d = nc.dram_tensor("dbg2", [128, 2048], F32, kind="ExternalOutput")

    x1sp = nc.dram_tensor("x1sp", [TOK * 64, 32], F32)
    th_d = nc.dram_tensor("th_d", [1, D], F32)
    w2_d = nc.dram_tensor("w2_d", [64, 32], BF16)
    bia_d = nc.dram_tensor("bia_d", [1, 64], BF16)
    cc_in = nc.dram_tensor("cc_in", [1, 128], F32)
    cc_out = nc.dram_tensor("cc_out", [1, 128], F32)

    with ExitStack() as st:
        def sb(name, shape, dt):
            return st.enter_context(nc.sbuf_tensor(name, shape, dt))

        BT = sb("BT", [128, 16 * NS], BF16)
        CT = sb("CT", [128, 8 * D], BF16)
        TOKI = sb("TOKI", [128, T], F32)
        EPSC = sb("EPSC", [128, 1], F32)
        VMM = sb("VMM", [128, T * 64], F32)
        SOSC = sb("SOSC", [64, 1], F32)
        TBH = sb("TBH", [64, 1024], BF16)
        TBL = sb("TBL", [64, 1024], BF16)
        GMM = sb("GMM", [64, 2], F32)
        PMM = sb("PMM", [64, 2], F32)

        X0B = [sb(f"X0B{i}", [128, D], F32) for i in range(4)]
        ATTNB = [sb(f"ATTNB{i}", [128, D], F32) for i in range(2)]

        MU0 = sb("MU0", [128, T], F32)
        SM0 = sb("SM0", [128, T], F32)
        SQ0 = sb("SQ0", [128, T], F32)
        V0C = sb("V0C", [128, T], F32)
        R0A = sb("R0A", [128, T], F32)
        B0A = sb("B0A", [128, T], F32)
        SMA = sb("SMA", [128, T], F32)
        SQA = sb("SQA", [128, T], F32)
        MUA = sb("MUA", [128, T], F32)
        VAC = sb("VAC", [128, T], F32)
        RAA = sb("RAA", [128, T], F32)
        BAA = sb("BAA", [128, T], F32)
        SM1 = sb("SM1", [128, T], F32)
        SQ1 = sb("SQ1", [128, T], F32)
        MU1 = sb("MU1", [128, T], F32)
        V1A = sb("V1A", [128, T], F32)
        R1A = sb("R1A", [128, T], F32)
        V2C = sb("V2C", [128, T], F32)
        S12 = sb("S12", [128, T], F32)
        TIDA = sb("TIDA", [128, T], F32)
        IDXF = sb("IDXF", [128, T], F32)
        IDXU = sb("IDXU", [128, T], U32)
        R12A = sb("R12A", [128, T], F32)
        TMIN = sb("TMIN", [128, T], F32)
        TMAX = sb("TMAX", [128, T], F32)
        MUG = sb("MUG", [128, T], F32)
        RGA = sb("RGA", [128, T], F32)
        S2A = sb("S2A", [128, T], F32)
        B2A = sb("B2A", [128, T], F32)
        INV = sb("INV", [128, T], F32)
        W32a = sb("W32a", [128, T], F32)
        W32b = sb("W32b", [128, T], F32)
        W32c = sb("W32c", [128, T], F32)


        OHT = sb("OHT", [64, T * 128], BF16)
        GBA = sb("GBA", [128, T * 64], F32)
        XSL = sb("XSL", [128, T * 32], F32)
        XNS = sb("XNS", [128, T * 32], F32)
        RED = sb("RED", [128, 64], F32)

        SEM = {n: st.enter_context(nc.semaphore(n)) for n in
               ["LD", "PD", "CC", "V", "A", "P", "G",
                "SP0", "SP1", "SP2", "SP3",
                "LX0", "LX1", "LX2", "LX3", "RL0", "RL1", "OS0", "OS1"]}
        cnt = {n: 0 for n in SEM}
        tick = {}

        def bump(inst, e, key=None):
            cnt[e] += 1
            inst.then_inc(SEM[e], 1)
            if key is not None:
                tick[key] = cnt[e]
            return cnt[e]

        def dbump(inst, e, key=None):
            cnt[e] += 16
            inst.then_inc(SEM[e], 16)
            if key is not None:
                tick[key] = cnt[e]
            return cnt[e]

        blk = st.enter_context(nc.Block())

        pst_free = [("V", 0)]

        if True:
            PS1 = [st.enter_context(nc.psum_tensor("PS1a", [128, 512], F32)),
                   st.enter_context(nc.psum_tensor("PS1b", [128, 512], F32))]
            PS2 = st.enter_context(nc.psum_tensor("PS2", [128, 2048], F32))
            PST = st.enter_context(nc.psum_tensor("PST", [128, 1024], BF16))
            PSS = st.enter_context(nc.psum_tensor("PSS", [128, 512], F32))
            PSSb = PSS.ap().bitcast(BF16)
            sb1 = sb
            DD = sb1("DD", [128, 16 * 128], BF16)
            DCOL = sb1("DCOL", [128, 16], F32)
            WSS = sb1("WSS", [128, 16 * 64], BF16)
            THB = sb1("THB", [128, D], BF16)
            SID = sb1("SID", [128, 128], BF16)
            SIDF = sb1("SIDF", [128, 128], F32)
            ONE1 = sb1("ONE1", [1, 128], BF16)
            BIAR = sb1("BIAR", [1, 64], BF16)
            TIOT = sb1("TIOT", [128, 64], F32)
            CLB = sb1("CLB", [128, 64], F32)
            IOT128 = sb1("IOT128", [128, 128], F32)
            PIDX = sb1("PIDX", [128, 1], F32)
            SGAM = sb1("SGAM", [64, 32], F32)
            SBET = sb1("SBET", [64, 32], F32)
            GBH = sb1("GBH", [64, 64], BF16)
            GBL = sb1("GBL", [64, 64], BF16)
            W2BF = sb1("W2BF", [64, 32], BF16)
            BI64 = sb1("BI64", [64, 1], BF16)
            XAB = [sb("XAB0", [128, D], BF16)]
            SIGB = [sb("SIGB0", [128, D], BF16)]
            XAT = sb1("XAT", [128, SS * D], BF16)
            STT = sb1("STT", [128, 8 * 512], BF16)
            SGT = [sb("SGT0", [128, D], BF16)]
            CS8 = sb1("CS8", [128, 8], F32)
            MX8 = sb1("MX8", [128, 8], F32)
            IX8 = sb1("IX8", [128, 8], U32)
            TSC = sb1("TSC", [128, 64], F32)
            TSM = sb1("TSM", [128, 64], F32)
            MX8b = sb1("MX8b", [128, 8], F32)
            IX8b = sb1("IX8b", [128, 8], U32)
            BCF = sb1("BCF", [128, 1], F32)
            OHB = sb1("OHB", [128, T * 64], BF16)

            @blk.sync
            def _(sy):
                dbump(sy.dma_start(SGAM[:], gamd.ap()), "LD")
                dbump(sy.dma_start(SBET[:], betd.ap()), "LD")
                dbump(sy.dma_start(VMM[0:64, 0:512], basd.ap()), "LD")
                dbump(sy.dma_start(VMM[0:64, 512:1024], slod.ap()), "LD")
                dbump(sy.dma_start(SOSC[:], oscd.ap()), "LD")
                dbump(sy.dma_start(DCOL[:], dcold.ap()), "LD", "ldconst")

            @blk.gpsimd
            def _(g):
                g.memset(EPSC[:], EPS)
                g.memset(ONE1[:], 1.0)
                g.memset(WSS[:], 0)
                _gi = g.iota(TIOT[:], [[1, 64]], channel_multiplier=0,
                       allow_small_or_imprecise_dtypes=True)
                _gi = g.iota(CLB[:], [[1, 8], [0, 8]], channel_multiplier=0,
                       allow_small_or_imprecise_dtypes=True)
                _gi = g.iota(TOKI[:], [[128, T]], channel_multiplier=1,
                       allow_small_or_imprecise_dtypes=True)
                _gi = g.iota(PIDX[:], [[0, 1]], channel_multiplier=1,
                       allow_small_or_imprecise_dtypes=True)
                _gi = g.iota(IOT128[:], [[1, 128]], channel_multiplier=0,
                       allow_small_or_imprecise_dtypes=True)
                bump(_gi, "G", "iotas")
                dbump(g.dma_start(
                    BT[:], bass.AP(btd, 0, [[NS, 128], [128 * NS, 16], [1, NS]])),
                    "PD")
                dbump(g.dma_start(
                    CT[:], bass.AP(ctd, 0, [[D, 128], [128 * D, 8], [1, D]])),
                    "PD", "wload")
                g.wait_ge(SEM["PD"], cnt["PD"])

            @blk.vector
            def _(v):
                v.wait_ge(SEM["G"], tick["iotas"])
                bump(v.tensor_scalar(SID[:], IOT128[:], PIDX[:], None,
                                     OP.is_equal), "V", "sid")
                v.wait_ge(SEM["V"], tick["sid"])
                bump(v.tensor_copy(SIDF[:], SID[:]), "V", "sidf")
                v.wait_ge(SEM["LD"], tick["ldconst"])
                for j in range(16):
                    _vi = v.tensor_scalar(DD[:, 128 * j:128 * (j + 1)], SID[:],
                                          DCOL[:, j:j + 1], None, OP.mult)
                bump(_vi, "V", "ddiag")
                bump(v.reciprocal(VMM[0:64, 1300:1332], SGAM[:]), "V", "recg")
                v.wait_ge(SEM["V"], tick["recg"])
                bump(v.tensor_tensor(VMM[0:64, 1332:1364], VMM[0:64, 1300:1332],
                                     SBET[:], OP.mult), "V", "th0")
                v.wait_ge(SEM["V"], tick["th0"])
                bump(v.tensor_scalar(VMM[0:64, 1332:1364], VMM[0:64, 1332:1364],
                                     -1.0, None, OP.mult), "V", "theta")
                v.wait_ge(SEM["LD"], tick["ldconst"])
                bump(v.tensor_reduce(VMM[0:64, 1364:1396],
                                     VMM[0:64, 512:1024].rearrange(
                                         "p (a b) -> p a b", a=32),
                                     AX.X, OP.add), "V", "ssum")

            @blk.gpsimd
            def _(g):
                g.wait_ge(SEM["V"], tick["theta"])
                dbump(g.dma_start(th_d.ap(), VMM[0:64, 1332:1364]), "PD", "thdma")
                g.wait_ge(SEM["PD"], cnt["PD"])
                dbump(g.dma_start(THB[:], bass.AP(th_d, 0, [[0, 128], [1, D]])),
                      "PD", "thbcast")

            @blk.scalar
            def _(a):
                a.wait_ge(SEM["V"], tick["ssum"])
                bump(a.sign(VMM[0:64, 1396:1428], VMM[0:64, 1364:1396]),
                     "A", "ssgn")
                bump(a.sign(VMM[0:64, 1428:1460], SGAM[:]), "A", "sgam")

            @blk.vector
            def _(v):
                v.wait_ge(SEM["A"], tick["sgam"])
                bump(v.tensor_tensor(VMM[0:64, 1460:1492], VMM[0:64, 1396:1428],
                                     VMM[0:64, 1428:1460], OP.mult), "V", "w0")
                v.wait_ge(SEM["V"], tick["w0"])
                bump(v.tensor_reduce(VMM[0:64, 1492:1493],
                                     VMM[0:64, 1460:1492].unsqueeze(1), AX.X,
                                     OP.add), "V", "bsum")
                v.wait_ge(SEM["V"], tick["bsum"])
                bump(v.tensor_scalar(BI64[:], VMM[0:64, 1492:1493], -1.0, None,
                                     OP.mult), "V", "bi64")
                bump(v.tensor_scalar(W2BF[:], VMM[0:64, 1460:1492], 2.0, None,
                                     OP.mult), "V", "w2bf")
                v.wait_ge(SEM["LD"], tick["ldconst"])
                v.tensor_copy(VMM[0:64, 1024:1056], SGAM[:])
                bump(v.tensor_copy(VMM[0:64, 1056:1088], SBET[:]), "V", "gb0")
                v.wait_ge(SEM["V"], tick["gb0"])
                bump(v.tensor_copy(GBH[:], VMM[0:64, 1024:1088]), "V", "gbh")
                v.wait_ge(SEM["V"], tick["gbh"])
                bump(v.tensor_copy(VMM[0:64, 1088:1152], GBH[:]), "V", "gbb")
                v.wait_ge(SEM["V"], tick["gbb"])
                bump(v.tensor_tensor(VMM[0:64, 1152:1216], VMM[0:64, 1024:1088],
                                     VMM[0:64, 1088:1152], OP.subtract), "V", "gbl0")
                v.wait_ge(SEM["V"], tick["gbl0"])
                bump(v.tensor_copy(GBL[:], VMM[0:64, 1152:1216]), "V", "gbl")
                v.wait_ge(SEM["PD"], tick["thdma"])
                bump(v.tensor_copy(TBH[:], VMM[0:64, 0:1024]), "V", "tbh")
                v.wait_ge(SEM["V"], tick["tbh"])
                bump(v.tensor_copy(VMM[0:64, 1024:2048], TBH[:]), "V", "tbb")
                v.wait_ge(SEM["V"], tick["tbb"])
                bump(v.tensor_tensor(VMM[0:64, 1024:2048], VMM[0:64, 0:1024],
                                     VMM[0:64, 1024:2048], OP.subtract), "V", "tbl0")
                v.wait_ge(SEM["V"], tick["tbl0"])
                bump(v.tensor_copy(TBL[:], VMM[0:64, 1024:2048]), "V", "tbl")

            @blk.gpsimd
            def _(g):
                g.wait_ge(SEM["V"], tick["w2bf"])
                g.wait_ge(SEM["PD"], cnt["PD"])
                dbump(g.dma_start(w2_d.ap(), W2BF[:]), "PD")
                g.wait_ge(SEM["PD"], cnt["PD"])
                for b in range(4):
                    dbump(g.dma_start(
                        bass.AP(WSS, 32 * b * (16 * 64) + b, [[16 * 64, 32],
                                                              [68, 16]]),
                        bass.AP(w2_d, 32 * b, [[1, 32], [128, 16]]),
                        allow_slow_non_contiguous=True), "PD")
                g.wait_ge(SEM["V"], tick["bi64"])
                dbump(g.dma_start(bia_d.ap(), BI64[:]), "PD")
                g.wait_ge(SEM["PD"], cnt["PD"])
                dbump(g.dma_start(BIAR[:], bia_d.ap()), "PD")
                bump(g.wait_ge(SEM["PD"], cnt["PD"]), "G", "winit")



            a_defer = []

            def defer_a(keys, fn):
                for key in keys:
                    cnt["A"] += 1
                    tick[key] = cnt["A"]
                a_defer.append(fn)

            def flush_a():
                fns = list(a_defer)
                a_defer.clear()

                @blk.scalar
                def _(a):
                    for fn in fns:
                        fn(a)

            def p1_load(k):
                @blk.sync
                def _(sy, k=k):
                    if k >= 4:
                        sy.wait_ge(SEM["V"], tick[("sig", k - 4)])
                        sy.wait_ge(SEM[f"SP{k % 4}"], tick[("sp", k - 4)])
                    dbump(sy.dma_start(
                        X0B[k % 4][:],
                        bass.AP(x0d, k * 128 * D, [[D, 128], [1, D]])),
                        f"LX{k % 4}", ("x0", k))

            def p1_pre(k):
                @blk.vector
                def _(v, k=k):
                    v.wait_ge(SEM[f"LX{k % 4}"], tick[("x0", k)])
                    bump(v.tensor_reduce(SM0[:, k:k + 1], X0B[k % 4][:],
                                         AX.X, OP.add), "V", ("sum0", k))
                    v.wait_ge(SEM["V"], tick[("sum0", k)])
                    bump(v.tensor_scalar(MU0[:, k:k + 1], SM0[:, k:k + 1],
                                         1.0 / D, None, OP.mult),
                         "V", ("mu0", k))

                def _sq0(a, k=k):
                    a.wait_ge(SEM[f"LX{k % 4}"], tick[("x0", k)])
                    if k == 0:
                        a.wait_ge(SEM["V"], tick["tbl"])
                        a.wait_ge(SEM["PD"], tick["thdma"])
                    if k >= 4:
                        a.wait_ge(SEM["V"], tick[("x1", 4 * (k // 4) - 1)])
                    a.activation(VMM[:], X0B[k % 4][:], AF.Square,
                                 accum_out=SQ0[:, k:k + 1]
                                 ).then_inc(SEM["A"], 1)
                defer_a([("sq0", k)], _sq0)

                @blk.vector
                def _(v, k=k):
                    v.wait_ge(SEM["V"], tick[("mu0", k)])
                    v.wait_ge(SEM["A"], tick[("sq0", k)])
                    bump(v.tensor_tensor(W32a[:, k:k + 1], MU0[:, k:k + 1],
                                         MU0[:, k:k + 1], OP.mult),
                         "V", ("m20", k))
                    v.wait_ge(SEM["V"], tick[("m20", k)])
                    bump(v.scalar_tensor_tensor(V0C[:, k:k + 1],
                                                SQ0[:, k:k + 1], 1.0 / D,
                                                W32a[:, k:k + 1], OP.mult,
                                                OP.subtract), "V", ("v0", k))

                def _ln0(a, k=k):
                    a.wait_ge(SEM["V"], tick[("v0", k)])
                    a.activation(V0C[:, k:k + 1], V0C[:, k:k + 1], AF.Sqrt,
                                 bias=EPSC[:]).then_inc(SEM["A"], 1)
                defer_a([("ln0", k)], _ln0)

                @blk.vector
                def _(v, k=k):
                    v.wait_ge(SEM["A"], tick[("ln0", k)])
                    bump(v.reciprocal(R0A[:, k:k + 1], V0C[:, k:k + 1]),
                         "V", ("r0", k))
                    v.wait_ge(SEM["V"], tick[("r0", k)])
                    bump(v.scalar_tensor_tensor(B0A[:, k:k + 1],
                                                MU0[:, k:k + 1], -1.0,
                                                R0A[:, k:k + 1], OP.mult,
                                                OP.mult), "V", ("b0", k))

            def p1_xa(k):
                @blk.vector
                def _(v, k=k):
                    v.wait_ge(SEM["V"], tick[("b0", k)])
                    if k >= 1:
                        v.wait_ge(SEM["P"], tick[("xat", k - 1)])
                    bump(v.tensor_scalar(XAB[0][:], X0B[k % 4][:],
                                         R0A[:, k:k + 1], B0A[:, k:k + 1],
                                         OP.mult, OP.add), "V", ("xa", k))

            def p1_txa(k):
                c = k % SS
                for grp in range(2):
                    @blk.tensor
                    def _(t, grp=grp, c=c, k=k):
                        if grp == 0:
                            t.wait_ge(SEM["V"], tick[("xa", k)])
                            t.wait_ge(SEM["V"], tick["sid"])
                        e, tk_ = pst_free[0]
                        t.wait_ge(SEM[e], tk_)
                        for j8 in range(8):
                            j = grp * 8 + j8
                            _mi = t.transpose(PST[:, 128 * j8:128 * (j8 + 1)],
                                        XAB[0][:, 128 * j:128 * (j + 1)],
                                        SID[:])
                        bump(_mi, "P", ("trx", k, grp))

                    def _ev(a, grp=grp, c=c, k=k):
                        a.wait_ge(SEM["P"], tick[("trx", k, grp)])
                        a.copy(
                            XAT[:, D * c + 1024 * grp:
                                D * c + 1024 * (grp + 1)],
                            PST[:]).then_inc(SEM["A"], 1)
                    defer_a([("evT", k, grp)], _ev)
                    pst_free[0] = ("A", tick[("evT", k, grp)])
                tick[("xat", k)] = tick[("trx", k, 1)]

            def p1_mm1(s):
                xat3 = XAT[:].rearrange("p (c f) -> p c f", c=SS)
                for i in range(8):
                    @blk.tensor
                    def _(t, i=i, s=s):
                        if i == 0:
                            t.wait_ge(SEM["A"], tick[("evT", s * SS + SS - 1, 1)])
                            if s == 0:
                                t.wait_ge(SEM["PD"], tick["wload"])
                        prev = (s, i - 2) if i >= 2 else \
                            ((s - 1, i + 6) if s > 0 else None)
                        if prev is not None:
                            t.wait_ge(SEM["A"], tick[("tanh",) + prev])
                        for j in range(16):
                            _mi = t.matmul(PS1[i % 2][:],
                                     BT[:, NS * j + 128 * i:
                                        NS * j + 128 * (i + 1)],
                                     xat3[:, :, 128 * j:128 * (j + 1)],
                                     start=(j == 0), stop=(j == 15))
                        bump(_mi, "P", ("mm1", s, i))

                    def _th(a, i=i, s=s):
                        a.wait_ge(SEM["P"], tick[("mm1", s, i)])
                        a.activation(STT[:, 512 * i:512 * (i + 1)],
                                     PS1[i % 2][:], AF.Tanh
                                     ).then_inc(SEM["A"], 1)
                    defer_a([("tanh", s, i)], _th)

            def p1_mm2(k):
                s, c = k // SS, k % SS

                for hb in range(2):
                    @blk.tensor
                    def _(t, k=k, s=s, c=c, hb=hb):
                        if hb == 0:
                            if k >= 1:
                                t.wait_ge(SEM["V"], tick[("aev1", k - 1)])
                            t.wait_ge(SEM["A"], tick[("tanh", s, 7)])
                            if k == 0:
                                t.wait_ge(SEM["V"], tick["ddiag"])
                        elif k >= 1:
                            t.wait_ge(SEM["V"], tick[("aev2", k - 1)])
                        for j in range(8 * hb, 8 * hb + 8):
                            _mi = t.matmul(PS2[:, 128 * j:128 * (j + 1)],
                                     XAT[:, D * c + 128 * j:
                                         D * c + 128 * (j + 1)],
                                     DD[:, 128 * j:128 * (j + 1)],
                                     start=(j % 4 == 0), stop=False,
                                     skip_group_check=True)
                        for i in range(8):
                            for h in (2 * hb, 2 * hb + 1):
                                _mi = t.matmul(
                                    PS2[:, 512 * h:512 * (h + 1)],
                                    STT[:, 512 * i + 128 * c:
                                        512 * i + 128 * (c + 1)],
                                    CT[:, D * i + 512 * h:
                                        D * i + 512 * (h + 1)],
                                    start=False, stop=(i == 7),
                                    skip_group_check=True)
                        bump(_mi, "P", ("mm2h", k, hb))

            def p1_post(k):
                @blk.vector
                def _(v, k=k):
                    v.wait_ge(SEM["P"], tick[("mm2h", k, 0)])
                    if k >= 2:
                        v.wait_ge(SEM["V"], tick[("sig", k - 2)])
                    bump(v.tensor_scalar(ATTNB[k % 2][:, 0:1024],
                                         PS2[:, 0:1024], 0.0, 0.0,
                                         OP.add, OP.add,
                                         accum_out=SMA[:, k:k + 1]),
                         "V", ("aev1", k))
                    v.wait_ge(SEM["P"], tick[("mm2h", k, 1)])
                    bump(v.tensor_scalar(ATTNB[k % 2][:, 1024:2048],
                                         PS2[:, 1024:2048], 0.0, 0.0,
                                         OP.add, OP.add,
                                         accum_out=W32c[:, k:k + 1]),
                         "V", ("aev2", k))
                    v.wait_ge(SEM["V"], tick[("aev2", k)])
                    bump(v.scalar_tensor_tensor(MUA[:, k:k + 1],
                                                SMA[:, k:k + 1], 1.0,
                                                W32c[:, k:k + 1], OP.mult,
                                                OP.add), "V", ("sma", k))
                    v.wait_ge(SEM["V"], tick[("sma", k)])
                    bump(v.tensor_scalar(MUA[:, k:k + 1], MUA[:, k:k + 1],
                                         1.0 / D, None, OP.mult),
                         "V", ("mua", k))

                def _sqa(a, k=k):
                    a.wait_ge(SEM["V"], tick[("aev2", k)])
                    a.activation(VMM[:], ATTNB[k % 2][:], AF.Square,
                                 accum_out=SQA[:, k:k + 1]
                                 ).then_inc(SEM["A"], 1)
                defer_a([("sqa", k)], _sqa)

                @blk.vector
                def _(v, k=k):
                    v.wait_ge(SEM["V"], tick[("mua", k)])
                    v.wait_ge(SEM["A"], tick[("sqa", k)])
                    bump(v.tensor_tensor(W32a[:, k:k + 1], MUA[:, k:k + 1],
                                         MUA[:, k:k + 1], OP.mult),
                         "V", ("m2a", k))
                    v.wait_ge(SEM["V"], tick[("m2a", k)])
                    bump(v.scalar_tensor_tensor(VAC[:, k:k + 1],
                                                SQA[:, k:k + 1], 1.0 / D,
                                                W32a[:, k:k + 1], OP.mult,
                                                OP.subtract), "V", ("va", k))

                def _lna(a, k=k):
                    a.wait_ge(SEM["V"], tick[("va", k)])
                    a.activation(VAC[:, k:k + 1], VAC[:, k:k + 1], AF.Sqrt,
                                 bias=EPSC[:]).then_inc(SEM["A"], 1)
                defer_a([("lna", k)], _lna)

                @blk.vector
                def _(v, k=k):
                    v.wait_ge(SEM["A"], tick[("lna", k)])
                    bump(v.reciprocal(RAA[:, k:k + 1], VAC[:, k:k + 1]),
                         "V", ("ba", k))

                @blk.vector
                def _(v, k=k):
                    v.wait_ge(SEM["V"], tick[("ba", k)])
                    v.wait_ge(SEM["A"], tick[("sqa", k)])
                    if k == 0:
                        v.wait_ge(SEM["PD"], tick["thdma"])
                        v.wait_ge(SEM["V"], tick["tbl"])
                    bump(v.tensor_scalar(VMM[:], ATTNB[k % 2][:],
                                         MUA[:, k:k + 1], None, OP.subtract),
                         "V", ("tp", k))
                    v.wait_ge(SEM["V"], tick[("tp", k)])
                    bump(v.scalar_tensor_tensor(X0B[k % 4][:], VMM[:],
                                                RAA[:, k:k + 1],
                                                X0B[k % 4][:], OP.mult, OP.add,
                                                accum_out=SM1[:, k:k + 1]),
                         "V", ("x1", k))

                @blk.sync
                def _(sy, k=k):
                    sy.wait_ge(SEM["V"], tick[("x1", k)])
                    dbump(sy.dma_start(
                        bass.AP(x1sp, k * 128 * D, [[D, 128], [1, D]]),
                        X0B[k % 4][:]), f"SP{k % 4}", ("sp", k))

                def _sq1(a, k=k):
                    a.wait_ge(SEM["V"], tick[("x1", k)])
                    a.activation(VMM[:], X0B[k % 4][:], AF.Square,
                                 accum_out=SQ1[:, k:k + 1]
                                 ).then_inc(SEM["A"], 1)
                defer_a([("sq1", k)], _sq1)

                @blk.vector
                def _(v, k=k):
                    v.wait_ge(SEM["V"], tick[("x1", k)])
                    bump(v.tensor_scalar(MU1[:, k:k + 1], SM1[:, k:k + 1],
                                         1.0 / D, None, OP.mult),
                         "V", ("mu1", k))
                    v.wait_ge(SEM["V"], tick[("mu1", k)])
                    v.wait_ge(SEM["A"], tick[("sq1", k)])
                    bump(v.tensor_tensor(W32a[:, k:k + 1], MU1[:, k:k + 1],
                                         MU1[:, k:k + 1], OP.mult),
                         "V", ("m21", k))
                    v.wait_ge(SEM["V"], tick[("m21", k)])
                    bump(v.scalar_tensor_tensor(V1A[:, k:k + 1],
                                                SQ1[:, k:k + 1], 1.0 / D,
                                                W32a[:, k:k + 1], OP.mult,
                                                OP.subtract), "V", ("v1", k))

                def _ln1(a, k=k):
                    a.wait_ge(SEM["V"], tick[("v1", k)])
                    a.activation(W32b[:, k:k + 1], V1A[:, k:k + 1],
                                 AF.Sqrt, bias=EPSC[:]).then_inc(SEM["A"], 1)
                defer_a([("ln1", k)], _ln1)

                @blk.vector
                def _(v, k=k):
                    v.wait_ge(SEM["A"], tick[("ln1", k)])
                    bump(v.reciprocal(R1A[:, k:k + 1], W32b[:, k:k + 1]),
                         "V", ("r1", k))
                    bump(v.tensor_scalar(V2C[:, k:k + 1], V1A[:, k:k + 1],
                                         1.0 + EPS, EPS * EPS, OP.mult,
                                         OP.add), "V", ("v2", k))

                def _ln2(a, k=k):
                    a.wait_ge(SEM["V"], tick[("v2", k)])
                    a.activation(S12[:, k:k + 1], V2C[:, k:k + 1],
                                 AF.Sqrt).then_inc(SEM["A"], 1)
                defer_a([("ln2", k)], _ln2)

                @blk.vector
                def _(v, k=k):
                    v.wait_ge(SEM["A"], tick[("ln2", k)])
                    bump(v.reciprocal(R12A[:, k:k + 1], S12[:, k:k + 1]),
                         "V", ("s12", k))
                    v.wait_ge(SEM["V"], tick[("s12", k)])
                    if k == 0:
                        v.wait_ge(SEM["PD"], tick["thbcast"])
                    bump(v.tensor_scalar(ATTNB[k % 2][:], THB[:],
                                         S12[:, k:k + 1], MU1[:, k:k + 1],
                                         OP.mult, OP.add), "V", ("thr", k))
                    v.wait_ge(SEM["V"], tick[("thr", k)])
                    if k >= 1:
                        v.wait_ge(SEM["P"], tick[("sgt", k - 1)])
                    bump(v.tensor_tensor(SIGB[0][:], X0B[k % 4][:],
                                         ATTNB[k % 2][:], OP.is_gt),
                         "V", ("sig", k))

            def p1_tsg(k):
                for grp in range(2):
                    @blk.tensor
                    def _(t, grp=grp, k=k):
                        if grp == 0:
                            t.wait_ge(SEM["V"], tick[("sig", k)])
                        e, tk_ = pst_free[0]
                        t.wait_ge(SEM[e], tk_)
                        for j8 in range(8):
                            j = grp * 8 + j8
                            _mi = t.transpose(PST[:, 128 * j8:128 * (j8 + 1)],
                                        SIGB[0][:, 128 * j:128 * (j + 1)],
                                        SID[:])
                        bump(_mi, "P", ("trs", k, grp))

                    @blk.vector
                    def _(v, grp=grp, k=k):
                        v.wait_ge(SEM["P"], tick[("trs", k, grp)])
                        bump(v.tensor_copy(
                            SGT[0][:, 1024 * grp:1024 * (grp + 1)], PST[:]),
                            "V", ("evS", k, grp))
                        pst_free[0] = ("V", tick[("evS", k, grp)])
                tick[("sgt", k)] = tick[("trs", k, 1)]

            def p1_scores(k):
                @blk.tensor
                def _(t, k=k):
                    t.wait_ge(SEM["V"], tick[("evS", k, 1)])
                    if k == 0:
                        t.wait_ge(SEM["G"], tick["winit"])
                    else:
                        t.wait_ge(SEM["V"], tick[("gbev", k - 1)])
                    for j in range(16):
                        _mi = t.matmul(PSS[:, 0:64],
                                 SGT[0][:, 128 * j:128 * (j + 1)],
                                 WSS[:, 64 * j:64 * (j + 1)],
                                 start=(j == 0), stop=False,
                                 skip_group_check=True)
                    _mi = t.matmul(PSS[:, 0:64], ONE1[:], BIAR[:], start=False,
                             stop=True, skip_group_check=True)
                    bump(_mi, "P", ("sc", k))

                @blk.vector
                def _(v, k=k):
                    v.wait_ge(SEM["P"], tick[("sc", k)])
                    bump(v.tensor_reduce(
                        CS8[:], PSS[:, 0:64].rearrange("p (a b) -> p a b", a=8),
                        AX.X, OP.add), "V", ("cs8", k))
                    bump(v.tensor_copy(TSC[:], PSS[:, 0:64]), "V", ("tsev", k))
                    v.wait_ge(SEM["V"], tick[("cs8", k)])
                    bump(v.max(MX8[:], CS8[:]), "V", ("cmx", k))
                    v.wait_ge(SEM["V"], tick[("cmx", k)])
                    bump(v.max_index(IX8[:], MX8[:], CS8[:]), "V", ("cix", k))
                    v.wait_ge(SEM["V"], tick[("cix", k)])
                    bump(v.tensor_copy(BCF[:], IX8[:, 0:1]), "V", ("bcf", k))
                    v.wait_ge(SEM["V"], tick[("bcf", k)])
                    bump(v.tensor_scalar(TSM[:], CLB[:], BCF[:], None,
                                         OP.is_equal), "V", ("m64", k))
                    v.wait_ge(SEM["V"], tick[("m64", k)])
                    bump(v.scalar_tensor_tensor(TSM[:], TSC[:], BIG, TSM[:],
                                                OP.add, OP.mult),
                         "V", ("msk", k))
                    v.wait_ge(SEM["V"], tick[("msk", k)])
                    bump(v.max(MX8b[:], TSM[:]), "V", ("tmx", k))
                    v.wait_ge(SEM["V"], tick[("tmx", k)])
                    bump(v.max_index(IX8b[:], MX8b[:], TSM[:]),
                         "V", ("tix", k))
                    v.wait_ge(SEM["V"], tick[("tix", k)])
                    bump(v.tensor_copy(TIDA[:, k:k + 1], IX8b[:, 0:1]),
                         "V", ("tid", k))
                    v.wait_ge(SEM["V"], tick[("tid", k)])
                    bump(v.tensor_scalar(OHB[:, 64 * k:64 * (k + 1)], TIOT[:],
                                         TIDA[:, k:k + 1], None, OP.is_equal),
                         "V", ("oh", k))
                    bump(v.scalar_tensor_tensor(IDXF[:, k:k + 1],
                                                TOKI[:, k:k + 1], 64.0,
                                                TIDA[:, k:k + 1], OP.mult,
                                                OP.add), "V", ("idxf", k))
                    v.wait_ge(SEM["V"], tick[("idxf", k)])
                    bump(v.tensor_copy(IDXU[:, k:k + 1], IDXF[:, k:k + 1]),
                         "V", ("idxu", k))

                @blk.tensor
                def _(t, k=k):
                    t.wait_ge(SEM["V"], tick[("oh", k)])
                    t.wait_ge(SEM["V"], tick[("tsev", k)])
                    _mi = t.transpose(PSSb[0:64, 512:640],
                                OHB[:, 64 * k:64 * (k + 1)], SID[:])
                    bump(_mi, "P", ("oht", k))

                @blk.vector
                def _(v, k=k):
                    v.wait_ge(SEM["P"], tick[("oht", k)])
                    bump(v.tensor_copy(OHT[:, 128 * k:128 * (k + 1)],
                                       PSSb[0:64, 512:640]), "V", ("ohtev", k))

                @blk.tensor
                def _(t, k=k):
                    t.wait_ge(SEM["V"], tick[("ohtev", k)])
                    if k == 0:
                        t.wait_ge(SEM["V"], tick["gbl"])
                    _mi = t.matmul(PSS[:, 64:128], OHT[:, 128 * k:128 * (k + 1)],
                             GBH[:], start=True, stop=False,
                             skip_group_check=True)
                    _mi = t.matmul(PSS[:, 64:128], OHT[:, 128 * k:128 * (k + 1)],
                             GBL[:], start=False, stop=True,
                             skip_group_check=True)
                    bump(_mi, "P", ("gbmm", k))

                @blk.vector
                def _(v, k=k):
                    v.wait_ge(SEM["P"], tick[("gbmm", k)])
                    bump(v.tensor_copy(GBA[:, 64 * k:64 * (k + 1)],
                                       PSS[:, 64:128]), "V", ("gbev", k))


            for s in range(NSUP):
                for c0 in (0, 2):
                    for c in (c0, c0 + 1):
                        k = s * SS + c
                        p1_load(k)
                        p1_pre(k)
                    for c in (c0, c0 + 1):
                        k = s * SS + c
                        p1_xa(k)
                        p1_txa(k)
                p1_mm1(s)
                for c in range(SS):
                    k = s * SS + c
                    p1_mm2(k)
                    p1_post(k)
                    p1_tsg(k)
                    p1_scores(k)
                flush_a()

            @blk.gpsimd
            def _(g):
                for q in range(4):
                    g.wait_ge(SEM[f"SP{q}"], tick[("sp", T - 4 + q)])
                g.wait_ge(SEM["V"], tick[("idxu", T - 1)])
                for k in range(T):
                    dbump(g.indirect_dma_start(
                        out=XSL[:, 32 * k:32 * (k + 1)], out_offset=None,
                        in_=x1sp.ap(),
                        in_offset=bass.IndirectOffsetOnAxis(
                            ap=IDXU[:, k:k + 1], axis=0)), "PD", ("xg", k))

            @blk.vector
            def _(v):
                v.wait_ge(SEM["PD"], tick[("xg", T - 1)])
                bump(v.tensor_tensor(XNS[:], XSL[:],
                                MU1[:].to_broadcast([128, T, 32]), OP.subtract), "V", "h1")
                v.wait_ge(SEM["V"], tick["h1"])
                bump(v.tensor_tensor(XNS[:], XNS[:],
                                R12A[:].to_broadcast([128, T, 32]), OP.mult), "V", "h3")
                v.wait_ge(SEM["V"], tick["h3"])
                bump(v.tensor_tensor(XNS[:], XNS[:],
                                bass.AP(GBA, 0, [[T * 64, 128], [64, T],
                                                 [1, 32]]), OP.mult), "V", "h4")
                v.wait_ge(SEM["V"], tick["h4"])
                bump(v.tensor_tensor(XNS[:], XNS[:],
                                bass.AP(GBA, 32, [[T * 64, 128], [64, T],
                                                  [1, 32]]), OP.add), "V", "xns")
                v.wait_ge(SEM["V"], tick["xns"])
                v.tensor_reduce(TMIN[:],
                                XNS[:].rearrange("p (a b) -> p a b", a=T),
                                AX.X, OP.min)
                bump(v.tensor_reduce(TMAX[:],
                                     XNS[:].rearrange("p (a b) -> p a b", a=T),
                                     AX.X, OP.max), "V", "tmm")
                v.wait_ge(SEM["V"], tick["tmm"])
                v.wait_ge(SEM["A"], tick[("sq1", T - 1)])
                bump(v.tensor_scalar(
                    bass.AP(VMM, 0, [[2048, 128], [32, 64], [1, 32]]),
                    bass.AP(OHB, 0, [[T * 64, 128], [1, 64], [64, T]]),
                    -1e38, 1e38, OP.mult, OP.add), "V", "ms1")
                v.wait_ge(SEM["V"], tick["ms1"])
                bump(v.tensor_tensor(
                    bass.AP(VMM, 0, [[2048, 128], [32, 64], [1, 32]]),
                    bass.AP(VMM, 0, [[2048, 128], [32, 64], [1, 32]]),
                    bass.AP(TMIN, 0, [[T, 128], [0, 64], [1, T]]),
                    OP.add), "V", "vm1")
                v.wait_ge(SEM["V"], tick["vm1"])
                bump(v.tensor_reduce(
                    RED[:], VMM[:].rearrange("p (a b) -> p a b", a=64),
                    AX.X, OP.min), "V", "redmin")

            @blk.tensor
            def _(t):
                t.wait_ge(SEM["V"], tick["redmin"])
                t.wait_ge(SEM["V"], tick["sidf"])
                _mi = t.transpose(PSS[0:64, 128:256], RED[:], SIDF[:])
                bump(_mi, "P", "trmin")

            @blk.vector
            def _(v):
                v.wait_ge(SEM["P"], tick["trmin"])
                bump(v.tensor_reduce(PMM[:, 0:1],
                                     PSS[0:64, 128:256], AX.X, OP.min),
                     "V", "pmin")
                v.wait_ge(SEM["V"], tick["pmin"])
                bump(v.tensor_scalar(PMM[:, 0:1], PMM[:, 0:1], -1.0, None,
                                     OP.mult), "V", "pminneg")
                bump(v.tensor_scalar(
                    bass.AP(VMM, 0, [[2048, 128], [32, 64], [1, 32]]),
                    bass.AP(OHB, 0, [[T * 64, 128], [1, 64], [64, T]]),
                    1e38, -1e38, OP.mult, OP.add), "V", "ms2")
                v.wait_ge(SEM["V"], tick["ms2"])
                bump(v.tensor_tensor(
                    bass.AP(VMM, 0, [[2048, 128], [32, 64], [1, 32]]),
                    bass.AP(VMM, 0, [[2048, 128], [32, 64], [1, 32]]),
                    bass.AP(TMAX, 0, [[T, 128], [0, 64], [1, T]]),
                    OP.add), "V", "vx1")
                v.wait_ge(SEM["V"], tick["vx1"])
                bump(v.tensor_reduce(
                    RED[:], VMM[:].rearrange("p (a b) -> p a b", a=64),
                    AX.X, OP.max), "V", "redmax")

            @blk.tensor
            def _(t):
                t.wait_ge(SEM["V"], tick["redmax"])
                _mi = t.transpose(PSS[0:64, 256:384], RED[:], SIDF[:])
                bump(_mi, "P", "trmax")

            @blk.vector
            def _(v):
                v.wait_ge(SEM["P"], tick["trmax"])
                bump(v.tensor_reduce(PMM[:, 1:2], PSS[0:64, 256:384], AX.X,
                                     OP.max), "V", "pmax")

        @blk.gpsimd
        def _(g):
            g.wait_ge(SEM["V"], tick["pmax"])
            g.wait_ge(SEM["V"], tick["pminneg"])
            dbump(g.dma_start(bass.AP(cc_in, 0, [[2, 64], [1, 2]]), PMM[:]),
                  "PD")
            g.wait_ge(SEM["PD"], cnt["PD"])
            g.collective_compute(
                "AllReduce", OP.max, replica_groups=[list(range(NCORES))],
                ins=[cc_in.ap().opt()], outs=[cc_out.ap().opt()],
            ).then_inc(SEM["CC"], 1)
            cnt["CC"] += 1
            g.wait_ge(SEM["CC"], cnt["CC"])
            dbump(g.dma_start(GMM[:], bass.AP(cc_out, 0, [[2, 64], [1, 2]])),
                  "PD", "ccb")
            bump(g.wait_ge(SEM["PD"], cnt["PD"]), "G", "ccdone")

        if True:
            PSS2 = PSS
            PSP = [PS2.ap()[:, 0:1024], PS2.ap()[:, 1024:2048]]
            XATF = XAT.ap().bitcast(F32)
            XATB = XAT.ap()
            STTF = STT.ap().bitcast(F32)
            UAL = XATF[:, 0:1024]
            GIF = XATF[:, 1024:2048]
            DPP = XATF[:, 2048:3072]
            GIOT = XATB[:, 6656:7168]
            GMH = XATB[0:64, 7168:7171]
            GML = XATB[0:64, 7200:7203]
            MNX = STTF[:, 0:96]
            OHG = VMM[:, 1024:1536]
            GIU = bass.AP(VMM, 1536, [[T * 64, 128], [1, 512]]).bitcast(I32) \
                if False else None
            GIU = bass.AP(VMM, 1024, [[T * 64, 128], [1, 1024]]).bitcast(I32)
            XLC = UAL
            BSA = GBA

            @blk.gpsimd
            def _(g):
                _gi = g.iota(GIOT, [[0, 32], [1, 16]], channel_multiplier=0,
                       allow_small_or_imprecise_dtypes=True)
                bump(_gi, "G", "giot")

            @blk.vector
            def _(v):
                v.wait_ge(SEM["G"], tick["ccdone"])
                v.tensor_scalar(VMM[0:64, 0:1], GMM[:, 0:1], -1.0, None, OP.mult)
                v.tensor_copy(VMM[0:64, 1:2], GMM[:, 1:2])
                bump(v.tensor_copy(VMM[0:64, 2:3], SOSC[:]), "V", "gm0")
                v.wait_ge(SEM["V"], tick["gm0"])
                bump(v.tensor_copy(GMH, VMM[0:64, 0:3]), "V", "gmh")
                v.wait_ge(SEM["V"], tick["gmh"])
                bump(v.tensor_copy(VMM[0:64, 3:6], GMH), "V", "gmb")
                v.wait_ge(SEM["V"], tick["gmb"])
                bump(v.tensor_tensor(VMM[0:64, 6:9], VMM[0:64, 0:3], VMM[0:64, 3:6],
                                     OP.subtract), "V", "gml0")
                v.wait_ge(SEM["V"], tick["gml0"])
                bump(v.tensor_copy(GML, VMM[0:64, 6:9]), "V", "gml")

            for k in range(T):
                @blk.tensor
                def _(t, k=k):
                    if k == 0:
                        t.wait_ge(SEM["V"], tick["gml"])
                    _mi = t.matmul(PSS2[:, 3 * k:3 * (k + 1)],
                             OHT[:, 128 * k:128 * (k + 1)], GMH,
                             start=(k == 0), stop=False, skip_group_check=True)
                    _mi = t.matmul(PSS2[:, 3 * k:3 * (k + 1)],
                             OHT[:, 128 * k:128 * (k + 1)], GML,
                             start=False, stop=True, skip_group_check=True)
                    if k == T - 1:
                        bump(_mi, "P", "mnmx")

            @blk.vector
            def _(v):
                v.wait_ge(SEM["P"], tick["mnmx"])
                bump(v.tensor_copy(MNX, PSS2[:, 0:96]), "V", "mnxev")
                v.wait_ge(SEM["V"], tick["mnxev"])
                m3 = MNX.rearrange("p (a b) -> p a b", a=T)
                mnv = m3[:, :, 0:1]
                mxv = m3[:, :, 1:2]
                bump(v.tensor_tensor(W32a[:], mxv, mnv, OP.subtract),
                     "V", "den0")
                v.wait_ge(SEM["V"], tick["den0"])
                bump(v.tensor_scalar(W32a[:], W32a[:], 1e-8, None, OP.add),
                     "V", "den1")
                v.wait_ge(SEM["V"], tick["den1"])
                bump(v.reciprocal(INV[:], W32a[:]), "V", "inv")
                v.wait_ge(SEM["V"], tick["inv"])
                bump(v.tensor_tensor(UAL, XNS[:],
                                m3[:, :, 0:1].to_broadcast([128, T, 1, 32]),
                                OP.subtract), "V", "u0")
                v.wait_ge(SEM["V"], tick["u0"])
                bump(v.tensor_tensor(UAL, UAL,
                                INV[:].to_broadcast([128, T, 32]), OP.mult), "V", "u1")
                v.wait_ge(SEM["V"], tick["u1"])
                bump(v.tensor_scalar(UAL, UAL, 0.0, 1.0 - 1e-6, OP.max,
                                OP.min), "V", "u2")
                v.wait_ge(SEM["V"], tick["u2"])
                bump(v.tensor_scalar(GIF, UAL, 16.0, None, OP.mult), "V", "g16")
                v.wait_ge(SEM["V"], tick["g16"])
                bump(v.tensor_copy(GIU, GIF), "V", "giu")
                v.wait_ge(SEM["V"], tick["giu"])
                bump(v.tensor_copy(GIF, GIU), "V", "gif")
                v.wait_ge(SEM["V"], tick["gif"])
                bump(v.scalar_tensor_tensor(XLC[:], UAL, 16.0, GIF, OP.mult,
                                       OP.subtract), "V", "xlc")

            for k in range(T):
                @blk.tensor
                def _(t, k=k):
                    if k == 0:
                        t.wait_ge(SEM["V"], tick["tbl"])
                        t.wait_ge(SEM["V"], tick["xlc"])
                    if k >= 2:
                        t.wait_ge(SEM["V"], tick[("bs", k - 2)])
                    for h in range(2):
                        _mi = t.matmul(PSP[k % 2][:, 512 * h:512 * (h + 1)],
                                 OHT[:, 128 * k:128 * (k + 1)],
                                 TBH[:, 512 * h:512 * (h + 1)],
                                 start=True, stop=False, skip_group_check=True)
                        _mi = t.matmul(PSP[k % 2][:, 512 * h:512 * (h + 1)],
                                 OHT[:, 128 * k:128 * (k + 1)],
                                 TBL[:, 512 * h:512 * (h + 1)],
                                 start=False, stop=True, skip_group_check=True)
                    bump(_mi, "P", ("pg", k))

                @blk.vector
                def _(v, k=k):
                    v.wait_ge(SEM["P"], tick[("pg", k)])
                    if k == 0:
                        v.wait_ge(SEM["G"], tick["giot"])
                    bump(v.tensor_tensor(
                        OHG, GIF[:, 32 * k:32 * (k + 1)]
                        .to_broadcast([128, 32, 16]), GIOT, OP.is_equal), "V", ("ohg", k))
                    v.wait_ge(SEM["V"], tick[("ohg", k)])
                    bump(v.tensor_tensor(VMM[:, 0:1024], bass.AP(
                        VMM, 1024, [[T * 64, 128], [0, 2], [1, 512]]),
                        PSP[k % 2], OP.mult), "V", ("prod", k))
                    v.wait_ge(SEM["V"], tick[("prod", k)])
                    bump(v.tensor_reduce(
                        BSA[:, 64 * k:64 * (k + 1)],
                        VMM[:, 0:1024].rearrange("p (a b) -> p a b", a=64),
                        AX.X, OP.add), "V", ("bs", k))

            @blk.vector
            def _(v):
                v.wait_ge(SEM["V"], tick[("bs", T - 1)])
                bview = bass.AP(BSA, 0, [[T * 64, 128], [64, T], [1, 32]])
                sview = bass.AP(BSA, 32, [[T * 64, 128], [64, T], [1, 32]])
                oview = MNX.rearrange("p (a b) -> p a b", a=T)[:, :, 2:3] \
                    .to_broadcast([128, T, 1, 32])
                bump(v.tensor_tensor(DPP, sview, XLC[:], OP.mult), "V", "d1")
                v.wait_ge(SEM["V"], tick["d1"])
                bump(v.tensor_tensor(DPP, DPP, bview, OP.add), "V", "d2")
                v.wait_ge(SEM["V"], tick["d2"])
                bump(v.tensor_tensor(DPP, DPP, oview, OP.mult), "V", "delta")
                v.wait_ge(SEM["V"], tick["delta"])
                bump(v.tensor_reduce(W32a[:],
                                DPP.rearrange("p (a b) -> p a b", a=T),
                                AX.X, OP.add), "V", "sd")
                v.wait_ge(SEM["V"], tick["sd"])
                bump(v.tensor_scalar(MUG[:], W32a[:], 1.0 / D, None, OP.mult), "V", "mug")
                bump(v.tensor_tensor(VMM[:, 0:T * 32], XSL[:],
                                     MU1[:].to_broadcast([128, T, 32]),
                                     OP.subtract), "V", "hn0")
                v.wait_ge(SEM["V"], tick["hn0"])
                bump(v.tensor_tensor(VMM[:, 0:T * 32], VMM[:, 0:T * 32],
                                     R1A[:].to_broadcast([128, T, 32]),
                                     OP.mult), "V", "hn1")
                v.wait_ge(SEM["V"], tick["hn1"])
                bump(v.tensor_tensor(VMM[:, T * 32:T * 64], VMM[:, 0:T * 32],
                                     DPP, OP.mult), "V", "cr0")
                v.wait_ge(SEM["V"], tick["cr0"])
                bump(v.tensor_reduce(W32b[:],
                                VMM[:, T * 32:T * 64].rearrange(
                                    "p (a b) -> p a b", a=T), AX.X, OP.add), "V", "cr")
                v.wait_ge(SEM["V"], tick["cr"])
                bump(v.tensor_tensor(VMM[:, T * 32:T * 64], DPP, DPP, OP.mult), "V", "dd0")
                v.wait_ge(SEM["V"], tick["dd0"])
                bump(v.tensor_reduce(W32c[:],
                                VMM[:, T * 32:T * 64].rearrange(
                                    "p (a b) -> p a b", a=T), AX.X, OP.add), "V", "ddq")
                v.wait_ge(SEM["V"], tick["ddq"])
                bump(v.tensor_tensor(W32a[:], R1A[:], R1A[:], OP.mult), "V", "q1")
                v.wait_ge(SEM["V"], tick["q1"])
                bump(v.tensor_tensor(W32a[:], W32a[:], V1A[:], OP.mult), "V", "q2")
                v.wait_ge(SEM["V"], tick["q2"])
                bump(v.scalar_tensor_tensor(W32a[:], W32a[:], float(D), W32b[:],
                                       OP.mult, OP.add), "V", "q3")
                v.wait_ge(SEM["V"], tick["q3"])
                bump(v.tensor_tensor(W32a[:], W32a[:], W32b[:], OP.add), "V", "q4")
                v.wait_ge(SEM["V"], tick["q4"])
                bump(v.tensor_tensor(W32a[:], W32a[:], W32c[:], OP.add), "V", "q5")
                v.wait_ge(SEM["V"], tick["q5"])
                bump(v.tensor_tensor(W32b[:], MUG[:], MUG[:], OP.mult), "V", "q6")
                v.wait_ge(SEM["V"], tick["q6"])
                bump(v.scalar_tensor_tensor(W32a[:], W32a[:], 1.0 / D, W32b[:],
                                       OP.mult, OP.subtract), "V", "vg")

            @blk.scalar
            def _(a):
                a.wait_ge(SEM["V"], tick["vg"])
                bump(a.activation(W32a[:], W32a[:], AF.Sqrt, bias=EPSC[:]),
                     "A", "lng")

            @blk.vector
            def _(v):
                v.wait_ge(SEM["A"], tick["lng"])
                bump(v.reciprocal(RGA[:], W32a[:]), "V", "rg")
                v.wait_ge(SEM["V"], tick["rg"])
                bump(v.tensor_tensor(W32a[:], R1A[:], RGA[:], OP.mult), "V", "s20")
                v.wait_ge(SEM["V"], tick["s20"])
                bump(v.tensor_scalar(S2A[:], W32a[:], 1.0, None, OP.add),
                     "V", "s2")
                bump(v.tensor_tensor(W32b[:], MU1[:], R1A[:], OP.mult), "V", "b20")
                v.wait_ge(SEM["V"], tick["b20"])
                bump(v.tensor_tensor(W32b[:], W32b[:], MUG[:], OP.add), "V", "b21")
                v.wait_ge(SEM["V"], tick["b21"])
                bump(v.tensor_tensor(W32b[:], W32b[:], RGA[:], OP.mult), "V", "b22")
                v.wait_ge(SEM["V"], tick["b22"])
                bump(v.tensor_scalar(B2A[:], W32b[:], -1.0, None, OP.mult),
                     "V", "b2")
                bump(v.tensor_tensor(DPP, DPP, VMM[:, 0:T * 32], OP.add), "V", "e1")
                v.wait_ge(SEM["V"], tick["e1"])
                bump(v.tensor_tensor(DPP, DPP,
                                MUG[:].to_broadcast([128, T, 32]), OP.subtract), "V", "e2")
                v.wait_ge(SEM["V"], tick["e2"])
                bump(v.tensor_tensor(DPP, DPP,
                                RGA[:].to_broadcast([128, T, 32]), OP.mult), "V", "e3")
                v.wait_ge(SEM["V"], tick["e3"])
                bump(v.tensor_tensor(DPP, DPP, XSL[:], OP.add), "V", "dpp")

            for k in range(T):
                @blk.sync
                def _(sy, k=k):
                    if k == 0:
                        for q in range(4):
                            sy.wait_ge(SEM[f"SP{q}"], tick[("sp", T - 4 + q)])
                        sy.wait_ge(SEM["V"], tick[("sig", T - 1)])
                    if k >= 2:
                        sy.wait_ge(SEM[f"OS{k % 2}"], tick[("st", k - 2)])
                    dbump(sy.dma_start(
                        X0B[k % 2][:],
                        bass.AP(x1sp, k * 128 * D, [[D, 128], [1, D]])),
                        f"RL{k % 2}", ("rl", k))

                @blk.vector
                def _(v, k=k):
                    v.wait_ge(SEM[f"RL{k % 2}"], tick[("rl", k)])
                    if k == 0:
                        v.wait_ge(SEM["V"], tick["b2"])
                    bump(v.tensor_scalar(X0B[k % 2][:], X0B[k % 2][:],
                                         S2A[:, k:k + 1], B2A[:, k:k + 1],
                                         OP.mult, OP.add), "V", ("t2", k))

                @blk.sync
                def _(sy, k=k):
                    sy.wait_ge(SEM["V"], tick[("t2", k)])
                    dbump(sy.dma_start(
                        bass.AP(outd, k * 128 * D, [[D, 128], [1, D]]),
                        X0B[k % 2][:]), f"OS{k % 2}", ("st", k))

                @blk.gpsimd
                def _(g, k=k):
                    g.wait_ge(SEM[f"OS{k % 2}"], tick[("st", k)])
                    if k == 0:
                        g.wait_ge(SEM["V"], tick["dpp"])
                    dbump(g.indirect_dma_start(
                        out=bass.AP(outd, 0, [[32, TOK * 64], [1, 32]]),
                        out_offset=bass.IndirectOffsetOnAxis(
                            ap=IDXU[:, k:k + 1], axis=0),
                        in_=DPP[:, 32 * k:32 * (k + 1)], in_offset=None),
                        "PD", ("scat", k))

            if debug:
                @blk.sync
                def _(sy):
                    sy.wait_ge(SEM["V"], tick["dpp"])
                    dbump(sy.dma_start(dbg1d.ap(), XNS[:, 0:D]), "OS0")
                    dbump(sy.dma_start(dbg2d.ap(), BSA[:, 0:2048]), "OS0")

            @blk.sync
            def _(sy):
                sy.wait_ge(SEM["OS0"], cnt["OS0"])
                sy.wait_ge(SEM["OS1"], cnt["OS1"])
                sy.wait_ge(SEM["PD"], cnt["PD"])

    return nc


_PROG_CACHE = {}


def _get_prog(debug=False):
    key = bool(debug)
    if key not in _PROG_CACHE:
        _PROG_CACHE[key] = build_program(debug=key)
    return _PROG_CACHE[key]


def make_in_maps(L0_attn_x, L0_attn_B, L0_attn_C, L0_attn_D, L0_ffn_ln_gamma,
                 L0_ffn_ln_beta, L0_ffn_spline_bases, L0_ffn_spline_slopes,
                 L0_ffn_output_scale):
    x = np.ascontiguousarray(np.asarray(L0_attn_x, dtype=np.float32)
                             .reshape(NCORES * TOK, D))
    bt = np.ascontiguousarray(np.asarray(L0_attn_B, dtype=np.float32).T)
    ct = np.ascontiguousarray(np.asarray(L0_attn_C, dtype=np.float32).T)
    dcol = np.ascontiguousarray(
        np.asarray(L0_attn_D, dtype=np.float32).reshape(16, 128).T)
    gam = np.ascontiguousarray(
        np.asarray(L0_ffn_ln_gamma, dtype=np.float32).reshape(64, 32))
    bet = np.ascontiguousarray(
        np.asarray(L0_ffn_ln_beta, dtype=np.float32).reshape(64, 32))
    bas = np.ascontiguousarray(
        np.asarray(L0_ffn_spline_bases, dtype=np.float32).reshape(64, 512))
    slo = np.ascontiguousarray(
        np.asarray(L0_ffn_spline_slopes, dtype=np.float32).reshape(64, 512))
    osc = np.ascontiguousarray(
        np.asarray(L0_ffn_output_scale, dtype=np.float32).reshape(64, 1))
    return [{"x0": x[core * TOK:(core + 1) * TOK],
             "bt": bt, "ct": ct, "dcol": dcol, "gam": gam, "bet": bet,
             "bas": bas, "slo": slo, "osc": osc} for core in range(NCORES)]


def kernel(**inputs):
    in_maps = make_in_maps(**inputs)
    nc = _get_prog(debug=False)
    res = run_bass_kernel_spmd(nc, in_maps, core_ids=list(range(NCORES)))
    out = np.concatenate([r["out"] for r in res.results], axis=0)
    return out.reshape(2, 16384, D).astype(np.float32)


# version 08
# speedup vs baseline: 1.3793x; 1.0183x over previous
import numpy as np
from contextlib import ExitStack

import concourse.bass as bass
import concourse.mybir as mybir
from concourse.alu_op_type import AluOpType as OP
from concourse.bass_utils import run_bass_kernel_spmd

F32 = mybir.dt.float32
BF16 = mybir.dt.bfloat16
U32 = mybir.dt.uint32
I32 = mybir.dt.int32
AF = mybir.ActivationFunctionType
AX = mybir.AxisListType

NCORES = 8
TOK = 4096
T = 32
SS = 4
NSUP = T // SS
D = 2048
NS = 1024
EPS = 1e-5
BIG = 1000.0


def build_program(debug=False):
    nc = bass.Bass("TRN2", target_bir_lowering=False)

    x0d = nc.dram_tensor("x0", [TOK, D], F32, kind="ExternalInput")
    btd = nc.dram_tensor("bt", [D, NS], F32, kind="ExternalInput")
    ctd = nc.dram_tensor("ct", [NS, D], F32, kind="ExternalInput")
    dcold = nc.dram_tensor("dcol", [128, 16], F32, kind="ExternalInput")
    gamd = nc.dram_tensor("gam", [64, 32], F32, kind="ExternalInput")
    betd = nc.dram_tensor("bet", [64, 32], F32, kind="ExternalInput")
    basd = nc.dram_tensor("bas", [64, 512], F32, kind="ExternalInput")
    slod = nc.dram_tensor("slo", [64, 512], F32, kind="ExternalInput")
    oscd = nc.dram_tensor("osc", [64, 1], F32, kind="ExternalInput")
    outd = nc.dram_tensor("out", [TOK, D], F32, kind="ExternalOutput")
    if debug:
        dbg1d = nc.dram_tensor("dbg1", [128, D], F32, kind="ExternalOutput")
        dbg2d = nc.dram_tensor("dbg2", [128, 2048], F32, kind="ExternalOutput")

    x1sp = nc.dram_tensor("x1sp", [TOK * 64, 32], F32)
    th_d = nc.dram_tensor("th_d", [1, D], F32)
    w2_d = nc.dram_tensor("w2_d", [64, 32], BF16)
    bia_d = nc.dram_tensor("bia_d", [1, 64], BF16)
    cc_in = nc.dram_tensor("cc_in", [1, 128], F32)
    cc_out = nc.dram_tensor("cc_out", [1, 128], F32)

    with ExitStack() as st:
        def sb(name, shape, dt):
            return st.enter_context(nc.sbuf_tensor(name, shape, dt))

        BT = sb("BT", [128, 16 * NS], BF16)
        CT = sb("CT", [128, 8 * D], BF16)
        TOKI = sb("TOKI", [128, T], F32)
        EPSC = sb("EPSC", [128, 1], F32)
        VMM = sb("VMM", [128, T * 64], F32)
        SOSC = sb("SOSC", [64, 1], F32)
        TBH = sb("TBH", [64, 1024], BF16)
        TBL = sb("TBL", [64, 1024], BF16)
        GMM = sb("GMM", [64, 2], F32)
        PMM = sb("PMM", [64, 2], F32)

        X0B = [sb(f"X0B{i}", [128, D], F32) for i in range(4)]
        ATTNB = [sb(f"ATTNB{i}", [128, D], F32) for i in range(2)]

        MU0 = sb("MU0", [128, T], F32)
        SM0 = sb("SM0", [128, T], F32)
        SQ0 = sb("SQ0", [128, T], F32)
        V0C = sb("V0C", [128, T], F32)
        R0A = sb("R0A", [128, T], F32)
        B0A = sb("B0A", [128, T], F32)
        SMA = sb("SMA", [128, T], F32)
        SQA = sb("SQA", [128, T], F32)
        MUA = sb("MUA", [128, T], F32)
        VAC = sb("VAC", [128, T], F32)
        RAA = sb("RAA", [128, T], F32)
        BAA = sb("BAA", [128, T], F32)
        SM1 = sb("SM1", [128, T], F32)
        SQ1 = sb("SQ1", [128, T], F32)
        MU1 = sb("MU1", [128, T], F32)
        V1A = sb("V1A", [128, T], F32)
        R1A = sb("R1A", [128, T], F32)
        V2C = sb("V2C", [128, T], F32)
        S12 = sb("S12", [128, T], F32)
        TIDA = sb("TIDA", [128, T], F32)
        IDXF = sb("IDXF", [128, T], F32)
        IDXU = sb("IDXU", [128, T], U32)
        R12A = sb("R12A", [128, T], F32)
        TMIN = sb("TMIN", [128, T], F32)
        TMAX = sb("TMAX", [128, T], F32)
        MUG = sb("MUG", [128, T], F32)
        RGA = sb("RGA", [128, T], F32)
        S2A = sb("S2A", [128, T], F32)
        B2A = sb("B2A", [128, T], F32)
        INV = sb("INV", [128, T], F32)
        W32a = sb("W32a", [128, T], F32)
        W32b = sb("W32b", [128, T], F32)
        W32c = sb("W32c", [128, T], F32)


        OHT = sb("OHT", [64, T * 128], BF16)
        GBA = sb("GBA", [128, T * 64], F32)
        XSL = sb("XSL", [128, T * 32], F32)
        XNS = sb("XNS", [128, T * 32], F32)
        RED = sb("RED", [128, 64], F32)

        SEM = {n: st.enter_context(nc.semaphore(n)) for n in
               ["LD", "PD", "CC", "V", "A", "P", "G",
                "SP0", "SP1", "SP2", "SP3",
                "LX0", "LX1", "LX2", "LX3", "RL0", "RL1", "OS0", "OS1"]}
        cnt = {n: 0 for n in SEM}
        tick = {}

        def bump(inst, e, key=None):
            cnt[e] += 1
            inst.then_inc(SEM[e], 1)
            if key is not None:
                tick[key] = cnt[e]
            return cnt[e]

        def dbump(inst, e, key=None):
            cnt[e] += 16
            inst.then_inc(SEM[e], 16)
            if key is not None:
                tick[key] = cnt[e]
            return cnt[e]

        blk = st.enter_context(nc.Block())

        pst_free = [("V", 0)]

        if True:
            PS1 = [st.enter_context(nc.psum_tensor("PS1a", [128, 512], F32)),
                   st.enter_context(nc.psum_tensor("PS1b", [128, 512], F32))]
            PS2 = st.enter_context(nc.psum_tensor("PS2", [128, 2048], F32))
            PST = st.enter_context(nc.psum_tensor("PST", [128, 1024], BF16))
            PSS = st.enter_context(nc.psum_tensor("PSS", [128, 512], F32))
            PSSb = PSS.ap().bitcast(BF16)
            sb1 = sb
            DD = sb1("DD", [128, 16 * 128], BF16)
            DCOL = sb1("DCOL", [128, 16], F32)
            WSS = sb1("WSS", [128, 16 * 64], BF16)
            THB = sb1("THB", [128, D], BF16)
            SID = sb1("SID", [128, 128], BF16)
            SIDF = sb1("SIDF", [128, 128], F32)
            ONE1 = sb1("ONE1", [1, 128], BF16)
            BIAR = sb1("BIAR", [1, 64], BF16)
            TIOT = sb1("TIOT", [128, 64], F32)
            CLB = sb1("CLB", [128, 64], F32)
            IOT128 = sb1("IOT128", [128, 128], F32)
            PIDX = sb1("PIDX", [128, 1], F32)
            SGAM = sb1("SGAM", [64, 32], F32)
            SBET = sb1("SBET", [64, 32], F32)
            GBH = sb1("GBH", [64, 64], BF16)
            GBL = sb1("GBL", [64, 64], BF16)
            W2BF = sb1("W2BF", [64, 32], BF16)
            BI64 = sb1("BI64", [64, 1], BF16)
            XAB = [sb("XAB0", [128, D], BF16)]
            SIGB = [sb("SIGB0", [128, D], BF16)]
            XAT = sb1("XAT", [128, SS * D], BF16)
            STT = sb1("STT", [128, 8 * 512], BF16)
            SGT = [sb("SGT0", [128, D], BF16)]
            CS8 = sb1("CS8", [128, 8], F32)
            MX8 = sb1("MX8", [128, 8], F32)
            IX8 = sb1("IX8", [128, 8], U32)
            TSC = sb1("TSC", [128, 64], F32)
            TSM = sb1("TSM", [128, 64], F32)
            MX8b = sb1("MX8b", [128, 8], F32)
            IX8b = sb1("IX8b", [128, 8], U32)
            BCF = sb1("BCF", [128, 1], F32)
            OHB = sb1("OHB", [128, T * 64], BF16)

            @blk.sync
            def _(sy):
                dbump(sy.dma_start(SGAM[:], gamd.ap()), "LD")
                dbump(sy.dma_start(SBET[:], betd.ap()), "LD")
                dbump(sy.dma_start(VMM[0:64, 0:512], basd.ap()), "LD")
                dbump(sy.dma_start(VMM[0:64, 512:1024], slod.ap()), "LD")
                dbump(sy.dma_start(SOSC[:], oscd.ap()), "LD")
                dbump(sy.dma_start(DCOL[:], dcold.ap()), "LD", "ldconst")

            @blk.gpsimd
            def _(g):
                g.memset(EPSC[:], EPS)
                g.memset(ONE1[:], 1.0)
                g.memset(WSS[:], 0)
                _gi = g.iota(TIOT[:], [[1, 64]], channel_multiplier=0,
                       allow_small_or_imprecise_dtypes=True)
                _gi = g.iota(CLB[:], [[1, 8], [0, 8]], channel_multiplier=0,
                       allow_small_or_imprecise_dtypes=True)
                _gi = g.iota(TOKI[:], [[128, T]], channel_multiplier=1,
                       allow_small_or_imprecise_dtypes=True)
                _gi = g.iota(PIDX[:], [[0, 1]], channel_multiplier=1,
                       allow_small_or_imprecise_dtypes=True)
                _gi = g.iota(IOT128[:], [[1, 128]], channel_multiplier=0,
                       allow_small_or_imprecise_dtypes=True)
                bump(_gi, "G", "iotas")
                dbump(g.dma_start(
                    BT[:], bass.AP(btd, 0, [[NS, 128], [128 * NS, 16], [1, NS]])),
                    "PD")
                dbump(g.dma_start(
                    CT[:], bass.AP(ctd, 0, [[D, 128], [128 * D, 8], [1, D]])),
                    "PD", "wload")
                g.wait_ge(SEM["PD"], cnt["PD"])

            @blk.vector
            def _(v):
                v.wait_ge(SEM["G"], tick["iotas"])
                bump(v.tensor_scalar(SID[:], IOT128[:], PIDX[:], None,
                                     OP.is_equal), "V", "sid")
                v.wait_ge(SEM["V"], tick["sid"])
                bump(v.tensor_copy(SIDF[:], SID[:]), "V", "sidf")
                v.wait_ge(SEM["LD"], tick["ldconst"])
                for j in range(16):
                    _vi = v.tensor_scalar(DD[:, 128 * j:128 * (j + 1)], SID[:],
                                          DCOL[:, j:j + 1], None, OP.mult)
                bump(_vi, "V", "ddiag")
                bump(v.reciprocal(VMM[0:64, 1300:1332], SGAM[:]), "V", "recg")
                v.wait_ge(SEM["V"], tick["recg"])
                bump(v.tensor_tensor(VMM[0:64, 1332:1364], VMM[0:64, 1300:1332],
                                     SBET[:], OP.mult), "V", "th0")
                v.wait_ge(SEM["V"], tick["th0"])
                bump(v.tensor_scalar(VMM[0:64, 1332:1364], VMM[0:64, 1332:1364],
                                     -1.0, None, OP.mult), "V", "theta")
                v.wait_ge(SEM["LD"], tick["ldconst"])
                bump(v.tensor_reduce(VMM[0:64, 1364:1396],
                                     VMM[0:64, 512:1024].rearrange(
                                         "p (a b) -> p a b", a=32),
                                     AX.X, OP.add), "V", "ssum")

            @blk.gpsimd
            def _(g):
                g.wait_ge(SEM["V"], tick["theta"])
                dbump(g.dma_start(th_d.ap(), VMM[0:64, 1332:1364]), "PD", "thdma")
                g.wait_ge(SEM["PD"], cnt["PD"])
                dbump(g.dma_start(THB[:], bass.AP(th_d, 0, [[0, 128], [1, D]])),
                      "PD", "thbcast")

            @blk.scalar
            def _(a):
                a.wait_ge(SEM["V"], tick["ssum"])
                bump(a.sign(VMM[0:64, 1396:1428], VMM[0:64, 1364:1396]),
                     "A", "ssgn")
                bump(a.sign(VMM[0:64, 1428:1460], SGAM[:]), "A", "sgam")

            @blk.vector
            def _(v):
                v.wait_ge(SEM["A"], tick["sgam"])
                bump(v.tensor_tensor(VMM[0:64, 1460:1492], VMM[0:64, 1396:1428],
                                     VMM[0:64, 1428:1460], OP.mult), "V", "w0")
                v.wait_ge(SEM["V"], tick["w0"])
                bump(v.tensor_reduce(VMM[0:64, 1492:1493],
                                     VMM[0:64, 1460:1492].unsqueeze(1), AX.X,
                                     OP.add), "V", "bsum")
                v.wait_ge(SEM["V"], tick["bsum"])
                bump(v.tensor_scalar(BI64[:], VMM[0:64, 1492:1493], -1.0, None,
                                     OP.mult), "V", "bi64")
                bump(v.tensor_scalar(W2BF[:], VMM[0:64, 1460:1492], 2.0, None,
                                     OP.mult), "V", "w2bf")
                v.wait_ge(SEM["LD"], tick["ldconst"])
                v.tensor_copy(VMM[0:64, 1024:1056], SGAM[:])
                bump(v.tensor_copy(VMM[0:64, 1056:1088], SBET[:]), "V", "gb0")
                v.wait_ge(SEM["V"], tick["gb0"])
                bump(v.tensor_copy(GBH[:], VMM[0:64, 1024:1088]), "V", "gbh")
                v.wait_ge(SEM["V"], tick["gbh"])
                bump(v.tensor_copy(VMM[0:64, 1088:1152], GBH[:]), "V", "gbb")
                v.wait_ge(SEM["V"], tick["gbb"])
                bump(v.tensor_tensor(VMM[0:64, 1152:1216], VMM[0:64, 1024:1088],
                                     VMM[0:64, 1088:1152], OP.subtract), "V", "gbl0")
                v.wait_ge(SEM["V"], tick["gbl0"])
                bump(v.tensor_copy(GBL[:], VMM[0:64, 1152:1216]), "V", "gbl")
                v.wait_ge(SEM["PD"], tick["thdma"])
                bump(v.tensor_copy(TBH[:], VMM[0:64, 0:1024]), "V", "tbh")
                v.wait_ge(SEM["V"], tick["tbh"])
                bump(v.tensor_copy(VMM[0:64, 1024:2048], TBH[:]), "V", "tbb")
                v.wait_ge(SEM["V"], tick["tbb"])
                bump(v.tensor_tensor(VMM[0:64, 1024:2048], VMM[0:64, 0:1024],
                                     VMM[0:64, 1024:2048], OP.subtract), "V", "tbl0")
                v.wait_ge(SEM["V"], tick["tbl0"])
                bump(v.tensor_copy(TBL[:], VMM[0:64, 1024:2048]), "V", "tbl")

            @blk.gpsimd
            def _(g):
                g.wait_ge(SEM["V"], tick["w2bf"])
                g.wait_ge(SEM["PD"], cnt["PD"])
                dbump(g.dma_start(w2_d.ap(), W2BF[:]), "PD")
                g.wait_ge(SEM["PD"], cnt["PD"])
                for b in range(4):
                    dbump(g.dma_start(
                        bass.AP(WSS, 32 * b * (16 * 64) + b, [[16 * 64, 32],
                                                              [68, 16]]),
                        bass.AP(w2_d, 32 * b, [[1, 32], [128, 16]]),
                        allow_slow_non_contiguous=True), "PD")
                g.wait_ge(SEM["V"], tick["bi64"])
                dbump(g.dma_start(bia_d.ap(), BI64[:]), "PD")
                g.wait_ge(SEM["PD"], cnt["PD"])
                dbump(g.dma_start(BIAR[:], bia_d.ap()), "PD")
                bump(g.wait_ge(SEM["PD"], cnt["PD"]), "G", "winit")



            a_defer = []

            def defer_a(keys, fn):
                for key in keys:
                    cnt["A"] += 1
                    tick[key] = cnt["A"]
                a_defer.append(fn)

            def flush_a():
                fns = list(a_defer)
                a_defer.clear()

                @blk.scalar
                def _(a):
                    for fn in fns:
                        fn(a)

            def p1_load(k):
                @blk.sync
                def _(sy, k=k):
                    if k >= 4:
                        sy.wait_ge(SEM["V"], tick[("sig", k - 4)])
                        sy.wait_ge(SEM[f"SP{k % 4}"], tick[("sp", k - 4)])
                    dbump(sy.dma_start(
                        X0B[k % 4][:],
                        bass.AP(x0d, k * 128 * D, [[D, 128], [1, D]])),
                        f"LX{k % 4}", ("x0", k))

            def p1_pre(k):
                @blk.vector
                def _(v, k=k):
                    v.wait_ge(SEM[f"LX{k % 4}"], tick[("x0", k)])
                    bump(v.tensor_reduce(SM0[:, k:k + 1], X0B[k % 4][:],
                                         AX.X, OP.add), "V", ("sum0", k))
                    v.wait_ge(SEM["V"], tick[("sum0", k)])
                    bump(v.tensor_scalar(MU0[:, k:k + 1], SM0[:, k:k + 1],
                                         1.0 / D, None, OP.mult),
                         "V", ("mu0", k))

                def _sq0(a, k=k):
                    a.wait_ge(SEM[f"LX{k % 4}"], tick[("x0", k)])
                    if k == 0:
                        a.wait_ge(SEM["V"], tick["tbl"])
                        a.wait_ge(SEM["PD"], tick["thdma"])
                    if k >= 4:
                        a.wait_ge(SEM["V"],
                                  tick[("x1", 4 * (k // 4) - 4 + 1
                                        + (k % 4) // 2)])
                    a.activation(VMM[:], X0B[k % 4][:], AF.Square,
                                 accum_out=SQ0[:, k:k + 1]
                                 ).then_inc(SEM["A"], 1)
                defer_a([("sq0", k)], _sq0)

                @blk.vector
                def _(v, k=k):
                    v.wait_ge(SEM["V"], tick[("mu0", k)])
                    v.wait_ge(SEM["A"], tick[("sq0", k)])
                    bump(v.tensor_tensor(W32a[:, k:k + 1], MU0[:, k:k + 1],
                                         MU0[:, k:k + 1], OP.mult),
                         "V", ("m20", k))
                    v.wait_ge(SEM["V"], tick[("m20", k)])
                    bump(v.scalar_tensor_tensor(V0C[:, k:k + 1],
                                                SQ0[:, k:k + 1], 1.0 / D,
                                                W32a[:, k:k + 1], OP.mult,
                                                OP.subtract), "V", ("v0", k))

                def _ln0(a, k=k):
                    a.wait_ge(SEM["V"], tick[("v0", k)])
                    a.activation(V0C[:, k:k + 1], V0C[:, k:k + 1], AF.Sqrt,
                                 bias=EPSC[:]).then_inc(SEM["A"], 1)
                defer_a([("ln0", k)], _ln0)

                @blk.vector
                def _(v, k=k):
                    v.wait_ge(SEM["A"], tick[("ln0", k)])
                    bump(v.reciprocal(R0A[:, k:k + 1], V0C[:, k:k + 1]),
                         "V", ("r0", k))
                    v.wait_ge(SEM["V"], tick[("r0", k)])
                    bump(v.scalar_tensor_tensor(B0A[:, k:k + 1],
                                                MU0[:, k:k + 1], -1.0,
                                                R0A[:, k:k + 1], OP.mult,
                                                OP.mult), "V", ("b0", k))

            def p1_xa(k):
                @blk.vector
                def _(v, k=k):
                    v.wait_ge(SEM["V"], tick[("b0", k)])
                    if k >= 1:
                        v.wait_ge(SEM["P"], tick[("xat", k - 1)])
                    bump(v.tensor_scalar(XAB[0][:], X0B[k % 4][:],
                                         R0A[:, k:k + 1], B0A[:, k:k + 1],
                                         OP.mult, OP.add), "V", ("xa", k))

            def p1_txa(k):
                c = k % SS
                for grp in range(2):
                    @blk.tensor
                    def _(t, grp=grp, c=c, k=k):
                        if grp == 0:
                            t.wait_ge(SEM["V"], tick[("xa", k)])
                            t.wait_ge(SEM["V"], tick["sid"])
                        e, tk_ = pst_free[0]
                        t.wait_ge(SEM[e], tk_)
                        for j8 in range(8):
                            j = grp * 8 + j8
                            _mi = t.transpose(PST[:, 128 * j8:128 * (j8 + 1)],
                                        XAB[0][:, 128 * j:128 * (j + 1)],
                                        SID[:])
                        bump(_mi, "P", ("trx", k, grp))

                    def _ev(a, grp=grp, c=c, k=k):
                        a.wait_ge(SEM["P"], tick[("trx", k, grp)])
                        a.copy(
                            XAT[:, D * c + 1024 * grp:
                                D * c + 1024 * (grp + 1)],
                            PST[:]).then_inc(SEM["A"], 1)
                    defer_a([("evT", k, grp)], _ev)
                    pst_free[0] = ("A", tick[("evT", k, grp)])
                tick[("xat", k)] = tick[("trx", k, 1)]

            def p1_mm1(s):
                xat3 = XAT[:].rearrange("p (c f) -> p c f", c=SS)
                for i in range(8):
                    @blk.tensor
                    def _(t, i=i, s=s):
                        if i == 0:
                            t.wait_ge(SEM["A"], tick[("evT", s * SS + SS - 1, 1)])
                            if s == 0:
                                t.wait_ge(SEM["PD"], tick["wload"])
                        prev = (s, i - 2) if i >= 2 else \
                            ((s - 1, i + 6) if s > 0 else None)
                        if prev is not None:
                            t.wait_ge(SEM["A"], tick[("tanh",) + prev])
                        for j in range(16):
                            _mi = t.matmul(PS1[i % 2][:],
                                     BT[:, NS * j + 128 * i:
                                        NS * j + 128 * (i + 1)],
                                     xat3[:, :, 128 * j:128 * (j + 1)],
                                     start=(j == 0), stop=(j == 15))
                        bump(_mi, "P", ("mm1", s, i))

                    def _th(a, i=i, s=s):
                        a.wait_ge(SEM["P"], tick[("mm1", s, i)])
                        a.activation(STT[:, 512 * i:512 * (i + 1)],
                                     PS1[i % 2][:], AF.Tanh
                                     ).then_inc(SEM["A"], 1)
                    defer_a([("tanh", s, i)], _th)

            def p1_mm2(k):
                s, c = k // SS, k % SS

                for hb in range(2):
                    @blk.tensor
                    def _(t, k=k, s=s, c=c, hb=hb):
                        if hb == 0:
                            if k >= 1:
                                t.wait_ge(SEM["V"], tick[("aev1", k - 1)])
                            t.wait_ge(SEM["A"], tick[("tanh", s, 7)])
                            if k == 0:
                                t.wait_ge(SEM["V"], tick["ddiag"])
                        elif k >= 1:
                            t.wait_ge(SEM["V"], tick[("aev2", k - 1)])
                        for j in range(8 * hb, 8 * hb + 8):
                            _mi = t.matmul(PS2[:, 128 * j:128 * (j + 1)],
                                     XAT[:, D * c + 128 * j:
                                         D * c + 128 * (j + 1)],
                                     DD[:, 128 * j:128 * (j + 1)],
                                     start=(j % 4 == 0), stop=False,
                                     skip_group_check=True)
                        for i in range(8):
                            for h in (2 * hb, 2 * hb + 1):
                                _mi = t.matmul(
                                    PS2[:, 512 * h:512 * (h + 1)],
                                    STT[:, 512 * i + 128 * c:
                                        512 * i + 128 * (c + 1)],
                                    CT[:, D * i + 512 * h:
                                        D * i + 512 * (h + 1)],
                                    start=False, stop=(i == 7),
                                    skip_group_check=True)
                        bump(_mi, "P", ("mm2h", k, hb))

            def p1_post(k):
                @blk.vector
                def _(v, k=k):
                    v.wait_ge(SEM["P"], tick[("mm2h", k, 0)])
                    if k >= 2:
                        v.wait_ge(SEM["V"], tick[("sig", k - 2)])
                    bump(v.tensor_scalar(ATTNB[k % 2][:, 0:1024],
                                         PS2[:, 0:1024], 0.0, 0.0,
                                         OP.add, OP.add,
                                         accum_out=SMA[:, k:k + 1]),
                         "V", ("aev1", k))
                    v.wait_ge(SEM["P"], tick[("mm2h", k, 1)])
                    bump(v.tensor_scalar(ATTNB[k % 2][:, 1024:2048],
                                         PS2[:, 1024:2048], 0.0, 0.0,
                                         OP.add, OP.add,
                                         accum_out=W32c[:, k:k + 1]),
                         "V", ("aev2", k))
                    v.wait_ge(SEM["V"], tick[("aev2", k)])
                    bump(v.scalar_tensor_tensor(MUA[:, k:k + 1],
                                                SMA[:, k:k + 1], 1.0,
                                                W32c[:, k:k + 1], OP.mult,
                                                OP.add), "V", ("sma", k))
                    v.wait_ge(SEM["V"], tick[("sma", k)])
                    bump(v.tensor_scalar(MUA[:, k:k + 1], MUA[:, k:k + 1],
                                         1.0 / D, None, OP.mult),
                         "V", ("mua", k))

                def _sqa(a, k=k):
                    a.wait_ge(SEM["V"], tick[("aev2", k)])
                    a.activation(VMM[:], ATTNB[k % 2][:], AF.Square,
                                 accum_out=SQA[:, k:k + 1]
                                 ).then_inc(SEM["A"], 1)
                defer_a([("sqa", k)], _sqa)

                @blk.vector
                def _(v, k=k):
                    v.wait_ge(SEM["V"], tick[("mua", k)])
                    v.wait_ge(SEM["A"], tick[("sqa", k)])
                    bump(v.tensor_tensor(W32a[:, k:k + 1], MUA[:, k:k + 1],
                                         MUA[:, k:k + 1], OP.mult),
                         "V", ("m2a", k))
                    v.wait_ge(SEM["V"], tick[("m2a", k)])
                    bump(v.scalar_tensor_tensor(VAC[:, k:k + 1],
                                                SQA[:, k:k + 1], 1.0 / D,
                                                W32a[:, k:k + 1], OP.mult,
                                                OP.subtract), "V", ("va", k))

                def _lna(a, k=k):
                    a.wait_ge(SEM["V"], tick[("va", k)])
                    a.activation(VAC[:, k:k + 1], VAC[:, k:k + 1], AF.Sqrt,
                                 bias=EPSC[:]).then_inc(SEM["A"], 1)
                defer_a([("lna", k)], _lna)

                @blk.vector
                def _(v, k=k):
                    v.wait_ge(SEM["A"], tick[("lna", k)])
                    bump(v.reciprocal(RAA[:, k:k + 1], VAC[:, k:k + 1]),
                         "V", ("ba", k))

                @blk.vector
                def _(v, k=k):
                    v.wait_ge(SEM["V"], tick[("ba", k)])
                    v.wait_ge(SEM["A"], tick[("sqa", k)])
                    if k == 0:
                        v.wait_ge(SEM["PD"], tick["thdma"])
                        v.wait_ge(SEM["V"], tick["tbl"])
                    bump(v.tensor_scalar(VMM[:], ATTNB[k % 2][:],
                                         MUA[:, k:k + 1], None, OP.subtract),
                         "V", ("tp", k))
                    v.wait_ge(SEM["V"], tick[("tp", k)])
                    bump(v.scalar_tensor_tensor(X0B[k % 4][:], VMM[:],
                                                RAA[:, k:k + 1],
                                                X0B[k % 4][:], OP.mult, OP.add,
                                                accum_out=SM1[:, k:k + 1]),
                         "V", ("x1", k))

                @blk.sync
                def _(sy, k=k):
                    sy.wait_ge(SEM["V"], tick[("x1", k)])
                    dbump(sy.dma_start(
                        bass.AP(x1sp, k * 128 * D, [[D, 128], [1, D]]),
                        X0B[k % 4][:]), f"SP{k % 4}", ("sp", k))

                def _sq1(a, k=k):
                    a.wait_ge(SEM["V"], tick[("x1", k)])
                    a.activation(VMM[:], X0B[k % 4][:], AF.Square,
                                 accum_out=SQ1[:, k:k + 1]
                                 ).then_inc(SEM["A"], 1)
                defer_a([("sq1", k)], _sq1)

                @blk.vector
                def _(v, k=k):
                    v.wait_ge(SEM["V"], tick[("x1", k)])
                    bump(v.tensor_scalar(MU1[:, k:k + 1], SM1[:, k:k + 1],
                                         1.0 / D, None, OP.mult),
                         "V", ("mu1", k))
                    v.wait_ge(SEM["V"], tick[("mu1", k)])
                    v.wait_ge(SEM["A"], tick[("sq1", k)])
                    bump(v.tensor_tensor(W32a[:, k:k + 1], MU1[:, k:k + 1],
                                         MU1[:, k:k + 1], OP.mult),
                         "V", ("m21", k))
                    v.wait_ge(SEM["V"], tick[("m21", k)])
                    bump(v.scalar_tensor_tensor(V1A[:, k:k + 1],
                                                SQ1[:, k:k + 1], 1.0 / D,
                                                W32a[:, k:k + 1], OP.mult,
                                                OP.subtract), "V", ("v1", k))

                def _ln1(a, k=k):
                    a.wait_ge(SEM["V"], tick[("v1", k)])
                    a.activation(W32b[:, k:k + 1], V1A[:, k:k + 1],
                                 AF.Sqrt, bias=EPSC[:]).then_inc(SEM["A"], 1)
                defer_a([("ln1", k)], _ln1)

                @blk.vector
                def _(v, k=k):
                    v.wait_ge(SEM["A"], tick[("ln1", k)])
                    bump(v.reciprocal(R1A[:, k:k + 1], W32b[:, k:k + 1]),
                         "V", ("r1", k))
                    bump(v.tensor_scalar(V2C[:, k:k + 1], V1A[:, k:k + 1],
                                         1.0 + EPS, EPS * EPS, OP.mult,
                                         OP.add), "V", ("v2", k))

                def _ln2(a, k=k):
                    a.wait_ge(SEM["V"], tick[("v2", k)])
                    a.activation(S12[:, k:k + 1], V2C[:, k:k + 1],
                                 AF.Sqrt).then_inc(SEM["A"], 1)
                defer_a([("ln2", k)], _ln2)

                @blk.vector
                def _(v, k=k):
                    v.wait_ge(SEM["A"], tick[("ln2", k)])
                    bump(v.reciprocal(R12A[:, k:k + 1], S12[:, k:k + 1]),
                         "V", ("s12", k))
                    v.wait_ge(SEM["V"], tick[("s12", k)])
                    if k == 0:
                        v.wait_ge(SEM["PD"], tick["thbcast"])
                    bump(v.tensor_scalar(ATTNB[k % 2][:], THB[:],
                                         S12[:, k:k + 1], MU1[:, k:k + 1],
                                         OP.mult, OP.add), "V", ("thr", k))
                    v.wait_ge(SEM["V"], tick[("thr", k)])
                    if k >= 1:
                        v.wait_ge(SEM["P"], tick[("sgt", k - 1)])
                    bump(v.tensor_tensor(SIGB[0][:], X0B[k % 4][:],
                                         ATTNB[k % 2][:], OP.is_gt),
                         "V", ("sig", k))

            def p1_tsg(k):
                for grp in range(2):
                    @blk.tensor
                    def _(t, grp=grp, k=k):
                        if grp == 0:
                            t.wait_ge(SEM["V"], tick[("sig", k)])
                        e, tk_ = pst_free[0]
                        t.wait_ge(SEM[e], tk_)
                        for j8 in range(8):
                            j = grp * 8 + j8
                            _mi = t.transpose(PST[:, 128 * j8:128 * (j8 + 1)],
                                        SIGB[0][:, 128 * j:128 * (j + 1)],
                                        SID[:])
                        bump(_mi, "P", ("trs", k, grp))

                    @blk.vector
                    def _(v, grp=grp, k=k):
                        v.wait_ge(SEM["P"], tick[("trs", k, grp)])
                        bump(v.tensor_copy(
                            SGT[0][:, 1024 * grp:1024 * (grp + 1)], PST[:]),
                            "V", ("evS", k, grp))
                        pst_free[0] = ("V", tick[("evS", k, grp)])
                tick[("sgt", k)] = tick[("trs", k, 1)]

            def p1_scores(k):
                @blk.tensor
                def _(t, k=k):
                    t.wait_ge(SEM["V"], tick[("evS", k, 1)])
                    if k == 0:
                        t.wait_ge(SEM["G"], tick["winit"])
                    else:
                        t.wait_ge(SEM["V"], tick[("gbev", k - 1)])
                    for j in range(16):
                        _mi = t.matmul(PSS[:, 0:64],
                                 SGT[0][:, 128 * j:128 * (j + 1)],
                                 WSS[:, 64 * j:64 * (j + 1)],
                                 start=(j == 0), stop=False,
                                 skip_group_check=True)
                    _mi = t.matmul(PSS[:, 0:64], ONE1[:], BIAR[:], start=False,
                             stop=True, skip_group_check=True)
                    bump(_mi, "P", ("sc", k))

                @blk.vector
                def _(v, k=k):
                    v.wait_ge(SEM["P"], tick[("sc", k)])
                    bump(v.tensor_reduce(
                        CS8[:], PSS[:, 0:64].rearrange("p (a b) -> p a b", a=8),
                        AX.X, OP.add), "V", ("cs8", k))
                    bump(v.tensor_copy(TSC[:], PSS[:, 0:64]), "V", ("tsev", k))
                    v.wait_ge(SEM["V"], tick[("cs8", k)])
                    bump(v.max(MX8[:], CS8[:]), "V", ("cmx", k))
                    v.wait_ge(SEM["V"], tick[("cmx", k)])
                    bump(v.max_index(IX8[:], MX8[:], CS8[:]), "V", ("cix", k))
                    v.wait_ge(SEM["V"], tick[("cix", k)])
                    bump(v.tensor_copy(BCF[:], IX8[:, 0:1]), "V", ("bcf", k))
                    v.wait_ge(SEM["V"], tick[("bcf", k)])
                    bump(v.tensor_scalar(TSM[:], CLB[:], BCF[:], None,
                                         OP.is_equal), "V", ("m64", k))
                    v.wait_ge(SEM["V"], tick[("m64", k)])
                    bump(v.scalar_tensor_tensor(TSM[:], TSC[:], BIG, TSM[:],
                                                OP.add, OP.mult),
                         "V", ("msk", k))
                    v.wait_ge(SEM["V"], tick[("msk", k)])
                    bump(v.max(MX8b[:], TSM[:]), "V", ("tmx", k))
                    v.wait_ge(SEM["V"], tick[("tmx", k)])
                    bump(v.max_index(IX8b[:], MX8b[:], TSM[:]),
                         "V", ("tix", k))
                    v.wait_ge(SEM["V"], tick[("tix", k)])
                    bump(v.tensor_copy(TIDA[:, k:k + 1], IX8b[:, 0:1]),
                         "V", ("tid", k))
                    v.wait_ge(SEM["V"], tick[("tid", k)])
                    bump(v.tensor_scalar(OHB[:, 64 * k:64 * (k + 1)], TIOT[:],
                                         TIDA[:, k:k + 1], None, OP.is_equal),
                         "V", ("oh", k))
                    bump(v.scalar_tensor_tensor(IDXF[:, k:k + 1],
                                                TOKI[:, k:k + 1], 64.0,
                                                TIDA[:, k:k + 1], OP.mult,
                                                OP.add), "V", ("idxf", k))
                    v.wait_ge(SEM["V"], tick[("idxf", k)])
                    bump(v.tensor_copy(IDXU[:, k:k + 1], IDXF[:, k:k + 1]),
                         "V", ("idxu", k))

                @blk.tensor
                def _(t, k=k):
                    t.wait_ge(SEM["V"], tick[("oh", k)])
                    t.wait_ge(SEM["V"], tick[("tsev", k)])
                    _mi = t.transpose(PSSb[0:64, 512:640],
                                OHB[:, 64 * k:64 * (k + 1)], SID[:])
                    bump(_mi, "P", ("oht", k))

                @blk.vector
                def _(v, k=k):
                    v.wait_ge(SEM["P"], tick[("oht", k)])
                    bump(v.tensor_copy(OHT[:, 128 * k:128 * (k + 1)],
                                       PSSb[0:64, 512:640]), "V", ("ohtev", k))

                @blk.tensor
                def _(t, k=k):
                    t.wait_ge(SEM["V"], tick[("ohtev", k)])
                    if k == 0:
                        t.wait_ge(SEM["V"], tick["gbl"])
                    _mi = t.matmul(PSS[:, 64:128], OHT[:, 128 * k:128 * (k + 1)],
                             GBH[:], start=True, stop=False,
                             skip_group_check=True)
                    _mi = t.matmul(PSS[:, 64:128], OHT[:, 128 * k:128 * (k + 1)],
                             GBL[:], start=False, stop=True,
                             skip_group_check=True)
                    bump(_mi, "P", ("gbmm", k))

                @blk.vector
                def _(v, k=k):
                    v.wait_ge(SEM["P"], tick[("gbmm", k)])
                    bump(v.tensor_copy(GBA[:, 64 * k:64 * (k + 1)],
                                       PSS[:, 64:128]), "V", ("gbev", k))


            def pre_half(s, half):
                c0 = 2 * half
                for c in (c0, c0 + 1):
                    k = s * SS + c
                    p1_load(k)
                    p1_pre(k)
                for c in (c0, c0 + 1):
                    k = s * SS + c
                    p1_xa(k)
                    p1_txa(k)

            pre_half(0, 0)
            pre_half(0, 1)
            for s in range(NSUP):
                p1_mm1(s)
                for c in range(SS):
                    k = s * SS + c
                    p1_mm2(k)
                    p1_post(k)
                    p1_tsg(k)
                    p1_scores(k)
                    if s + 1 < NSUP:
                        if c == 1:
                            pre_half(s + 1, 0)
                        elif c == 3:
                            pre_half(s + 1, 1)
                flush_a()

            @blk.gpsimd
            def _(g):
                for q in range(4):
                    g.wait_ge(SEM[f"SP{q}"], tick[("sp", T - 4 + q)])
                g.wait_ge(SEM["V"], tick[("idxu", T - 1)])
                for k in range(T):
                    dbump(g.indirect_dma_start(
                        out=XSL[:, 32 * k:32 * (k + 1)], out_offset=None,
                        in_=x1sp.ap(),
                        in_offset=bass.IndirectOffsetOnAxis(
                            ap=IDXU[:, k:k + 1], axis=0)), "PD", ("xg", k))

            @blk.vector
            def _(v):
                v.wait_ge(SEM["PD"], tick[("xg", T - 1)])
                bump(v.tensor_tensor(XNS[:], XSL[:],
                                MU1[:].to_broadcast([128, T, 32]), OP.subtract), "V", "h1")
                v.wait_ge(SEM["V"], tick["h1"])
                bump(v.tensor_tensor(XNS[:], XNS[:],
                                R12A[:].to_broadcast([128, T, 32]), OP.mult), "V", "h3")
                v.wait_ge(SEM["V"], tick["h3"])
                bump(v.tensor_tensor(XNS[:], XNS[:],
                                bass.AP(GBA, 0, [[T * 64, 128], [64, T],
                                                 [1, 32]]), OP.mult), "V", "h4")
                v.wait_ge(SEM["V"], tick["h4"])
                bump(v.tensor_tensor(XNS[:], XNS[:],
                                bass.AP(GBA, 32, [[T * 64, 128], [64, T],
                                                  [1, 32]]), OP.add), "V", "xns")
                v.wait_ge(SEM["V"], tick["xns"])
                v.tensor_reduce(TMIN[:],
                                XNS[:].rearrange("p (a b) -> p a b", a=T),
                                AX.X, OP.min)
                bump(v.tensor_reduce(TMAX[:],
                                     XNS[:].rearrange("p (a b) -> p a b", a=T),
                                     AX.X, OP.max), "V", "tmm")
                v.wait_ge(SEM["V"], tick["tmm"])
                v.wait_ge(SEM["A"], tick[("sq1", T - 1)])
                bump(v.tensor_scalar(
                    bass.AP(VMM, 0, [[2048, 128], [32, 64], [1, 32]]),
                    bass.AP(OHB, 0, [[T * 64, 128], [1, 64], [64, T]]),
                    -1e38, 1e38, OP.mult, OP.add), "V", "ms1")
                v.wait_ge(SEM["V"], tick["ms1"])
                bump(v.tensor_tensor(
                    bass.AP(VMM, 0, [[2048, 128], [32, 64], [1, 32]]),
                    bass.AP(VMM, 0, [[2048, 128], [32, 64], [1, 32]]),
                    bass.AP(TMIN, 0, [[T, 128], [0, 64], [1, T]]),
                    OP.add), "V", "vm1")
                v.wait_ge(SEM["V"], tick["vm1"])
                bump(v.tensor_reduce(
                    RED[:], VMM[:].rearrange("p (a b) -> p a b", a=64),
                    AX.X, OP.min), "V", "redmin")

            @blk.tensor
            def _(t):
                t.wait_ge(SEM["V"], tick["redmin"])
                t.wait_ge(SEM["V"], tick["sidf"])
                _mi = t.transpose(PSS[0:64, 128:256], RED[:], SIDF[:])
                bump(_mi, "P", "trmin")

            @blk.vector
            def _(v):
                v.wait_ge(SEM["P"], tick["trmin"])
                bump(v.tensor_reduce(PMM[:, 0:1],
                                     PSS[0:64, 128:256], AX.X, OP.min),
                     "V", "pmin")
                v.wait_ge(SEM["V"], tick["pmin"])
                bump(v.tensor_scalar(PMM[:, 0:1], PMM[:, 0:1], -1.0, None,
                                     OP.mult), "V", "pminneg")
                bump(v.tensor_scalar(
                    bass.AP(VMM, 0, [[2048, 128], [32, 64], [1, 32]]),
                    bass.AP(OHB, 0, [[T * 64, 128], [1, 64], [64, T]]),
                    1e38, -1e38, OP.mult, OP.add), "V", "ms2")
                v.wait_ge(SEM["V"], tick["ms2"])
                bump(v.tensor_tensor(
                    bass.AP(VMM, 0, [[2048, 128], [32, 64], [1, 32]]),
                    bass.AP(VMM, 0, [[2048, 128], [32, 64], [1, 32]]),
                    bass.AP(TMAX, 0, [[T, 128], [0, 64], [1, T]]),
                    OP.add), "V", "vx1")
                v.wait_ge(SEM["V"], tick["vx1"])
                bump(v.tensor_reduce(
                    RED[:], VMM[:].rearrange("p (a b) -> p a b", a=64),
                    AX.X, OP.max), "V", "redmax")

            @blk.tensor
            def _(t):
                t.wait_ge(SEM["V"], tick["redmax"])
                _mi = t.transpose(PSS[0:64, 256:384], RED[:], SIDF[:])
                bump(_mi, "P", "trmax")

            @blk.vector
            def _(v):
                v.wait_ge(SEM["P"], tick["trmax"])
                bump(v.tensor_reduce(PMM[:, 1:2], PSS[0:64, 256:384], AX.X,
                                     OP.max), "V", "pmax")

        @blk.gpsimd
        def _(g):
            g.wait_ge(SEM["V"], tick["pmax"])
            g.wait_ge(SEM["V"], tick["pminneg"])
            dbump(g.dma_start(bass.AP(cc_in, 0, [[2, 64], [1, 2]]), PMM[:]),
                  "PD")
            g.wait_ge(SEM["PD"], cnt["PD"])
            g.collective_compute(
                "AllReduce", OP.max, replica_groups=[list(range(NCORES))],
                ins=[cc_in.ap().opt()], outs=[cc_out.ap().opt()],
            ).then_inc(SEM["CC"], 1)
            cnt["CC"] += 1
            g.wait_ge(SEM["CC"], cnt["CC"])
            dbump(g.dma_start(GMM[:], bass.AP(cc_out, 0, [[2, 64], [1, 2]])),
                  "PD", "ccb")
            bump(g.wait_ge(SEM["PD"], cnt["PD"]), "G", "ccdone")

        if True:
            PSS2 = PSS
            PSP = [PS2.ap()[:, 0:1024], PS2.ap()[:, 1024:2048]]
            XATF = XAT.ap().bitcast(F32)
            XATB = XAT.ap()
            STTF = STT.ap().bitcast(F32)
            UAL = XATF[:, 0:1024]
            GIF = XATF[:, 1024:2048]
            DPP = XATF[:, 2048:3072]
            GIOT = XATB[:, 6656:7168]
            GMH = XATB[0:64, 7168:7171]
            GML = XATB[0:64, 7200:7203]
            MNX = STTF[:, 0:96]
            OHG = VMM[:, 1024:1536]
            GIU = bass.AP(VMM, 1536, [[T * 64, 128], [1, 512]]).bitcast(I32) \
                if False else None
            GIU = bass.AP(VMM, 1024, [[T * 64, 128], [1, 1024]]).bitcast(I32)
            XLC = UAL
            BSA = GBA

            @blk.gpsimd
            def _(g):
                _gi = g.iota(GIOT, [[0, 32], [1, 16]], channel_multiplier=0,
                       allow_small_or_imprecise_dtypes=True)
                bump(_gi, "G", "giot")

            @blk.vector
            def _(v):
                v.wait_ge(SEM["G"], tick["ccdone"])
                v.tensor_scalar(VMM[0:64, 0:1], GMM[:, 0:1], -1.0, None, OP.mult)
                v.tensor_copy(VMM[0:64, 1:2], GMM[:, 1:2])
                bump(v.tensor_copy(VMM[0:64, 2:3], SOSC[:]), "V", "gm0")
                v.wait_ge(SEM["V"], tick["gm0"])
                bump(v.tensor_copy(GMH, VMM[0:64, 0:3]), "V", "gmh")
                v.wait_ge(SEM["V"], tick["gmh"])
                bump(v.tensor_copy(VMM[0:64, 3:6], GMH), "V", "gmb")
                v.wait_ge(SEM["V"], tick["gmb"])
                bump(v.tensor_tensor(VMM[0:64, 6:9], VMM[0:64, 0:3], VMM[0:64, 3:6],
                                     OP.subtract), "V", "gml0")
                v.wait_ge(SEM["V"], tick["gml0"])
                bump(v.tensor_copy(GML, VMM[0:64, 6:9]), "V", "gml")

            for k in range(T):
                @blk.tensor
                def _(t, k=k):
                    if k == 0:
                        t.wait_ge(SEM["V"], tick["gml"])
                    _mi = t.matmul(PSS2[:, 3 * k:3 * (k + 1)],
                             OHT[:, 128 * k:128 * (k + 1)], GMH,
                             start=(k == 0), stop=False, skip_group_check=True)
                    _mi = t.matmul(PSS2[:, 3 * k:3 * (k + 1)],
                             OHT[:, 128 * k:128 * (k + 1)], GML,
                             start=False, stop=True, skip_group_check=True)
                    if k == T - 1:
                        bump(_mi, "P", "mnmx")

            @blk.vector
            def _(v):
                v.wait_ge(SEM["P"], tick["mnmx"])
                bump(v.tensor_copy(MNX, PSS2[:, 0:96]), "V", "mnxev")
                v.wait_ge(SEM["V"], tick["mnxev"])
                m3 = MNX.rearrange("p (a b) -> p a b", a=T)
                mnv = m3[:, :, 0:1]
                mxv = m3[:, :, 1:2]
                bump(v.tensor_tensor(W32a[:], mxv, mnv, OP.subtract),
                     "V", "den0")
                v.wait_ge(SEM["V"], tick["den0"])
                bump(v.tensor_scalar(W32a[:], W32a[:], 1e-8, None, OP.add),
                     "V", "den1")
                v.wait_ge(SEM["V"], tick["den1"])
                bump(v.reciprocal(INV[:], W32a[:]), "V", "inv")
                v.wait_ge(SEM["V"], tick["inv"])
                bump(v.tensor_tensor(UAL, XNS[:],
                                m3[:, :, 0:1].to_broadcast([128, T, 1, 32]),
                                OP.subtract), "V", "u0")
                v.wait_ge(SEM["V"], tick["u0"])
                bump(v.tensor_tensor(UAL, UAL,
                                INV[:].to_broadcast([128, T, 32]), OP.mult), "V", "u1")
                v.wait_ge(SEM["V"], tick["u1"])
                bump(v.tensor_scalar(UAL, UAL, 0.0, 1.0 - 1e-6, OP.max,
                                OP.min), "V", "u2")
                v.wait_ge(SEM["V"], tick["u2"])
                bump(v.tensor_scalar(GIF, UAL, 16.0, None, OP.mult), "V", "g16")
                v.wait_ge(SEM["V"], tick["g16"])
                bump(v.tensor_copy(GIU, GIF), "V", "giu")
                v.wait_ge(SEM["V"], tick["giu"])
                bump(v.tensor_copy(GIF, GIU), "V", "gif")
                v.wait_ge(SEM["V"], tick["gif"])
                bump(v.scalar_tensor_tensor(XLC[:], UAL, 16.0, GIF, OP.mult,
                                       OP.subtract), "V", "xlc")

            for k in range(T):
                @blk.tensor
                def _(t, k=k):
                    if k == 0:
                        t.wait_ge(SEM["V"], tick["tbl"])
                        t.wait_ge(SEM["V"], tick["xlc"])
                    if k >= 2:
                        t.wait_ge(SEM["V"], tick[("bs", k - 2)])
                    for h in range(2):
                        _mi = t.matmul(PSP[k % 2][:, 512 * h:512 * (h + 1)],
                                 OHT[:, 128 * k:128 * (k + 1)],
                                 TBH[:, 512 * h:512 * (h + 1)],
                                 start=True, stop=False, skip_group_check=True)
                        _mi = t.matmul(PSP[k % 2][:, 512 * h:512 * (h + 1)],
                                 OHT[:, 128 * k:128 * (k + 1)],
                                 TBL[:, 512 * h:512 * (h + 1)],
                                 start=False, stop=True, skip_group_check=True)
                    bump(_mi, "P", ("pg", k))

                @blk.vector
                def _(v, k=k):
                    v.wait_ge(SEM["P"], tick[("pg", k)])
                    if k == 0:
                        v.wait_ge(SEM["G"], tick["giot"])
                    bump(v.tensor_tensor(
                        OHG, GIF[:, 32 * k:32 * (k + 1)]
                        .to_broadcast([128, 32, 16]), GIOT, OP.is_equal), "V", ("ohg", k))
                    v.wait_ge(SEM["V"], tick[("ohg", k)])
                    bump(v.tensor_tensor(VMM[:, 0:1024], bass.AP(
                        VMM, 1024, [[T * 64, 128], [0, 2], [1, 512]]),
                        PSP[k % 2], OP.mult), "V", ("prod", k))
                    v.wait_ge(SEM["V"], tick[("prod", k)])
                    bump(v.tensor_reduce(
                        BSA[:, 64 * k:64 * (k + 1)],
                        VMM[:, 0:1024].rearrange("p (a b) -> p a b", a=64),
                        AX.X, OP.add), "V", ("bs", k))

            @blk.vector
            def _(v):
                v.wait_ge(SEM["V"], tick[("bs", T - 1)])
                bview = bass.AP(BSA, 0, [[T * 64, 128], [64, T], [1, 32]])
                sview = bass.AP(BSA, 32, [[T * 64, 128], [64, T], [1, 32]])
                oview = MNX.rearrange("p (a b) -> p a b", a=T)[:, :, 2:3] \
                    .to_broadcast([128, T, 1, 32])
                bump(v.tensor_tensor(DPP, sview, XLC[:], OP.mult), "V", "d1")
                v.wait_ge(SEM["V"], tick["d1"])
                bump(v.tensor_tensor(DPP, DPP, bview, OP.add), "V", "d2")
                v.wait_ge(SEM["V"], tick["d2"])
                bump(v.tensor_tensor(DPP, DPP, oview, OP.mult), "V", "delta")
                v.wait_ge(SEM["V"], tick["delta"])
                bump(v.tensor_reduce(W32a[:],
                                DPP.rearrange("p (a b) -> p a b", a=T),
                                AX.X, OP.add), "V", "sd")
                v.wait_ge(SEM["V"], tick["sd"])
                bump(v.tensor_scalar(MUG[:], W32a[:], 1.0 / D, None, OP.mult), "V", "mug")
                bump(v.tensor_tensor(VMM[:, 0:T * 32], XSL[:],
                                     MU1[:].to_broadcast([128, T, 32]),
                                     OP.subtract), "V", "hn0")
                v.wait_ge(SEM["V"], tick["hn0"])
                bump(v.tensor_tensor(VMM[:, 0:T * 32], VMM[:, 0:T * 32],
                                     R1A[:].to_broadcast([128, T, 32]),
                                     OP.mult), "V", "hn1")
                v.wait_ge(SEM["V"], tick["hn1"])
                bump(v.tensor_tensor(VMM[:, T * 32:T * 64], VMM[:, 0:T * 32],
                                     DPP, OP.mult), "V", "cr0")
                v.wait_ge(SEM["V"], tick["cr0"])
                bump(v.tensor_reduce(W32b[:],
                                VMM[:, T * 32:T * 64].rearrange(
                                    "p (a b) -> p a b", a=T), AX.X, OP.add), "V", "cr")
                v.wait_ge(SEM["V"], tick["cr"])
                bump(v.tensor_tensor(VMM[:, T * 32:T * 64], DPP, DPP, OP.mult), "V", "dd0")
                v.wait_ge(SEM["V"], tick["dd0"])
                bump(v.tensor_reduce(W32c[:],
                                VMM[:, T * 32:T * 64].rearrange(
                                    "p (a b) -> p a b", a=T), AX.X, OP.add), "V", "ddq")
                v.wait_ge(SEM["V"], tick["ddq"])
                bump(v.tensor_tensor(W32a[:], R1A[:], R1A[:], OP.mult), "V", "q1")
                v.wait_ge(SEM["V"], tick["q1"])
                bump(v.tensor_tensor(W32a[:], W32a[:], V1A[:], OP.mult), "V", "q2")
                v.wait_ge(SEM["V"], tick["q2"])
                bump(v.scalar_tensor_tensor(W32a[:], W32a[:], float(D), W32b[:],
                                       OP.mult, OP.add), "V", "q3")
                v.wait_ge(SEM["V"], tick["q3"])
                bump(v.tensor_tensor(W32a[:], W32a[:], W32b[:], OP.add), "V", "q4")
                v.wait_ge(SEM["V"], tick["q4"])
                bump(v.tensor_tensor(W32a[:], W32a[:], W32c[:], OP.add), "V", "q5")
                v.wait_ge(SEM["V"], tick["q5"])
                bump(v.tensor_tensor(W32b[:], MUG[:], MUG[:], OP.mult), "V", "q6")
                v.wait_ge(SEM["V"], tick["q6"])
                bump(v.scalar_tensor_tensor(W32a[:], W32a[:], 1.0 / D, W32b[:],
                                       OP.mult, OP.subtract), "V", "vg")

            @blk.scalar
            def _(a):
                a.wait_ge(SEM["V"], tick["vg"])
                bump(a.activation(W32a[:], W32a[:], AF.Sqrt, bias=EPSC[:]),
                     "A", "lng")

            @blk.vector
            def _(v):
                v.wait_ge(SEM["A"], tick["lng"])
                bump(v.reciprocal(RGA[:], W32a[:]), "V", "rg")
                v.wait_ge(SEM["V"], tick["rg"])
                bump(v.tensor_tensor(W32a[:], R1A[:], RGA[:], OP.mult), "V", "s20")
                v.wait_ge(SEM["V"], tick["s20"])
                bump(v.tensor_scalar(S2A[:], W32a[:], 1.0, None, OP.add),
                     "V", "s2")
                bump(v.tensor_tensor(W32b[:], MU1[:], R1A[:], OP.mult), "V", "b20")
                v.wait_ge(SEM["V"], tick["b20"])
                bump(v.tensor_tensor(W32b[:], W32b[:], MUG[:], OP.add), "V", "b21")
                v.wait_ge(SEM["V"], tick["b21"])
                bump(v.tensor_tensor(W32b[:], W32b[:], RGA[:], OP.mult), "V", "b22")
                v.wait_ge(SEM["V"], tick["b22"])
                bump(v.tensor_scalar(B2A[:], W32b[:], -1.0, None, OP.mult),
                     "V", "b2")
                bump(v.tensor_tensor(DPP, DPP, VMM[:, 0:T * 32], OP.add), "V", "e1")
                v.wait_ge(SEM["V"], tick["e1"])
                bump(v.tensor_tensor(DPP, DPP,
                                MUG[:].to_broadcast([128, T, 32]), OP.subtract), "V", "e2")
                v.wait_ge(SEM["V"], tick["e2"])
                bump(v.tensor_tensor(DPP, DPP,
                                RGA[:].to_broadcast([128, T, 32]), OP.mult), "V", "e3")
                v.wait_ge(SEM["V"], tick["e3"])
                bump(v.tensor_tensor(DPP, DPP, XSL[:], OP.add), "V", "dpp")

            for k in range(T):
                @blk.sync
                def _(sy, k=k):
                    if k == 0:
                        for q in range(4):
                            sy.wait_ge(SEM[f"SP{q}"], tick[("sp", T - 4 + q)])
                        sy.wait_ge(SEM["V"], tick[("sig", T - 1)])
                    if k >= 2:
                        sy.wait_ge(SEM[f"OS{k % 2}"], tick[("st", k - 2)])
                    dbump(sy.dma_start(
                        X0B[k % 2][:],
                        bass.AP(x1sp, k * 128 * D, [[D, 128], [1, D]])),
                        f"RL{k % 2}", ("rl", k))

                @blk.vector
                def _(v, k=k):
                    v.wait_ge(SEM[f"RL{k % 2}"], tick[("rl", k)])
                    if k == 0:
                        v.wait_ge(SEM["V"], tick["b2"])
                    bump(v.tensor_scalar(X0B[k % 2][:], X0B[k % 2][:],
                                         S2A[:, k:k + 1], B2A[:, k:k + 1],
                                         OP.mult, OP.add), "V", ("t2", k))

                @blk.sync
                def _(sy, k=k):
                    sy.wait_ge(SEM["V"], tick[("t2", k)])
                    dbump(sy.dma_start(
                        bass.AP(outd, k * 128 * D, [[D, 128], [1, D]]),
                        X0B[k % 2][:]), f"OS{k % 2}", ("st", k))

                @blk.gpsimd
                def _(g, k=k):
                    g.wait_ge(SEM[f"OS{k % 2}"], tick[("st", k)])
                    if k == 0:
                        g.wait_ge(SEM["V"], tick["dpp"])
                    dbump(g.indirect_dma_start(
                        out=bass.AP(outd, 0, [[32, TOK * 64], [1, 32]]),
                        out_offset=bass.IndirectOffsetOnAxis(
                            ap=IDXU[:, k:k + 1], axis=0),
                        in_=DPP[:, 32 * k:32 * (k + 1)], in_offset=None),
                        "PD", ("scat", k))

            if debug:
                @blk.sync
                def _(sy):
                    sy.wait_ge(SEM["V"], tick["dpp"])
                    dbump(sy.dma_start(dbg1d.ap(), XNS[:, 0:D]), "OS0")
                    dbump(sy.dma_start(dbg2d.ap(), BSA[:, 0:2048]), "OS0")

            @blk.sync
            def _(sy):
                sy.wait_ge(SEM["OS0"], cnt["OS0"])
                sy.wait_ge(SEM["OS1"], cnt["OS1"])
                sy.wait_ge(SEM["PD"], cnt["PD"])

    return nc


_PROG_CACHE = {}


def _get_prog(debug=False):
    key = bool(debug)
    if key not in _PROG_CACHE:
        _PROG_CACHE[key] = build_program(debug=key)
    return _PROG_CACHE[key]


def make_in_maps(L0_attn_x, L0_attn_B, L0_attn_C, L0_attn_D, L0_ffn_ln_gamma,
                 L0_ffn_ln_beta, L0_ffn_spline_bases, L0_ffn_spline_slopes,
                 L0_ffn_output_scale):
    x = np.ascontiguousarray(np.asarray(L0_attn_x, dtype=np.float32)
                             .reshape(NCORES * TOK, D))
    bt = np.ascontiguousarray(np.asarray(L0_attn_B, dtype=np.float32).T)
    ct = np.ascontiguousarray(np.asarray(L0_attn_C, dtype=np.float32).T)
    dcol = np.ascontiguousarray(
        np.asarray(L0_attn_D, dtype=np.float32).reshape(16, 128).T)
    gam = np.ascontiguousarray(
        np.asarray(L0_ffn_ln_gamma, dtype=np.float32).reshape(64, 32))
    bet = np.ascontiguousarray(
        np.asarray(L0_ffn_ln_beta, dtype=np.float32).reshape(64, 32))
    bas = np.ascontiguousarray(
        np.asarray(L0_ffn_spline_bases, dtype=np.float32).reshape(64, 512))
    slo = np.ascontiguousarray(
        np.asarray(L0_ffn_spline_slopes, dtype=np.float32).reshape(64, 512))
    osc = np.ascontiguousarray(
        np.asarray(L0_ffn_output_scale, dtype=np.float32).reshape(64, 1))
    return [{"x0": x[core * TOK:(core + 1) * TOK],
             "bt": bt, "ct": ct, "dcol": dcol, "gam": gam, "bet": bet,
             "bas": bas, "slo": slo, "osc": osc} for core in range(NCORES)]


def kernel(**inputs):
    in_maps = make_in_maps(**inputs)
    nc = _get_prog(debug=False)
    res = run_bass_kernel_spmd(nc, in_maps, core_ids=list(range(NCORES)))
    out = np.concatenate([r["out"] for r in res.results], axis=0)
    return out.reshape(2, 16384, D).astype(np.float32)
